# version 05
# baseline (speedup 1.0000x reference)
import numpy as np
import ml_dtypes
import concourse.bass as bass
import concourse.mybir as mybir
from concourse.bass_utils import run_bass_kernel_spmd

F32 = mybir.dt.float32
BF16 = mybir.dt.bfloat16
AF = mybir.ActivationFunctionType
ALU = mybir.AluOpType
AX = mybir.AxisListType.X

T = 4096
D = 1024
NG = 8
GS = 512
NT = 32
HID0 = 4096
E1 = 8
I1 = 1024
DEXP = 2048
EPS = 1e-5

ENGINES = ("sync", "gpsimd", "scalar", "vector", "tensor")
import os
DEV_SKIP = set(os.environ.get("DEV_SKIP", "").split(","))
DEV_NG = int(os.environ.get("DEV_NG", NG if 'NG' in dir() else 8))
SLOPES = [2.0 ** x for x in np.linspace(0.0, -8.0, 8)]
SBUF_LIMIT = 229344


class Sem:
    def __init__(self, h):
        self.h = h
        self.n = 0

    def inc(self, k=1):
        self.n += k
        return (self, k)

    @property
    def now(self):
        return (self, self.n)


class B:

    def __init__(self, nc):
        self.nc = nc
        self.streams = {e: [] for e in ENGINES}
        self.bar = Sem(nc.alloc_semaphore("bar"))
        self.P = {e: Sem(nc.alloc_semaphore(f"P_{e}")) for e in ENGINES}
        self.DW = {e: Sem(nc.alloc_semaphore(f"D_{e}")) for e in ("sync", "gpsimd", "scalar")}

    def R(self, eng, build, waits=(), dma=None):
        waits = tuple((s, v) for (s, v) in waits if v > 0)
        if build is None:
            assert not dma

            def op(e, waits=waits):
                for (s, v) in waits:
                    e.wait_ge(s.h, v)

            self.streams[eng].append(op)
            return None
        if dma:
            sem = self.DW[eng]
            sem.n += 16
            amt = 16
        else:
            sem = self.P[eng]
            sem.n += 1
            amt = 1
        mark = (sem, sem.n)

        def op(e, build=build, waits=waits, sem=sem, amt=amt):
            for (s, v) in waits:
                e.wait_ge(s.h, v)
            build(e).then_inc(sem.h, amt)

        self.streams[eng].append(op)
        return mark

    def barrier(self):
        self.bar.n += 5
        tgt = self.bar.n
        for eng in ENGINES:
            w = [(self.P[eng], self.P[eng].n)]
            if eng in self.DW:
                w.append((self.DW[eng], self.DW[eng].n))
            self.R(eng, None, waits=w)
            self.streams[eng].append(lambda e, h=self.bar.h: e.nop().then_inc(h, 1))
            self.R(eng, None, waits=[(self.bar, tgt)])

    def replay(self):
        nc = self.nc
        with nc.Block() as block:
            for eng in ENGINES:
                ops = self.streams[eng]
                getattr(block, eng)(lambda e, ops=ops: [op(e) for op in ops])



def build_program(dump=(), upto=None):
    nc = bass.Bass("TRN2", target_bir_lowering=False)

    def dram(name, shape, dt):
        kind = "ExternalOutput" if name in dump else "Internal"
        return nc.dram_tensor(name, shape, dt, kind=kind).ap()

    I = {}

    def inp(name, shape, dt=F32):
        I[name] = nc.dram_tensor(name, shape, dt, kind="ExternalInput").ap()

    inp("x0", [T, D])
    inp("L0_attn_ln_gamma", [D]); inp("L0_attn_ln_beta", [D])
    inp("L0_attn_w_qkv", [3 * D, D]); inp("L0_attn_w_out", [D, D]); inp("L0_attn_b_out", [D])
    inp("L0_ffn_gate_w", [2, D])
    inp("L0_ffn_w1", [2, D, HID0]); inp("L0_ffn_b1", [2, HID0])
    inp("L0_ffn_w2", [2, HID0, D]); inp("L0_ffn_b2", [2, D])
    inp("L0_ffn_sw1", [D, HID0]); inp("L0_ffn_sb1", [HID0])
    inp("L0_ffn_sw2", [HID0, D]); inp("L0_ffn_sb2", [D])
    for nm in ("in_ln", "q_ln", "k_ln", "out_ln"):
        inp(f"L1_attn_{nm}_w", [D]); inp(f"L1_attn_{nm}_b", [D])
    inp("L1_attn_W_in", [D, 2 * DEXP]); inp("L1_attn_W_out", [DEXP, D])
    inp("L1_attn_Wq", [D, DEXP]); inp("L1_attn_bq", [DEXP])
    inp("L1_attn_Wk", [D, DEXP]); inp("L1_attn_bk", [DEXP])
    inp("L1_attn_smear_factor", [16]); inp("L1_attn_log_scale", [16])
    inp("L1_ffn_Wg", [E1, D, I1]); inp("L1_ffn_Wu", [E1, D, I1]); inp("L1_ffn_Wd", [E1, I1, D])
    inp("L1_ffn_Wr", [D, E1]); inp("L1_ffn_temperature", [1])

    out = nc.dram_tensor("out", [T, D], F32, kind="ExternalOutput").ap()

    hbm = dict(
        dbg0=dram("dbg0", [128, 4, 1024], BF16),
        dbg1=dram("dbg1", [128, 4, 1024], F32),
        dbg2=dram("dbg2", [128, 1024], F32),
        dbg3=dram("dbg3", [128, 24], F32),
        dbg4=dram("dbg4", [128, 1024], F32),
        dbg5=dram("dbg5", [128, 32], F32),
        dbg6=dram("dbg6", [1, 2048], F32),
        dbg7=dram("dbg7", [1, 128], F32),
        dbg8=dram("dbg8", [128, 1024], F32),
        dbg9=dram("dbg9", [128, 1024], F32),
        xn0=dram("xn0", [8, 128, T], BF16),
        q0h=dram("q0h", [8, 128, T], BF16),
        k0h=dram("k0h", [8, 128, T], BF16),
        v0h=dram("v0h", [T, 16 * 66], BF16),
        mask01=dram("mask01", [2, T], BF16),
        s0h=dram("s0h", [T, D], F32),
        y0h=dram("y0h", [T, D], F32),
        x1h=dram("x1h", [T, D], F32),
        xn1=dram("xn1", [8, 128, T], BF16),
        xin1=dram("xin1", [8, 128, T], BF16),
        xq1=dram("xq1", [8, 128, T], BF16),
        xk1=dram("xk1", [8, 128, T], BF16),
        wrr=dram("wrr", [8, 1024], F32),
        q1h=dram("q1h", [16, 128, T], BF16),
        k1h=dram("k1h", [16, 128, T], BF16),
        v1h=dram("v1h", [T, DEXP], BF16),
        p1h=dram("p1h", [16, 128, T], BF16),
        rw1=dram("rw1", [E1, T], BF16),
        denh=dram("denh", [16, 512], F32),
        dbgA=dram("dbgA", [128, 16, 512], BF16),
        s1h=dram("s1h", [T, D], F32),
        y1h=dram("y1h", [T, D], F32),
    )

    b = B(nc)
    st = {"nc": nc, "I": I, "out": out, "hbm": hbm}

    A = nc.alloc_sbuf_tensor
    st["ident"] = A("ident", [128, 128], BF16)
    st["ident_f"] = A("ident_f", [128, 128], F32)
    st["ones_bf"] = A("ones_bf", [1, 128], BF16)
    st["ones_fr"] = A("ones_fr", [1, 128], F32)
    st["iotaD"] = A("iotaD", [128, 512], F32)
    st["b1e"] = A("b1e", [128, 3 * 32], F32)
    st["g0a"] = A("g0a", [128, 8], F32)
    st["g0b"] = A("g0b", [128, 8], F32)
    st["l1w"] = A("l1w", [128, 4 * 8], F32)
    st["l1b"] = A("l1b", [128, 4 * 8], F32)
    st["l1b_bf"] = A("l1b_bf", [128, 4 * 8], BF16)
    st["bq1"] = A("bq1", [128, 16], F32)
    st["bk1"] = A("bk1", [128, 16], F32)
    st["smr"] = A("smr", [128, 2 * 16], F32)
    st["srecip_bc"] = A("srecip_bc", [128, 16], F32)
    st["rtemp_bc"] = A("rtemp_bc", [128, 1], F32)
    st["qkvbias"] = A("qkvbias", [128, 24], F32)
    st["scr2"] = A("scr2", [128, 64], F32)
    st["rowA"] = A("rowA", [1, 1024], F32)
    st["rowB"] = A("rowB", [1, 1024], F32)
    st["rowC"] = A("rowC", [1, 256], F32)
    st["bout_row"] = A("bout_row", [1, 1024], BF16)
    st["b2s_row"] = A("b2s_row", [1, 1024], BF16)
    st["epsb"] = A("epsb", [128, 1], F32)

    st["pb"] = [nc.alloc_psum_tensor(f"pb{i}", [128, 512], F32) for i in range(8)]
    st["manual_base"] = (nc.sbuf_base + 31) // 32 * 32

    phases = [
        phase_setup,
        phase_l0_lnqkv,
        phase_l0_attn,
        phase_l0_ffn,
        phase_l1_ln,
        phase_l1_winp,
        phase_l1_qk,
        phase_l1_attn,
        phase_l1_ffn,
    ]
    if upto is not None:
        phases = phases[:upto]
    for i, ph in enumerate(phases):
        if i:
            b.barrier()
        ph(b, st)
    b.barrier()
    b.replay()
    return nc


def man(st, name, shape, dt, off):
    assert off % 32 == 0
    h = st["nc"].alloc_sbuf_tensor_at(name, shape, dt, offset=st["manual_base"] + off)
    nb = int(np.prod(shape[1:])) * (2 if dt == BF16 else 4)
    end = st["manual_base"] + off + nb
    assert end <= SBUF_LIMIT, f"sbuf overflow: {name} ends at {end}"
    return h



def phase_setup(b, st):
    I = st["I"]
    R = b.R

    m_iota = R("gpsimd", lambda e: e.iota(st["iotaD"][:], [[1, 512]], channel_multiplier=-1,
                                          allow_small_or_imprecise_dtypes=True))
    m_iotai = R("gpsimd", lambda e: e.iota(st["ident"][:], [[1, 128]], channel_multiplier=-1,
                                           allow_small_or_imprecise_dtypes=True))
    m_id = R("vector", lambda e: e.tensor_single_scalar(st["ident"][:], st["ident"][:], 0.0,
                                                        ALU.is_equal), waits=[m_iotai])
    m_idf = R("vector", lambda e: e.tensor_copy(st["ident_f"][:], st["ident"][:]), waits=[m_id])
    R("vector", lambda e: e.memset(st["ones_bf"][:], 1.0))
    R("vector", lambda e: e.memset(st["ones_fr"][:], 1.0))
    R("vector", lambda e: e.memset(st["epsb"][:], EPS))

    pm = lambda ap: ap.rearrange("(c p) -> p c", p=128)
    loads = []

    def gld(dst, src, **kw):
        loads.append(R("gpsimd", lambda e, d=dst, s=src, kw=kw: e.dma_start(d, s, **kw), dma=True))

    gld(st["b1e"][:, 0:32], pm(I["L0_ffn_b1"][0]), allow_slow_non_contiguous=True)
    gld(st["b1e"][:, 32:64], pm(I["L0_ffn_b1"][1]), allow_slow_non_contiguous=True)
    gld(st["b1e"][:, 64:96], pm(I["L0_ffn_sb1"]), allow_slow_non_contiguous=True)
    gld(st["g0a"][:], pm(I["L0_attn_ln_gamma"]), allow_slow_non_contiguous=True)
    gld(st["g0b"][:], pm(I["L0_attn_ln_beta"]), allow_slow_non_contiguous=True)
    for i, nm in enumerate(("in_ln", "q_ln", "k_ln", "out_ln")):
        gld(st["l1w"][:, 8 * i:8 * i + 8], pm(I[f"L1_attn_{nm}_w"]), allow_slow_non_contiguous=True)
        gld(st["l1b"][:, 8 * i:8 * i + 8], pm(I[f"L1_attn_{nm}_b"]), allow_slow_non_contiguous=True)
    gld(st["bq1"][:], pm(I["L1_attn_bq"]), allow_slow_non_contiguous=True)
    gld(st["bk1"][:], pm(I["L1_attn_bk"]), allow_slow_non_contiguous=True)
    gld(st["rowC"][0:1, 0:16], I["L1_attn_smear_factor"].rearrange("(one h) -> one h", one=1))
    gld(st["rowC"][0:1, 16:32], I["L1_attn_log_scale"].rearrange("(one h) -> one h", one=1))
    gld(st["rowC"][0:1, 32:33], I["L1_ffn_temperature"].rearrange("(one h) -> one h", one=1))
    gld(st["bout_row"][:], I["L0_attn_b_out"].rearrange("(one d) -> one d", one=1))
    gld(st["rowA"][0:1, :], I["L0_ffn_b2"][0].rearrange("(one d) -> one d", one=1))
    gld(st["rowB"][0:1, :], I["L0_ffn_b2"][1].rearrange("(one d) -> one d", one=1))
    mld = loads[-1]
    m_b2a = R("vector", lambda e: e.tensor_add(st["rowA"][0:1, :], st["rowA"][0:1, :],
                                               st["rowB"][0:1, :]), waits=[mld])
    m_sb2 = R("gpsimd", lambda e: e.dma_start(
        st["rowB"][0:1, :], I["L0_ffn_sb2"].rearrange("(one d) -> one d", one=1)),
        waits=[m_b2a], dma=True)
    m_b2b = R("vector", lambda e: e.tensor_add(st["rowA"][0:1, :], st["rowA"][0:1, :],
                                               st["rowB"][0:1, :]), waits=[m_sb2])
    R("vector", lambda e: e.tensor_copy(st["b2s_row"][:], st["rowA"][0:1, :]), waits=[m_b2b])

    R("vector", lambda e: e.tensor_copy(st["l1b_bf"][:], st["l1b"][:]), waits=[mld])
    m_sg = R("scalar", lambda e: e.activation(st["rowC"][0:1, 64:80], st["rowC"][0:1, 0:16],
                                              AF.Sigmoid), waits=[mld])
    m_sr = R("scalar", lambda e: e.activation(st["rowC"][0:1, 80:96], st["rowC"][0:1, 16:32],
                                              AF.Exp, scale=-1.0), waits=[m_sg])
    m_a0 = R("vector", lambda e: e.tensor_scalar(st["rowC"][0:1, 48:64], st["rowC"][0:1, 64:80],
                                                 -1.0, 1.0, ALU.mult, ALU.add), waits=[m_sr])
    m_a = R("vector", lambda e: e.tensor_mul(st["rowC"][0:1, 96:112], st["rowC"][0:1, 48:64],
                                             st["rowC"][0:1, 80:96]), waits=[m_a0])
    m_b = R("vector", lambda e: e.tensor_mul(st["rowC"][0:1, 112:128], st["rowC"][0:1, 64:80],
                                             st["rowC"][0:1, 80:96]), waits=[m_a])
    m_rt = R("vector", lambda e: e.reciprocal(st["rowC"][0:1, 33:34], st["rowC"][0:1, 32:33]),
             waits=[m_b])

    pb = st["pb"]
    prev_copy = None
    for dst, off_, n, bank in ((st["smr"][:, 0:16], 96, 16, 2), (st["smr"][:, 16:32], 112, 16, 3),
                               (st["srecip_bc"][:], 80, 16, 4), (st["rtemp_bc"][:], 33, 1, 5)):
        mm = R("tensor", lambda e, d=dst, o=off_, n=n, bank=bank:
               e.matmul(pb[bank][:, 0:n], st["ones_fr"][:], st["rowC"][0:1, o:o + n],
                        start=True, stop=True), waits=[m_rt])
        prev_copy = R("scalar", lambda e, d=dst, n=n, bank=bank: e.copy(d, pb[bank][:, 0:n]),
                      waits=[mm])



def phase_l0_lnqkv(b, st):
    I, hbm, pb = st["I"], st["hbm"], st["pb"]
    R = b.R

    off = 0
    wnat = man(st, "p2_wnat", [128, 24, 1024], BF16, off)
    xnfm = man(st, "p2_xnfm", [128, 8, 512], BF16, off)
    xn2fm = man(st, "p2_xn2fm", [128, 8, 512], BF16, off + 8192)
    qsb = man(st, "p2_qsb", [128, 8, 512], BF16, off + 16384)
    ksb = man(st, "p2_ksb", [128, 8, 512], BF16, off + 24576)
    vsb = man(st, "p2_vsb", [128, 4, 1056], BF16, off + 32768)
    off += 48 * 1024
    wqkvT = man(st, "p2_wqkvT", [128, 8, 3072], BF16, off); off += 48 * 1024
    xg = [man(st, "p2_xg0", [128, 4, 1024], F32, off),
          man(st, "p2_xg1", [128, 4, 1024], F32, off + 16 * 1024)]
    off += 32 * 1024
    sq = man(st, "p2_sq", [128, 1024], F32, off); off += 4 * 1024
    xn_tm = man(st, "p2_xntm", [128, 4, 1024], BF16, off); off += 8 * 1024
    xn2_tm = man(st, "p2_xn2tm", [128, 4, 1024], BF16, off); off += 8 * 1024
    gdiff_bc = man(st, "p2_gdbc", [128, 1024], F32, off); off += 4 * 1024
    stt = man(st, "p2_stt", [128, 24], F32, off); off += 96
    mcols = man(st, "p2_mcols", [128, 32], F32, off); off += 160
    grow = man(st, "p2_grow", [1, 2 * 1024], F32, off); off += 8 * 1024
    p2scr = man(st, "p2_scr", [128, 256], F32, off); off += 1024
    bcbeta = man(st, "p2_bcbeta", [128, 1024], BF16, off); off += 2 * 1024

    pbh = [pb[6].bitcast(BF16), pb[7].bitcast(BF16)]
    pbh4 = [pb[4].bitcast(BF16), pb[5].bitcast(BF16), pb[6].bitcast(BF16), pb[7].bitcast(BF16)]

    mg0 = R("gpsimd", lambda e: e.dma_start(
        grow[0:1, 0:1024], I["L0_ffn_gate_w"][0].rearrange("(one d) -> one d", one=1)), dma=True)
    mg1 = R("gpsimd", lambda e: e.dma_start(
        grow[0:1, 1024:2048], I["L0_ffn_gate_w"][1].rearrange("(one d) -> one d", one=1)), dma=True)
    m_sub = R("vector", lambda e: e.tensor_sub(grow[0:1, 0:1024], grow[0:1, 1024:2048],
                                               grow[0:1, 0:1024]), waits=[mg0, mg1])
    for half in range(2):
        mm = R("tensor", lambda e, half=half: e.matmul(
            pb[half][:], st["ones_fr"][:], grow[0:1, 512 * half:512 * (half + 1)],
            start=True, stop=True), waits=[m_sub])
        m_gd = R("scalar", lambda e, half=half: e.copy(gdiff_bc[:, 512 * half:512 * (half + 1)],
                                                       pb[half][:]), waits=[mm])

    m_wnat = R("gpsimd", lambda e: e.dma_start(
        wnat[:], I["L0_attn_w_qkv"].rearrange("(c p) d -> p c d", p=128)), dma=True)
    m_brow = R("sync", lambda e: e.dma_start(
        st["rowA"][0:1, :], I["L0_attn_ln_beta"].rearrange("(one d) -> one d", one=1)), dma=True)
    m_bb = []
    for half in range(2):
        mm = R("tensor", lambda e, half=half: e.matmul(
            pb[2 + half][:], st["ones_fr"][:], st["rowA"][0:1, 512 * half:512 * (half + 1)],
            start=True, stop=True), waits=[m_brow])
        m_bb.append(R("scalar", lambda e, half=half: e.copy(
            bcbeta[:, 512 * half:512 * (half + 1)], pb[2 + half][:]), waits=[mm]))

    m_gq = R("vector", lambda e: e.tensor_scalar(st["scr2"][:, 16:24], st["g0a"][:], 0.125, None,
                                                 ALU.mult))
    m_bias = None
    for ec in range(24):
        m_t = R("vector", lambda e, ec=ec: e.tensor_mul(sq[:], wnat[:, ec, :], bcbeta[:]),
                waits=([m_wnat] + m_bb) if ec == 0 else [m_bias])
        m_bias = R("vector", lambda e, ec=ec: e.reduce_sum(st["qkvbias"][:, ec:ec + 1], sq[:],
                                                           axis=AX), waits=[m_t])
    m_bias = R("vector", lambda e: e.tensor_scalar(st["qkvbias"][:, 0:8], st["qkvbias"][:, 0:8],
                                                   0.125, None, ALU.mult), waits=[m_bias])

    tcnt = 0
    evac = {}
    for ec in range(24):
        for dc in range(8):
            bank = tcnt % 4
            wl = [m_wnat, m_gq] if tcnt == 0 else []
            if tcnt >= 4:
                wl.append(evac[tcnt - 4])
            m_tr = R("tensor", lambda e, ec=ec, dc=dc, bank=bank:
                     e.transpose(pbh4[bank].ap()[:, 0:128], wnat[:, ec, 128 * dc:128 * (dc + 1)],
                                 st["ident"][:]), waits=wl)
            scl = st["scr2"][:, 16 + dc:17 + dc] if ec < 8 else st["g0a"][:, dc:dc + 1]
            evac[tcnt] = R("scalar", lambda e, ec=ec, dc=dc, bank=bank, scl=scl:
                           e.activation(wqkvT[:, dc, 128 * ec:128 * (ec + 1)],
                                        pbh4[bank].ap()[:, 0:128], AF.Identity, scale=scl),
                           waits=[m_tr])
            tcnt += 1
    m_wq = [evac[tcnt - 1], evac[tcnt - 2], evac[tcnt - 3], evac[tcnt - 4]]

    m_vones = None
    for tt in range(4):
        m_vones = R("vector", lambda e, tt=tt: e.memset(
            vsb[:, tt, :].rearrange("p (h x) -> p h x", h=16)[:, :, 64:66], 1.0),
            waits=(m_wq if tt == 0 else []))
    for tt in range(4):
        m_vones = R("vector", lambda e, tt=tt: e.memset(
            vsb[:, tt, :].rearrange("p (h x) -> p h x", h=16)[:, :, 65:66], 0.0),
            waits=[m_vones] if tt == 0 else [])

    rel_x = [[], []]
    rel_q = []
    rel_k = []
    rel_v = []
    rel_xnfm = []
    rel_tm = []
    for g in range(NG):
        buf = g % 2
        xga = xg[buf]
        m_x = R("sync", lambda e, g=g, xga=xga: e.dma_start(
            xga[:], I["x0"][512 * g:512 * (g + 1), :].rearrange("(a p) d -> p a d", p=128)),
            waits=rel_x[buf], dma=True)

        m_tm = []
        m_gate = []
        last_v = None
        for tt in range(4):
            ti = 4 * g + tt
            m1 = R("vector", lambda e, xga=xga, tt=tt: e.bn_stats(stt[:, 0:6], xga[:, tt, 0:512]),
                   waits=[m_x] + ([m_vones] if g == 0 and tt == 0 else [])
                   + (rel_tm if tt == 0 else []) + ([last_v] if last_v else []))
            m2 = R("vector", lambda e, xga=xga, tt=tt: e.bn_stats(stt[:, 6:12], xga[:, tt, 512:1024]),
                   waits=[m1])
            m3 = R("vector", lambda e: e.bn_aggr(stt[:, 12:14], stt[:, 0:12]), waits=[m2])
            mr1 = R("scalar", lambda e: e.activation(stt[:, 14:15], stt[:, 13:14], AF.Ln,
                                                     bias=st["epsb"][:]), waits=[m3])
            mr2 = R("scalar", lambda e: e.activation(stt[:, 15:16], stt[:, 14:15], AF.Exp,
                                                     scale=-0.5), waits=[mr1])
            ma1 = R("vector", lambda e: e.tensor_scalar(stt[:, 16:17], stt[:, 13:14], EPS, None,
                                                        ALU.add), waits=[m3])
            ma2 = R("vector", lambda e: e.reciprocal(stt[:, 17:18], stt[:, 16:17]), waits=[ma1])
            ma3 = R("vector", lambda e: e.tensor_mul(stt[:, 18:19], stt[:, 13:14], stt[:, 17:18]),
                    waits=[ma2])
            ma4 = R("scalar", lambda e: e.activation(stt[:, 19:20], stt[:, 18:19], AF.Ln,
                                                     bias=st["epsb"][:]), waits=[ma3, mr2])
            ma5 = R("scalar", lambda e: e.activation(stt[:, 20:21], stt[:, 19:20], AF.Exp,
                                                     scale=-0.5), waits=[ma4])
            mra = R("vector", lambda e: e.tensor_mul(stt[:, 21:22], stt[:, 15:16], stt[:, 20:21]),
                    waits=[ma5])
            mx1 = R("vector", lambda e, xga=xga, tt=tt: e.tensor_scalar(
                xn_tm[:, tt, :], xga[:, tt, :], stt[:, 12:13], stt[:, 15:16],
                ALU.subtract, ALU.mult), waits=[mra])
            mx2 = R("vector", lambda e, xga=xga, tt=tt: e.tensor_scalar(
                xn2_tm[:, tt, :], xga[:, tt, :], stt[:, 12:13], stt[:, 21:22],
                ALU.subtract, ALU.mult), waits=[mx1])
            last_v = R("vector", lambda e, xga=xga, tt=tt, ti=ti: e.scalar_tensor_tensor(
                sq[:], xga[:, tt, :], stt[:, 12:13], gdiff_bc[:],
                ALU.subtract, ALU.mult, accum_out=mcols[:, ti:ti + 1]), waits=[mx2])
            m_tm.append(last_v)
        rel_x[buf] = [last_v]

        tpos = 0
        evf = {}
        for tt in range(4):
            for c in range(8):
                for (src, dst) in ((xn_tm, xnfm), (xn2_tm, xn2fm)):
                    bank = tpos % 4
                    wl = [m_tm[tt]]
                    if tpos == 0:
                        wl += rel_xnfm
                    if tpos >= 4:
                        wl.append(evf[tpos - 4])
                    m_tr = R("tensor", lambda e, tt=tt, c=c, bank=bank, src=src: e.transpose(
                        pbh4[bank].ap()[:, 0:128], src[:, tt, 128 * c:128 * (c + 1)],
                        st["ident"][:]), waits=wl)
                    evf[tpos] = R("scalar", lambda e, tt=tt, c=c, bank=bank, dst=dst: e.copy(
                        dst[:, c, 128 * tt:128 * (tt + 1)], pbh4[bank].ap()[:, 0:128]),
                        waits=[m_tr])
                    tpos += 1
        m_fm = [evf[tpos - 1], evf[tpos - 2], evf[tpos - 3], evf[tpos - 4]]

        m_st_xn = R("sync", lambda e, g=g: e.dma_start(
            hbm["xn0"][:, :, 512 * g:512 * (g + 1)].rearrange("c p t -> p c t"), xnfm[:]),
            waits=m_fm, dma=True)
        rel_xnfm = [m_st_xn]

        m_ev = {}
        for part, dstsb, rel in ((0, qsb, rel_q), (1, ksb, rel_k)):
            for ec8 in range(8):
                ec = part * 8 + ec8
                bank = ec8 % 2
                key = (part, ec8)
                wl0 = list(m_fm)
                if g == 0 and ec == 0:
                    wl0.append(m_bias)
                if ec8 == 0:
                    wl0 += rel
                prevk = (part, ec8 - 2) if ec8 >= 2 else ((1 - part, 6 + ec8) if (part == 1) else None)
                if prevk in m_ev:
                    wl0.append(m_ev[prevk])
                mlast = None
                for dc in range(8):
                    mlast = R("tensor", lambda e, ec=ec, dc=dc, bank=bank: e.matmul(
                        pb[bank][:], wqkvT[:, dc, 128 * ec:128 * (ec + 1)], xn2fm[:, dc, :],
                        start=(dc == 0), stop=(dc == 7)), waits=(wl0 if dc == 0 else []))
                m_ev[key] = R("scalar", lambda e, ec=ec, ec8=ec8, bank=bank, dstsb=dstsb:
                              e.activation(dstsb[:, ec8, :], pb[bank][:], AF.Identity,
                                           bias=st["qkvbias"][:, ec:ec + 1]), waits=[mlast])
        m_q = [m_ev[(0, 7)], m_ev[(0, 6)]]
        m_k = [m_ev[(1, 7)], m_ev[(1, 6)]]
        rel_q = [R("sync", lambda e, g=g: e.dma_start(
            hbm["q0h"][:, :, 512 * g:512 * (g + 1)].rearrange("c p t -> p c t"), qsb[:]),
            waits=m_q, dma=True)]
        rel_k = [R("sync", lambda e, g=g: e.dma_start(
            hbm["k0h"][:, :, 512 * g:512 * (g + 1)].rearrange("c p t -> p c t"), ksb[:]),
            waits=m_k, dma=True)]

        m_vev = {}
        last_vm = None
        for tt in range(4):
            for vc in range(2):
                vi = tt * 2 + vc
                bank = 2 + vi % 2
                wl0 = [m_ev[(1, 7)]]
                if vi == 0:
                    wl0 += rel_v
                if vi >= 2:
                    wl0.append(m_vev[vi - 2])
                for dc in range(8):
                    last_vm = R("tensor", lambda e, tt=tt, vc=vc, dc=dc, bank=bank: e.matmul(
                        pb[bank][:], xn2fm[:, dc, 128 * tt:128 * (tt + 1)],
                        wqkvT[:, dc, 2048 + 512 * vc:2048 + 512 * (vc + 1)],
                        start=(dc == 0), stop=(dc == 7)), waits=(wl0 if dc == 0 else []))
                m_vev[vi] = R("scalar", lambda e, tt=tt, vc=vc, bank=bank: e.activation(
                    vsb[:, tt, 528 * vc:528 * (vc + 1)].rearrange("p (h x) -> p h x", h=8)[:, :, 0:64],
                    pb[bank][:].rearrange("p (h x) -> p h x", h=8), AF.Identity), waits=[last_vm])
        rel_tm = [last_vm]
        rel_v = [R("sync", lambda e, g=g: e.dma_start(
            hbm["v0h"][512 * g:512 * (g + 1), :].rearrange("(a p) x -> p a x", p=128), vsb[:]),
            waits=[m_vev[6], m_vev[7]], dma=True)]

    m_mt = R("tensor", lambda e: e.transpose(pb[4][0:32, 0:128], mcols[:, 0:32], st["ident_f"][:]),
             waits=[last_v] + m_fm)
    m_m1 = R("vector", lambda e: e.tensor_single_scalar(p2scr[0:32, 0:128], pb[4][0:32, 0:128],
                                                        0.0, ALU.is_gt), waits=[m_mt])
    m_m0 = R("vector", lambda e: e.tensor_scalar(p2scr[0:32, 128:256], p2scr[0:32, 0:128],
                                                 -1.0, 1.0, ALU.mult, ALU.add), waits=[m_m1])
    R("gpsimd", lambda e: e.dma_start(hbm["mask01"][1:2, :], p2scr[0:32, 0:128]),
      waits=[m_m0], dma=True)
    R("gpsimd", lambda e: e.dma_start(hbm["mask01"][0:1, :], p2scr[0:32, 128:256]),
      waits=[m_m0], dma=True)


def phase_l0_attn(b, st):
    I, hbm, pb = st["I"], st["hbm"], st["pb"]
    R = b.R
    pbh = [pb[6].bitcast(BF16), pb[7].bitcast(BF16)]

    off = 0
    wonat = man(st, "p3_wonat", [128, 8, 1024], BF16, off)
    o_sb = man(st, "p3_osb", [128, 8, 512], BF16, off)
    s_sb = man(st, "p3_ssb", [128, 4, 1024], F32, off + 8192)
    off += 24 * 1024
    woutT = man(st, "p3_woutT", [128, 8, 1024], BF16, off); off += 16 * 1024
    q_aug = man(st, "p3_qaug", [65, 16, 512], BF16, off); off += 16 * 1024
    k_aug = [man(st, "p3_kaug0", [65, 16, 512], BF16, off),
             man(st, "p3_kaug1", [65, 16, 512], BF16, off + 16 * 1024)]
    off += 32 * 1024
    v_st = [man(st, "p3_vst0", [128, 4, 1056], BF16, off),
            man(st, "p3_vst1", [128, 4, 1056], BF16, off + 8448)]
    off += 16896
    p_t = [man(st, f"p3_pt{i}", [128, 512], BF16, off + 1024 * i) for i in range(3)]
    off += 3 * 1024
    p_t2 = [man(st, f"p3_pt2{i}", [128, 512], BF16, off + 1024 * i) for i in range(3)]
    off += 3 * 1024
    cT = man(st, "p3_cT", [16, 512], BF16, off); off += 1024
    cg = man(st, "p3_cg", [128, 16], F32, off); off += 64
    mwin = man(st, "p3_mwin", [128, 16, 4, 2], F32, off); off += 512
    mtri = man(st, "p3_mtri", [128, 128], F32, off); off += 512
    s1m = man(st, "p3_s1m", [128, 128], F32, off); off += 512
    rrow = man(st, "p3_rrow", [1, 512], F32, off); off += 2048
    r_bc = man(st, "p3_rbc", [128, 512], F32, off); off += 2048
    o_t = man(st, "p3_ot", [64, 512], F32, off); off += 2048
    o_acc = man(st, "p3_oacc", [66, 16, 512], F32, off); off += 32 * 1024

    m1 = R("vector", lambda e: e.tensor_single_scalar(mtri[:], st["iotaD"][:, 0:128], 0.0, ALU.is_gt))
    m_mtri = R("vector", lambda e: e.tensor_scalar(mtri[:], mtri[:], -1e30, None, ALU.mult),
               waits=[m1])

    m_wo = R("gpsimd", lambda e: e.dma_start(
        wonat[:], I["L0_attn_w_out"].rearrange("(c p) e -> p c e", p=128)), dma=True)
    evw = {}
    for dc in range(8):
        for ec in range(8):
            t = dc * 8 + ec
            bank = t % 2
            wl = [m_wo] if t == 0 else []
            if t >= 2:
                wl.append(evw[t - 2])
            m_tr = R("tensor", lambda e, dc=dc, ec=ec, bank=bank: e.transpose(
                pbh[bank].ap()[:, 0:128], wonat[:, dc, 128 * ec:128 * (ec + 1)], st["ident"][:]),
                waits=wl)
            evw[t] = R("scalar", lambda e, dc=dc, ec=ec, bank=bank: e.copy(
                woutT[:, ec, 128 * dc:128 * (dc + 1)], pbh[bank].ap()[:, 0:128]), waits=[m_tr])
    m_woutT = [evw[62], evw[63]]

    m_krow = []
    for buf in range(2):
        m_krow.append(R("vector", lambda e, buf=buf: e.memset(k_aug[buf][64:65, :, :], -1.0)))

    qsrc = hbm["q0h"].rearrange("c (p2 d) t -> d (c p2) t", p2=2)
    ksrc = hbm["k0h"].rearrange("c (p2 d) t -> d (c p2) t", p2=2)

    rel_q = []
    rel_k = [[], []]
    rel_v = [[], []]
    rel_osb = []
    rel_ssb = []
    rel_pt = []
    rel_stb = []
    prev_evc = []
    add_by_bank = {}
    m_rbe = None
    for g in range(NG):
        gsl = slice(512 * g, 512 * (g + 1))
        m_q = R("gpsimd", lambda e, gsl=gsl: e.dma_start(q_aug[0:64, :, :], qsrc[:, :, gsl]),
                waits=rel_q + ([m_krow[0], m_krow[1], m_mtri] if g == 0 else []), dma=True)

        dbuf = g % 2
        m_k = R("sync", lambda e, g=g, dbuf=dbuf: e.dma_start(
            k_aug[dbuf][0:64, :, :], ksrc[:, :, 512 * g:512 * (g + 1)]),
            waits=rel_k[dbuf] + ([m_woutT[0]] if g == 0 else []), dma=True)
        red_marks = {}
        pp = 0
        last_p1 = None
        for h in range(16):
            for qt in range(4):
                bank = pp % 2
                N = 128 * qt + 128
                wl = [m_q, m_k]
                if pp >= 2:
                    wl += red_marks[pp - 2]
                elif rel_osb:
                    wl += rel_osb
                mm = R("tensor", lambda e, h=h, qt=qt, bank=bank, N=N, dbuf=dbuf: e.matmul(
                    pb[bank][:, 0:N], q_aug[0:64, h, 128 * qt:128 * (qt + 1)],
                    k_aug[dbuf][0:64, h, 0:N], start=True, stop=True), waits=wl)
                last_p1 = mm
                rm = []
                if qt > 0:
                    rm.append(R("vector", lambda e, h=h, qt=qt, bank=bank: e.reduce_max(
                        mwin[:, h, qt, 0:1], pb[bank][:, 0:128 * qt], axis=AX),
                        waits=[mm] + (prev_evc if pp == 0 else [])))
                mt = R("vector", lambda e, h=h, qt=qt, bank=bank: e.tensor_add(
                    s1m[:], pb[bank][:, 128 * qt:128 * qt + 128], mtri[:]),
                    waits=[mm] + (rm if rm else []) + (prev_evc if (pp == 0 and not rm) else []))
                rm.append(R("vector", lambda e, h=h, qt=qt: e.reduce_max(
                    mwin[:, h, qt, 1:2], s1m[:], axis=AX), waits=[mt]))
                red_marks[pp] = rm
                pp += 1
        rel_k[dbuf] = [last_p1]
        m_p1 = red_marks[pp - 1] + red_marks[pp - 2]

        ev_c = []
        for qt in range(4):
            mred = []
            for h in range(16):
                lo = 1 if qt == 0 else 0
                mred.append(R("vector", lambda e, h=h, qt=qt, lo=lo: e.reduce_max(
                    cg[:, h:h + 1], mwin[:, h, qt, lo:2], axis=AX),
                    waits=(m_p1 if (qt == 0 and h == 0) else [])
                    + ([ev_c[-1]] if (h == 0 and qt > 0) else [])))
            m_ct = R("tensor", lambda e, qt=qt: e.transpose(pb[4][0:16, 0:128], cg[:, 0:16],
                                                            st["ident_f"][:]),
                     waits=mred + ([m_rbe] if (qt == 0 and m_rbe) else []))
            ev_c.append(R("scalar", lambda e, qt=qt: e.copy(cT[:, 128 * qt:128 * (qt + 1)],
                                                            pb[4][0:16, 0:128]), waits=[m_ct]))
        m_cdma = R("gpsimd", lambda e: e.dma_start(q_aug[64:65, :, :], cT[:]),
                   waits=ev_c, dma=True)
        prev_evc = [ev_c[-1]]

        pp2 = 0
        pvcnt = 0
        pv_marks = {}
        exp_marks = {}
        st_marks = {}
        cur_w = -1
        m_k2 = m_v2 = None
        lkb = {0: None, 1: None}
        lkb[dbuf] = last_p1
        lvb = {0: None, 1: None}
        m_add = {}
        pend = None

        def rec_pv(t):
            wl2 = list(t["wl_pv"])
            if t["fst"] and t["pvb"] in add_by_bank:
                wl2.append(add_by_bank.pop(t["pvb"]))
            mpv = R("tensor", lambda e, h=t["h"], kc=t["kc"], pvb=t["pvb"], buf=t["buf"],
                    fst=t["fst"], fsp=t["fsp"], pvt=t["pvt"]: e.matmul(
                        pb[pvb][0:66, :], v_st[buf][:, kc, 66 * h:66 * h + 66],
                        pvt[:], start=fst, stop=fsp), waits=wl2)
            pv_marks[t["pp"]] = mpv
            lvb[t["buf"]] = mpv
            if t["fsp"]:
                h = t["h"]
                pvb = t["pvb"]
                if t["w"] == 0:
                    m_add[h] = R("vector", lambda e, h=h, pvb=pvb: e.tensor_copy(
                        o_acc[:, h, :], pb[pvb][0:66, :]), waits=[mpv])
                else:
                    m_add[h] = R("vector", lambda e, h=h, pvb=pvb: e.tensor_add(
                        o_acc[:, h, :], o_acc[:, h, :], pb[pvb][0:66, :]),
                        waits=[mpv, m_add[h]])
                add_by_bank[pvb] = m_add[h]

        for w in range(g + 1):
            buf = w % 2
            wk = [lkb[buf]] if lkb[buf] else rel_k[buf]
            wv = [lvb[buf]] if lvb[buf] else rel_v[buf]
            m_k2 = R("sync", lambda e, w=w, buf=buf: e.dma_start(
                k_aug[buf][0:64, :, :], ksrc[:, :, 512 * w:512 * (w + 1)]),
                waits=wk, dma=True)
            m_v2 = R("sync", lambda e, w=w, buf=buf: e.dma_start(
                v_st[buf][:], hbm["v0h"][512 * w:512 * (w + 1), :].rearrange(
                    "(a p) x -> p a x", p=128)),
                waits=wv, dma=True)
            for h in range(16):
                pvb = pvcnt % 4
                pvcnt += 1
                for kc in range(4):
                    stbank = 6 + pp2 % 2
                    rot = pp2 % 3
                    diag = (w == g)
                    wl = [m_k2, m_cdma]
                    if pp2 >= 2:
                        wl.append(exp_marks[pp2 - 2])
                    else:
                        wl += rel_stb
                    mm = R("tensor", lambda e, h=h, kc=kc, stbank=stbank, buf=buf: e.matmul(
                        pb[stbank][:], k_aug[buf][0:65, h, 128 * kc:128 * (kc + 1)],
                        q_aug[0:65, h, :], start=True, stop=True), waits=wl)
                    lkb[buf] = mm
                    st_marks[pp2] = mm
                    wl_e = [mm]
                    if pp2 >= 3:
                        wl_e.append(pv_marks[pp2 - 3])
                    elif rel_pt:
                        wl_e.append(rel_pt[min(pp2, len(rel_pt) - 1)])
                    m_exp = R("scalar", lambda e, rot=rot, stbank=stbank: e.activation(
                        p_t[rot][:], pb[stbank][:], AF.Exp), waits=wl_e)
                    exp_marks[pp2] = m_exp
                    wl_pv = [m_exp, m_v2]
                    pvt = p_t[rot]
                    if diag:
                        m_af = R("gpsimd", lambda e, rot=rot, kc=kc: e.affine_select(
                            p_t2[rot][:], p_t[rot][:], [[1, 512]], ALU.is_ge, 0.0,
                            base=-128 * kc, channel_multiplier=-1), waits=[m_exp])
                        wl_pv = [m_af, m_v2]
                        pvt = p_t2[rot]
                    t = dict(pp=pp2, h=h, kc=kc, pvb=pvb, buf=buf, pvt=pvt, wl_pv=wl_pv,
                             fst=(kc == 0), fsp=(kc == 3), w=w)
                    if pend is not None:
                        rec_pv(pend)
                    pend = t
                    pp2 += 1
        rec_pv(pend)
        pend = None
        for bb in (0, 1):
            if lkb[bb]:
                rel_k[bb] = [lkb[bb]]
            if lvb[bb]:
                rel_v[bb] = [lvb[bb]]
        rel_stb = [exp_marks[pp2 - 1], exp_marks[pp2 - 2]]
        rel_pt = [pv_marks[k] for k in sorted(pv_marks)[-3:]]
        rel_q = [st_marks[pp2 - 1]]

        ev_o = []
        m_on_prev = None
        for h in range(16):
            m_r = R("vector", lambda e, h=h: e.reciprocal(rrow[:], o_acc[64:65, h, :]),
                    waits=[m_add[h]] + ([m_rbe] if m_rbe else []))
            m_rb = R("tensor", lambda e: e.matmul(pb[4][:], st["ones_fr"][:], rrow[:],
                                                  start=True, stop=True),
                     waits=[m_r] + ([m_rbe] if m_rbe else []))
            m_rbe = R("scalar", lambda e: e.copy(r_bc[:], pb[4][:]), waits=[m_rb])
            m_on = R("vector", lambda e, h=h: e.tensor_mul(
                o_t[:], o_acc[0:64, h, :], r_bc[0:64, :]),
                waits=[m_rbe] + ([m_on_prev] if m_on_prev else []))
            m_on_prev = m_on
            vb = st["qkvbias"][(h % 2) * 64:(h % 2) * 64 + 64, 16 + h // 2:17 + h // 2]
            ev_o.append(R("scalar", lambda e, h=h, vb=vb: e.activation(
                o_sb[(h % 2) * 64:(h % 2) * 64 + 64, h // 2, :], o_t[:], AF.Identity,
                bias=vb), waits=[m_on] + (rel_osb if (g > 0 and h == 0) else []
                                          + ([m_woutT[1]] if g == 0 and h == 0 else []))))

        ev_s = {}
        for tt in range(4):
            for ds in range(2):
                ti2 = tt * 2 + ds
                bank = ti2 % 2
                wl = []
                if ti2 == 0:
                    wl = list(ev_o) + rel_ssb
                    if 0 in add_by_bank:
                        wl.append(add_by_bank.pop(0))
                if ti2 == 1 and 1 in add_by_bank:
                    wl.append(add_by_bank.pop(1))
                if ti2 >= 2:
                    wl.append(ev_s[ti2 - 2])
                mlast = None
                for ec in range(8):
                    mlast = R("tensor", lambda e, tt=tt, ds=ds, ec=ec, bank=bank: e.matmul(
                        pb[bank][:], o_sb[:, ec, 128 * tt:128 * (tt + 1)],
                        woutT[:, ec, 512 * ds:512 * (ds + 1)],
                        start=(ec == 0), stop=False), waits=(wl if ec == 0 else []))
                mlast = R("tensor", lambda e, ds=ds, bank=bank: e.matmul(
                    pb[bank][:], st["ones_bf"][:], st["bout_row"][0:1, 512 * ds:512 * (ds + 1)],
                    start=False, stop=True), waits=[])
                ev_s[ti2] = R("scalar", lambda e, tt=tt, ds=ds, bank=bank: e.copy(
                    s_sb[:, tt, 512 * ds:512 * (ds + 1)], pb[bank][:]), waits=[mlast])
        m_sst = R("sync", lambda e, g=g, gsl=gsl: e.dma_start(
            hbm["s0h"][gsl, :].rearrange("(a p) d -> p a d", p=128), s_sb[:]),
            waits=[ev_s[6], ev_s[7]], dma=True)
        rel_ssb = [m_sst]
        rel_osb = [ev_s[6], ev_s[7]]
        add_by_bank[0] = ev_s[6]
        add_by_bank[1] = ev_s[7]


def phase_l0_ffn(b, st):
    I, hbm, pb = st["I"], st["hbm"], st["pb"]
    R = b.R

    off = 0
    xnr = man(st, "p4_xnr", [128, 8, T], BF16, off); off += 64 * 1024
    w1h = man(st, "p4_w1h", [128, 8, 2048], BF16, off)
    yg = man(st, "p4_yg", [128, 4, 1024], F32, off)
    sg = man(st, "p4_sg", [128, 4, 1024], F32, off + 16384)
    off += 32 * 1024
    w2h = man(st, "p4_w2h", [128, 16, 1024], BF16, off)
    xg2 = man(st, "p4_xg2", [128, 4, 1024], F32, off)
    x1s = man(st, "p4_x1s", [128, 4, 1024], F32, off + 16384)
    off += 32 * 1024
    xm = man(st, "p4_xm", [128, 8, 512], BF16, off); off += 8 * 1024
    h_sb = man(st, "p4_hsb", [128, 16, 512], BF16, off); off += 16 * 1024
    ysb = man(st, "p4_ysb", [128, 4, 1024], F32, off); off += 16 * 1024
    mrow = man(st, "p4_mrow", [1, T], BF16, off); off += 8 * 1024
    mbc = man(st, "p4_mbc", [128, 512], BF16, off); off += 1024

    m_xn = R("gpsimd", lambda e: e.dma_start(xnr[:], hbm["xn0"].rearrange("c p t -> p c t")),
             dma=True)


    WSRC = [(I["L0_ffn_w1"][0], I["L0_ffn_w2"][0]), (I["L0_ffn_w1"][1], I["L0_ffn_w2"][1]),
            (I["L0_ffn_sw1"], I["L0_ffn_sw2"])]
    rel_w = []
    rel_xm = []
    rel_hsb = []
    rel_ysb = []
    rel_yacc = {}
    rel_mbc = []
    first = True
    for e_i in range(3):
        w1s, w2s = WSRC[e_i]
        if e_i < 2:
            m_mr = R("sync", lambda e, e_i=e_i: e.dma_start(
                mrow[:], hbm["mask01"][e_i:e_i + 1, :]), waits=rel_mbc, dma=True)
        for hh in range(2):
            m_w1 = R("gpsimd", lambda e, w1s=w1s, hh=hh: e.dma_start(
                w1h[:], w1s[:, 2048 * hh:2048 * (hh + 1)].rearrange("(c p) h -> p c h", p=128)),
                waits=rel_w, dma=True)
            m_w2 = R("gpsimd", lambda e, w2s=w2s, hh=hh: e.dma_start(
                w2h[:], w2s[2048 * hh:2048 * (hh + 1), :].rearrange("(c p) d -> p c d", p=128)),
                waits=rel_w, dma=True)
            for g in range(NG):
                gsl = slice(512 * g, 512 * (g + 1))
                if e_i < 2:
                    m_mm = R("tensor", lambda e, e_i=e_i, g=g: e.matmul(
                        pb[6][:], st["ones_bf"][:], mrow[0:1, 512 * g:512 * (g + 1)],
                        start=True, stop=True), waits=[m_mr] + rel_mbc)
                    m_mb = R("scalar", lambda e: e.copy(mbc[:], pb[6][:]), waits=[m_mm])
                    rel_mbc = [m_mb]
                    m_xm = None
                    for c in range(8):
                        m_xm = R("vector", lambda e, c=c, g=g: e.tensor_mul(
                            xm[:, c, :], xnr[:, c, 512 * g:512 * (g + 1)], mbc[:]),
                            waits=([m_mb, m_xn] + rel_xm) if c == 0 else [])
                    rhs_x = xm
                    xoff = 0
                    m_rx = m_xm
                else:
                    rhs_x = xnr
                    xoff = 512 * g
                    m_rx = m_xn if first else None

                ev_h = {}
                for hc in range(16):
                    bank = 4 + hc % 2
                    wl = []
                    if hc == 0:
                        wl = [m_w1] + ([m_rx] if m_rx else []) + rel_hsb
                    if hc >= 2:
                        wl.append(ev_h[hc - 2])
                    mlast = None
                    for dc in range(8):
                        mlast = R("tensor", lambda e, hc=hc, dc=dc, bank=bank, rhs_x=rhs_x,
                                  xoff=xoff: e.matmul(
                                      pb[bank][:], w1h[:, dc, 128 * hc:128 * (hc + 1)],
                                      rhs_x[:, dc, xoff:xoff + 512],
                                      start=(dc == 0), stop=(dc == 7)),
                                  waits=(wl if dc == 0 else []))
                    bcol = 32 * e_i + 16 * hh + hc
                    ev_h[hc] = R("scalar", lambda e, hc=hc, bank=bank, bcol=bcol: e.activation(
                        h_sb[:, hc, :], pb[bank][:], AF.Relu, bias=st["b1e"][:, bcol:bcol + 1]),
                        waits=[mlast])
                rel_hsb = []

                ev_y = {}
                for tt in range(4):
                    for ds in range(2):
                        ti2 = tt * 2 + ds
                        bank = ti2 % 4
                        wl = [m_w2]
                        if ti2 == 0:
                            wl += rel_ysb
                        if ti2 < 2:
                            wl += [ev_h[14], ev_h[15]]
                        if ti2 >= 4:
                            wl.append(ev_y[ti2 - 4])
                        mlast = None
                        for hc in range(16):
                            mlast = R("tensor", lambda e, tt=tt, ds=ds, hc=hc, bank=bank: e.matmul(
                                pb[bank][:], h_sb[:, hc, 128 * tt:128 * (tt + 1)],
                                w2h[:, hc, 512 * ds:512 * (ds + 1)],
                                start=(hc == 0), stop=(hc == 15 and not first)),
                                waits=(wl if hc == 0 else []))
                        if first:
                            mlast = R("tensor", lambda e, ds=ds, bank=bank: e.matmul(
                                pb[bank][:], st["ones_bf"][:],
                                st["b2s_row"][0:1, 512 * ds:512 * (ds + 1)],
                                start=False, stop=True), waits=[])
                        ev_y[ti2] = R("scalar", lambda e, tt=tt, ds=ds, bank=bank: e.copy(
                            ysb[:, tt, 512 * ds:512 * (ds + 1)], pb[bank][:]), waits=[mlast])
                rel_hsb = [mlast]
                wl = [ev_y[k] for k in (6, 7)] + ([rel_yacc[g]] if g in rel_yacc else [])
                if first:
                    m_acc = R("gpsimd", lambda e, g=g, gsl=gsl: e.dma_start(
                        hbm["y0h"][gsl, :].rearrange("(a p) d -> p a d", p=128), ysb[:]),
                        waits=wl, dma=True)
                else:
                    m_acc = R("gpsimd", lambda e, g=g, gsl=gsl: e.dma_start(
                        hbm["y0h"][gsl, :].rearrange("(a p) d -> p a d", p=128), ysb[:],
                        accum_op=ALU.add), waits=wl, dma=True)
                rel_yacc[g] = m_acc
                rel_ysb = [m_acc]
                rel_xm = [mlast]
            rel_w = [mlast, ev_h[15]]
            first = False

    rel_bufs = []
    for g in range(NG):
        gsl = slice(512 * g, 512 * (g + 1))
        m_y = R("sync", lambda e, g=g, gsl=gsl: e.dma_start(
            yg[:], hbm["y0h"][gsl, :].rearrange("(a p) d -> p a d", p=128)),
            waits=[rel_yacc[g]] + rel_bufs + (rel_w if g == 0 else []), dma=True)
        m_s = R("sync", lambda e, g=g, gsl=gsl: e.dma_start(
            sg[:], hbm["s0h"][gsl, :].rearrange("(a p) d -> p a d", p=128)), dma=True)
        m_x = R("sync", lambda e, g=g, gsl=gsl: e.dma_start(
            xg2[:], I["x0"][gsl, :].rearrange("(a p) d -> p a d", p=128)), dma=True)
        m_si = R("scalar", lambda e: e.activation(x1s[:], yg[:], AF.Silu), waits=[m_y] + rel_bufs)
        m_t1 = R("vector", lambda e: e.tensor_mul(yg[:], sg[:], x1s[:]), waits=[m_si, m_s])
        m_t2 = R("vector", lambda e: e.tensor_add(x1s[:], xg2[:], yg[:]), waits=[m_t1, m_x])
        m_st = R("sync", lambda e, g=g, gsl=gsl: e.dma_start(
            hbm["x1h"][gsl, :].rearrange("(a p) d -> p a d", p=128), x1s[:]),
            waits=[m_t2], dma=True)
        rel_bufs = [m_st, m_t2]


def phase_l1_ln(b, st):
    I, hbm, pb = st["I"], st["hbm"], st["pb"]
    R = b.R
    pbh = [pb[6].bitcast(BF16), pb[7].bitcast(BF16)]

    off = 0
    xg = [man(st, "p5_xg0", [128, 4, 1024], F32, off),
          man(st, "p5_xg1", [128, 4, 1024], F32, off + 16 * 1024)]
    off += 32 * 1024
    xn_tm = man(st, "p5_xntm", [128, 4, 1024], BF16, off); off += 8 * 1024
    xnn_tm = man(st, "p5_xnntm", [128, 4, 1024], BF16, off); off += 8 * 1024
    xnfm = man(st, "p5_xnfm", [128, 8, 512], BF16, off); off += 8 * 1024
    xnnfm = man(st, "p5_xnnfm", [128, 8, 512], BF16, off); off += 8 * 1024
    xifm = man(st, "p5_xifm", [128, 8, 512], BF16, off); off += 8 * 1024
    xqfm = man(st, "p5_xqfm", [128, 8, 512], BF16, off); off += 8 * 1024
    xkfm = man(st, "p5_xkfm", [128, 8, 512], BF16, off); off += 8 * 1024
    wrbc = man(st, "p5_wrbc", [128, 8, 1024], F32, off); off += 32 * 1024
    wrfm = man(st, "p5_wrfm", [128, 8, 8], F32, off); off += 256
    wrows = man(st, "p5_wrows", [8, 8, 128], F32, off); off += 4 * 1024
    stt = man(st, "p5_stt", [128, 24], F32, off); off += 96
    zc = man(st, "p5_zc", [128, 32, 8], F32, off); off += 1024
    RT = {}
    for i, nm in enumerate(("inh", "dist", "ez", "fb", "t1", "t2", "w", "fbw", "wf", "tmp")):
        RT[nm] = man(st, f"p5_rt_{nm}", [128, 32, 8], F32, off); off += 1024
    RC = {}
    for i, nm in enumerate(("ws", "ne", "m1", "m2", "fbs", "rfb", "om", "ws2", "rw2")):
        RC[nm] = man(st, f"p5_rc_{nm}", [128, 32], F32, off); off += 128
    off = (off + 31) // 32 * 32
    rw_sb = man(st, "p5_rwsb", [8, T], BF16, off); off += 8 * 1024
    cgoldb = man(st, "p5_cgold", [128, 1], F32, off); off += 32
    sq5 = man(st, "p5_sq", [128, 1024], F32, off); off += 4 * 1024

    m_cg = R("vector", lambda e: e.memset(cgoldb[:], -0.36787944117144233))

    m_wr = R("gpsimd", lambda e: e.dma_start(
        wrfm[:], I["L1_ffn_Wr"].rearrange("(c p) e -> p c e", p=128)), dma=True)
    ev_wr = {}
    for c in range(8):
        m_t = R("tensor", lambda e, c=c: e.transpose(pb[4][0:8, 0:128], wrfm[:, c, :],
                                                     st["ident_f"][:]),
                waits=[m_wr] + ([ev_wr[c - 1]] if c else []))
        ev_wr[c] = R("scalar", lambda e, c=c: e.copy(wrows[:, c, :], pb[4][0:8, 0:128]),
                     waits=[m_t])
    m_wrst = R("sync", lambda e: e.dma_start(hbm["wrr"].rearrange("e (c d) -> e c d", c=8),
                                             wrows[:]), waits=[ev_wr[7]], dma=True)
    prev_ev = None
    for e_i in range(8):
        m_ld = R("sync", lambda e, e_i=e_i: e.dma_start(
            st["rowB"][0:1, :], hbm["wrr"][e_i:e_i + 1, :]),
            waits=[m_wrst] + ([prev_ev] if prev_ev else []), dma=True)
        for half in range(2):
            m_mm = R("tensor", lambda e, half=half: e.matmul(
                pb[4][:], st["ones_fr"][:], st["rowB"][0:1, 512 * half:512 * (half + 1)],
                start=True, stop=True), waits=[m_ld] + ([prev_ev] if prev_ev else []))
            prev_ev = R("scalar", lambda e, e_i=e_i, half=half: e.copy(
                wrbc[:, e_i, 512 * half:512 * (half + 1)], pb[4][:]), waits=[m_mm])

    rel_x = [[], []]
    rel_tm = []
    rel_fm = []
    last_v = None
    for g in range(NG):
        buf = g % 2
        xga = xg[buf]
        gsl = slice(512 * g, 512 * (g + 1))
        m_x = R("sync", lambda e, gsl=gsl, xga=xga: e.dma_start(
            xga[:], hbm["x1h"][gsl, :].rearrange("(a p) d -> p a d", p=128)),
            waits=rel_x[buf], dma=True)
        m_tm = []
        for tt in range(4):
            ti = 4 * g + tt
            m1 = R("vector", lambda e, xga=xga, tt=tt: e.bn_stats(stt[:, 0:6], xga[:, tt, 0:512]),
                   waits=[m_x] + ([m_cg, prev_ev] if g == 0 and tt == 0 else [])
                   + (rel_tm if tt == 0 else []) + ([last_v] if last_v else []))
            m2 = R("vector", lambda e, xga=xga, tt=tt: e.bn_stats(stt[:, 6:12], xga[:, tt, 512:1024]),
                   waits=[m1])
            m3 = R("vector", lambda e: e.bn_aggr(stt[:, 12:14], stt[:, 0:12]), waits=[m2])
            mr1 = R("scalar", lambda e: e.activation(stt[:, 14:15], stt[:, 13:14], AF.Ln,
                                                     bias=st["epsb"][:]), waits=[m3])
            mr2 = R("scalar", lambda e: e.activation(stt[:, 15:16], stt[:, 14:15], AF.Exp,
                                                     scale=-0.5), waits=[mr1])
            ma1 = R("vector", lambda e: e.tensor_scalar(stt[:, 16:17], stt[:, 13:14], EPS, None,
                                                        ALU.add), waits=[m3])
            ma2 = R("vector", lambda e: e.reciprocal(stt[:, 17:18], stt[:, 16:17]), waits=[ma1])
            ma3 = R("vector", lambda e: e.tensor_mul(stt[:, 18:19], stt[:, 13:14], stt[:, 17:18]),
                    waits=[ma2])
            ma4 = R("scalar", lambda e: e.activation(stt[:, 19:20], stt[:, 18:19], AF.Ln,
                                                     bias=st["epsb"][:]), waits=[ma3, mr2])
            ma5 = R("scalar", lambda e: e.activation(stt[:, 20:21], stt[:, 19:20], AF.Exp,
                                                     scale=-0.5), waits=[ma4])
            mra = R("vector", lambda e: e.tensor_mul(stt[:, 21:22], stt[:, 15:16], stt[:, 20:21]),
                    waits=[ma5])
            mx1 = R("vector", lambda e, xga=xga, tt=tt: e.tensor_scalar(
                xn_tm[:, tt, :], xga[:, tt, :], stt[:, 12:13], stt[:, 15:16],
                ALU.subtract, ALU.mult), waits=[mra])
            mx2 = R("vector", lambda e, xga=xga, tt=tt: e.tensor_scalar(
                xnn_tm[:, tt, :], xga[:, tt, :], stt[:, 12:13], stt[:, 21:22],
                ALU.subtract, ALU.mult), waits=[mx1])
            mz = mx2
            for e_i in range(8):
                mz = R("vector", lambda e, xga=xga, tt=tt, ti=ti, e_i=e_i: e.scalar_tensor_tensor(
                    sq5[:], xga[:, tt, :], stt[:, 12:13], wrbc[:, e_i, :],
                    ALU.subtract, ALU.mult, accum_out=zc[:, ti, e_i:e_i + 1]), waits=[mz])
            last_v = R("vector", lambda e, ti=ti: e.tensor_scalar(
                zc[:, ti, :], zc[:, ti, :], stt[:, 15:16], None, ALU.mult), waits=[mz])
            m_tm.append(last_v)
        rel_x[buf] = [last_v]

        tpos = 0
        evf = {}
        for tt in range(4):
            for c in range(8):
                for (src, dst) in ((xn_tm, xnfm), (xnn_tm, xnnfm)):
                    bank = tpos % 2
                    wl = [m_tm[tt]]
                    if tpos == 0:
                        wl += rel_fm
                    if tpos >= 2:
                        wl.append(evf[tpos - 2])
                    m_tr = R("tensor", lambda e, tt=tt, c=c, bank=bank, src=src: e.transpose(
                        pbh[bank].ap()[:, 0:128], src[:, tt, 128 * c:128 * (c + 1)],
                        st["ident"][:]), waits=wl)
                    evf[tpos] = R("scalar", lambda e, tt=tt, c=c, bank=bank, dst=dst: e.copy(
                        dst[:, c, 128 * tt:128 * (tt + 1)], pbh[bank].ap()[:, 0:128]),
                        waits=[m_tr])
                    tpos += 1
        m_fm = [evf[tpos - 1], evf[tpos - 2]]
        ev_ap = []
        for idx, dstt in ((0, xifm), (1, xqfm), (2, xkfm)):
            for c in range(8):
                ev_ap.append(R("vector", lambda e, idx=idx, dstt=dstt, c=c: e.tensor_scalar(
                    dstt[:, c, :], xnnfm[:, c, :], st["l1w"][:, 8 * idx + c:8 * idx + c + 1],
                    st["l1b"][:, 8 * idx + c:8 * idx + c + 1], ALU.mult, ALU.add),
                    waits=(m_fm if (idx == 0 and c == 0) else []) + (rel_fm if (idx == 0 and c == 0) else [])))
        m_st = [
            R("sync", lambda e, gsl=gsl: e.dma_start(
                hbm["xn1"][:, :, gsl].rearrange("c p t -> p c t"), xnfm[:]),
                waits=m_fm, dma=True),
            R("sync", lambda e, gsl=gsl: e.dma_start(
                hbm["xin1"][:, :, gsl].rearrange("c p t -> p c t"), xifm[:]),
                waits=ev_ap[0:8], dma=True),
            R("sync", lambda e, gsl=gsl: e.dma_start(
                hbm["xq1"][:, :, gsl].rearrange("c p t -> p c t"), xqfm[:]),
                waits=ev_ap[8:16], dma=True),
            R("sync", lambda e, gsl=gsl: e.dma_start(
                hbm["xk1"][:, :, gsl].rearrange("c p t -> p c t"), xkfm[:]),
                waits=ev_ap[16:24], dma=True),
        ]
        rel_fm = list(m_st) + [ev_ap[-1]]
        rel_tm = [evf[tpos - 1]]

    zall = zc[:].rearrange("p a b -> p (a b)")
    m_inh = R("scalar", lambda e: e.activation(RT["inh"][:].rearrange("p a b -> p (a b)"), zall,
                                               AF.Sigmoid, scale=st["rtemp_bc"][:]),
              waits=[last_v])
    m_d = R("scalar", lambda e: e.activation(RT["dist"][:].rearrange("p a b -> p (a b)"),
                                             RT["inh"][:].rearrange("p a b -> p (a b)"),
                                             AF.Abs, bias=cgoldb[:]), waits=[m_inh])
    m_ez = R("scalar", lambda e: e.activation(RT["ez"][:].rearrange("p a b -> p (a b)"),
                                              RT["dist"][:].rearrange("p a b -> p (a b)"),
                                              AF.Exp, scale=-10.0), waits=[m_d])
    m_fb = R("scalar", lambda e: e.activation(RT["fb"][:].rearrange("p a b -> p (a b)"),
                                              RT["dist"][:].rearrange("p a b -> p (a b)"),
                                              AF.Exp, scale=-1.0 / 0.3), waits=[m_ez])
    GL = 0.5 - float(np.log(4.0 / 3.0))
    m_t1 = R("vector", lambda e: e.tensor_single_scalar(RT["t1"][:], RT["inh"][:], GL, ALU.is_ge),
             waits=[m_inh])
    m_t2 = R("vector", lambda e: e.tensor_single_scalar(RT["t2"][:], RT["inh"][:], 0.5, ALU.is_le),
             waits=[m_t1])
    m_zn = R("vector", lambda e: e.tensor_mul(RT["t1"][:], RT["t1"][:], RT["t2"][:]), waits=[m_t2])
    m_w = R("vector", lambda e: e.tensor_mul(RT["w"][:], RT["ez"][:], RT["t1"][:]),
            waits=[m_zn, m_ez])
    m_ws = R("vector", lambda e: e.reduce_sum(RC["ws"][:], RT["w"][:], axis=AX), waits=[m_w])
    m_ne = R("vector", lambda e: e.tensor_single_scalar(RC["ne"][:], RC["ws"][:], 1e-8, ALU.is_lt),
             waits=[m_ws])
    m_m1 = R("vector", lambda e: e.reduce_max(RC["m1"][:], RT["fb"][:], axis=AX), waits=[m_fb])
    m_eq = R("vector", lambda e: e.tensor_tensor(RT["t2"][:], RT["fb"][:],
                                                 RC["m1"][:].unsqueeze(2).broadcast_to((128, 32, 8)),
                                                 ALU.is_ge), waits=[m_m1])
    m_fbm = R("vector", lambda e: e.scalar_tensor_tensor(RT["tmp"][:], RT["t2"][:], -1e30,
                                                         RT["fb"][:], ALU.mult, ALU.add),
              waits=[m_eq])
    m_m2 = R("vector", lambda e: e.reduce_max(RC["m2"][:], RT["tmp"][:], axis=AX), waits=[m_fbm])
    m_k2 = R("vector", lambda e: e.tensor_tensor(RT["t2"][:], RT["fb"][:],
                                                 RC["m2"][:].unsqueeze(2).broadcast_to((128, 32, 8)),
                                                 ALU.is_ge), waits=[m_m2])
    m_fbw = R("vector", lambda e: e.tensor_mul(RT["fbw"][:], RT["fb"][:], RT["t2"][:]),
              waits=[m_k2])
    m_fbs = R("vector", lambda e: e.reduce_sum(RC["fbs"][:], RT["fbw"][:], axis=AX), waits=[m_fbw])
    m_fbc = R("vector", lambda e: e.tensor_scalar_max(RC["fbs"][:], RC["fbs"][:], 1e-8),
              waits=[m_fbs])
    m_rfb = R("vector", lambda e: e.reciprocal(RC["rfb"][:], RC["fbs"][:]), waits=[m_fbc])
    m_fbn = R("vector", lambda e: e.tensor_tensor(RT["fbw"][:], RT["fbw"][:],
                                                  RC["rfb"][:].unsqueeze(2).broadcast_to((128, 32, 8)),
                                                  ALU.mult), waits=[m_rfb])
    m_om = R("vector", lambda e: e.tensor_scalar(RC["om"][:], RC["ne"][:], -1.0, 1.0,
                                                 ALU.mult, ALU.add), waits=[m_ne])
    m_wa = R("vector", lambda e: e.tensor_tensor(RT["w"][:], RT["w"][:],
                                                 RC["om"][:].unsqueeze(2).broadcast_to((128, 32, 8)),
                                                 ALU.mult), waits=[m_om, m_ws])
    m_wb = R("vector", lambda e: e.tensor_tensor(RT["fbw"][:], RT["fbw"][:],
                                                 RC["ne"][:].unsqueeze(2).broadcast_to((128, 32, 8)),
                                                 ALU.mult), waits=[m_fbn, m_wa])
    m_wf = R("vector", lambda e: e.tensor_add(RT["wf"][:], RT["w"][:], RT["fbw"][:]),
             waits=[m_wb])
    m_ws2 = R("vector", lambda e: e.reduce_sum(RC["ws2"][:], RT["wf"][:], axis=AX), waits=[m_wf])
    m_wsc = R("vector", lambda e: e.tensor_scalar_max(RC["ws2"][:], RC["ws2"][:], 1e-8),
              waits=[m_ws2])
    m_rw2 = R("vector", lambda e: e.reciprocal(RC["rw2"][:], RC["ws2"][:]), waits=[m_wsc])
    m_wn = R("vector", lambda e: e.tensor_tensor(RT["wf"][:], RT["wf"][:],
                                                 RC["rw2"][:].unsqueeze(2).broadcast_to((128, 32, 8)),
                                                 ALU.mult), waits=[m_rw2])
    prev = None
    for ti in range(32):
        m_t = R("tensor", lambda e, ti=ti: e.transpose(pb[4][0:8, 0:128], RT["wf"][:, ti, :],
                                                       st["ident_f"][:]),
                waits=[m_wn] + ([prev] if prev else []))
        prev = R("scalar", lambda e, ti=ti: e.copy(rw_sb[:, 128 * ti:128 * (ti + 1)],
                                                   pb[4][0:8, 0:128]), waits=[m_t])
    R("sync", lambda e: e.dma_start(hbm["rw1"], rw_sb[:]), waits=[prev], dma=True)


def phase_l1_winp(b, st):
    I, hbm, pb = st["I"], st["hbm"], st["pb"]
    R = b.R

    off = 0
    winv = man(st, "p6_winv", [128, 8, 2048], BF16, off); off += 32 * 1024
    winp = man(st, "p6_winp", [128, 8, 2048], BF16, off); off += 32 * 1024
    xi = [man(st, "p6_xi0", [128, 8, 512], BF16, off),
          man(st, "p6_xi1", [128, 8, 512], BF16, off + 8192)]
    off += 16 * 1024
    vsb = man(st, "p6_vsb", [128, 4, 2048], BF16, off); off += 16 * 1024
    psb = man(st, "p6_psb", [128, 16, 512], BF16, off); off += 16 * 1024

    m_wv = R("gpsimd", lambda e: e.dma_start(
        winv[:], I["L1_attn_W_in"][:, 0:2048].rearrange("(c p) o -> p c o", p=128)), dma=True)
    m_wp = R("gpsimd", lambda e: e.dma_start(
        winp[:], I["L1_attn_W_in"][:, 2048:4096].rearrange("(c p) o -> p c o", p=128)), dma=True)

    rel_xi = [[], []]
    rel_v = []
    rel_p = []
    for g in range(NG):
        buf = g % 2
        gsl = slice(512 * g, 512 * (g + 1))
        m_x = R("gpsimd", lambda e, gsl=gsl, buf=buf: e.dma_start(
            xi[buf][:], hbm["xin1"][:, :, gsl].rearrange("c p t -> p c t")),
            waits=rel_xi[buf], dma=True)
        ev_v = {}
        for tt in range(4):
            for vs in range(4):
                ti2 = tt * 4 + vs
                bank = ti2 % 2
                wl = [m_x, m_wv] if ti2 == 0 else []
                if ti2 == 0:
                    wl += rel_v
                if ti2 >= 2:
                    wl.append(ev_v[ti2 - 2])
                mlast = None
                for dc in range(8):
                    mlast = R("tensor", lambda e, tt=tt, vs=vs, dc=dc, bank=bank, buf=buf: e.matmul(
                        pb[bank][:], xi[buf][:, dc, 128 * tt:128 * (tt + 1)],
                        winv[:, dc, 512 * vs:512 * (vs + 1)],
                        start=(dc == 0), stop=(dc == 7)), waits=(wl if dc == 0 else []))
                ev_v[ti2] = R("scalar", lambda e, tt=tt, vs=vs, bank=bank: e.copy(
                    vsb[:, tt, 512 * vs:512 * (vs + 1)], pb[bank][:]), waits=[mlast])
        rel_v = [R("sync", lambda e, gsl=gsl: e.dma_start(
            hbm["v1h"][gsl, :].rearrange("(a p) x -> p a x", p=128), vsb[:]),
            waits=[ev_v[14], ev_v[15]], dma=True)]
        ev_p = {}
        for pc in range(16):
            bank = 2 + pc % 2
            wl = [m_x, m_wp] if pc == 0 else []
            if pc == 0:
                wl += rel_p
            if pc >= 2:
                wl.append(ev_p[pc - 2])
            mlast = None
            for dc in range(8):
                mlast = R("tensor", lambda e, pc=pc, dc=dc, bank=bank, buf=buf: e.matmul(
                    pb[bank][:], winp[:, dc, 128 * pc:128 * (pc + 1)], xi[buf][:, dc, :],
                    start=(dc == 0), stop=(dc == 7)), waits=(wl if dc == 0 else []))
            ev_p[pc] = R("scalar", lambda e, pc=pc, bank=bank: e.copy(
                psb[:, pc, :], pb[bank][:]), waits=[mlast])
        rel_p = [R("sync", lambda e, gsl=gsl: e.dma_start(
            hbm["p1h"][:, :, gsl].rearrange("c p t -> p c t"), psb[:]),
            waits=[ev_p[14], ev_p[15]], dma=True)]
        rel_xi[buf] = [mlast, ev_v[15]]




def phase_l1_qk(b, st):
    I, hbm, pb = st["I"], st["hbm"], st["pb"]
    R = b.R
    pbh = [pb[6].bitcast(BF16), pb[7].bitcast(BF16)]

    off = 0
    wq = man(st, "p6b_wq", [128, 8, 2048], BF16, off); off += 32 * 1024
    wk = man(st, "p6b_wk", [128, 8, 2048], BF16, off); off += 32 * 1024
    xq = [man(st, "p6b_xq0", [128, 8, 512], BF16, off),
          man(st, "p6b_xq1", [128, 8, 512], BF16, off + 8192)]
    off += 16 * 1024
    xk = [man(st, "p6b_xk0", [128, 8, 512], BF16, off),
          man(st, "p6b_xk1", [128, 8, 512], BF16, off + 8192)]
    off += 16 * 1024
    tm = man(st, "p6b_tm", [128, 4, 2048], BF16, off); off += 16 * 1024
    tb = man(st, "p6b_tb", [128, 4, 2048], BF16, off); off += 16 * 1024
    qfm = man(st, "p6b_qfm", [128, 16, 512], BF16, off); off += 16 * 1024
    kfm = man(st, "p6b_kfm", [128, 16, 512], BF16, off); off += 16 * 1024
    ksm = man(st, "p6b_ksm", [128, 16, 512], BF16, off); off += 16 * 1024
    kprev = man(st, "p6b_kprev", [128, 16], BF16, off); off += 32
    sfx = man(st, "p6b_sfx", [128, 64], F32, off); off += 256
    bqrow = st["rowA"].bitcast(BF16)
    bkrow = st["rowB"].bitcast(BF16)

    m_wq = R("gpsimd", lambda e: e.dma_start(
        wq[:], I["L1_attn_Wq"].rearrange("(c p) o -> p c o", p=128)), dma=True)
    m_wk = R("gpsimd", lambda e: e.dma_start(
        wk[:], I["L1_attn_Wk"].rearrange("(c p) o -> p c o", p=128)), dma=True)
    m_bq = R("gpsimd", lambda e: e.dma_start(
        bqrow[:], I["L1_attn_bq"].rearrange("(one o) -> one o", one=1)), dma=True)
    m_bk = R("gpsimd", lambda e: e.dma_start(
        bkrow[:], I["L1_attn_bk"].rearrange("(one o) -> one o", one=1)), dma=True)
    m_kp = R("vector", lambda e: e.memset(kprev[:], 0.0))

    rel_x = [[], []]
    rel_tm = []
    rel_tb = []
    rel_qfm = []
    rel_kfm = []
    rel_ksm = []
    for g in range(NG):
        buf = g % 2
        gsl = slice(512 * g, 512 * (g + 1))
        m_xq = R("gpsimd", lambda e, gsl=gsl, buf=buf: e.dma_start(
            xq[buf][:], hbm["xq1"][:, :, gsl].rearrange("c p t -> p c t")),
            waits=rel_x[buf], dma=True)
        m_xk = R("gpsimd", lambda e, gsl=gsl, buf=buf: e.dma_start(
            xk[buf][:], hbm["xk1"][:, :, gsl].rearrange("c p t -> p c t")), dma=True)

        for which in range(2):
            xsrc = xq if which == 0 else xk
            wsrc = wq if which == 0 else wk
            brow = bqrow if which == 0 else bkrow
            m_ld = m_xq if which == 0 else m_xk
            dstf = qfm if which == 0 else kfm
            relf = rel_qfm if which == 0 else rel_kfm
            ev = {}
            for tt in range(4):
                for qs in range(4):
                    ti2 = tt * 4 + qs
                    bank = ti2 % 2
                    wl = []
                    if ti2 == 0:
                        wl = [m_ld, m_wq, m_wk, m_bq, m_bk] + rel_tm
                    if ti2 >= 2:
                        wl.append(ev[ti2 - 2])
                    mlast = None
                    for dc in range(8):
                        mlast = R("tensor", lambda e, tt=tt, qs=qs, dc=dc, bank=bank,
                                  xsrc=xsrc, wsrc=wsrc, buf=buf: e.matmul(
                                      pb[bank][:], xsrc[buf][:, dc, 128 * tt:128 * (tt + 1)],
                                      wsrc[:, dc, 512 * qs:512 * (qs + 1)],
                                      start=(dc == 0), stop=False), waits=(wl if dc == 0 else []))
                    mlast = R("tensor", lambda e, qs=qs, bank=bank, brow=brow: e.matmul(
                        pb[bank][:], st["ones_bf"][:], brow[0:1, 512 * qs:512 * (qs + 1)],
                        start=False, stop=True), waits=[])
                    ev[ti2] = R("scalar", lambda e, tt=tt, qs=qs, bank=bank: e.copy(
                        tm[:, tt, 512 * qs:512 * (qs + 1)], pb[bank][:]), waits=[mlast])
            ev_sm = []
            for tt in range(4):
                base = [ev[tt * 4 + qs] for qs in range(4)]
                m1 = R("vector", lambda e, tt=tt: e.reduce_max(
                    sfx[:, 0:16], tm[:, tt, :].rearrange("p (h d) -> p h d", h=16), axis=AX),
                    waits=base + (ev_sm[-1:] if ev_sm else []) + ([m_kp] if g == 0 else []))
                m2 = R("vector", lambda e, tt=tt: e.tensor_tensor(
                    tm[:, tt, :].rearrange("p (h d) -> p h d", h=16),
                    tm[:, tt, :].rearrange("p (h d) -> p h d", h=16),
                    sfx[:, 0:16].unsqueeze(2).broadcast_to((128, 16, 128)), ALU.subtract),
                    waits=[m1])
                m3 = R("scalar", lambda e, tt=tt: e.activation(
                    tm[:, tt, :], tm[:, tt, :], AF.Exp), waits=[m2])
                m4 = R("vector", lambda e, tt=tt: e.reduce_sum(
                    sfx[:, 16:32], tm[:, tt, :].rearrange("p (h d) -> p h d", h=16), axis=AX),
                    waits=[m3])
                m5 = R("vector", lambda e: e.reciprocal(sfx[:, 32:48], sfx[:, 16:32]), waits=[m4])
                if which == 0:
                    m5 = R("vector", lambda e: e.tensor_mul(sfx[:, 32:48], sfx[:, 32:48],
                                                            st["srecip_bc"][:]), waits=[m5])
                m7 = R("vector", lambda e, tt=tt: e.tensor_tensor(
                    tb[:, tt, :].rearrange("p (h d) -> p h d", h=16),
                    tm[:, tt, :].rearrange("p (h d) -> p h d", h=16),
                    sfx[:, 32:48].unsqueeze(2).broadcast_to((128, 16, 128)), ALU.mult),
                    waits=[m5] + (rel_tb if tt == 0 else []))
                ev_sm.append(m7)
            rel_tm = [ev_sm[-1]]
            evt = {}
            tpos = 0
            for tt in range(4):
                for c in range(16):
                    bank = tpos % 2
                    wl = [ev_sm[tt]]
                    if tpos == 0:
                        wl += relf
                    if tpos >= 2:
                        wl.append(evt[tpos - 2])
                    m_tr = R("tensor", lambda e, tt=tt, c=c, bank=bank: e.transpose(
                        pbh[bank].ap()[:, 0:128], tb[:, tt, 128 * c:128 * (c + 1)],
                        st["ident"][:]), waits=wl)
                    evt[tpos] = R("scalar", lambda e, tt=tt, c=c, bank=bank, dstf=dstf: e.copy(
                        dstf[:, c, 128 * tt:128 * (tt + 1)], pbh[bank].ap()[:, 0:128]),
                        waits=[m_tr])
                    tpos += 1
            rel_tb = [evt[tpos - 1]]
            if which == 0:
                m_stq = R("sync", lambda e, gsl=gsl: e.dma_start(
                    hbm["q1h"][:, :, gsl].rearrange("h p t -> p h t"), qfm[:]),
                    waits=[evt[62], evt[63]], dma=True)
                rel_qfm = [m_stq]
            else:
                m_fmk = [evt[62], evt[63]]

        ev_s = []
        for h in range(16):
            wl = m_fmk + (rel_ksm if h == 0 else []) + (ev_s[-1:] if ev_s else [])
            m_a = R("vector", lambda e, h=h: e.tensor_scalar(
                ksm[:, h, :], kfm[:, h, :], st["smr"][:, h:h + 1], None, ALU.mult), waits=wl)
            m_b = R("vector", lambda e, h=h: e.scalar_tensor_tensor(
                ksm[:, h, 1:512], kfm[:, h, 0:511], st["smr"][:, 16 + h:16 + h + 1],
                ksm[:, h, 1:512], ALU.mult, ALU.add), waits=[m_a])
            m_c = R("vector", lambda e, h=h: e.scalar_tensor_tensor(
                ksm[:, h, 0:1], kprev[:, h:h + 1], st["smr"][:, 16 + h:16 + h + 1],
                ksm[:, h, 0:1], ALU.mult, ALU.add), waits=[m_b])
            ev_s.append(m_c)
        m_kpv = R("vector", lambda e: e.tensor_copy(kprev[:], kfm[:, :, 511]), waits=[ev_s[-1]])
        m_stk = R("sync", lambda e, gsl=gsl: e.dma_start(
            hbm["k1h"][:, :, gsl].rearrange("h p t -> p h t"), ksm[:]),
            waits=[m_kpv], dma=True)
        rel_kfm = [m_kpv]
        rel_ksm = [m_stk]
        rel_x[buf] = [rel_tm[0]]


def phase_l1_attn(b, st):
    I, hbm, pb = st["I"], st["hbm"], st["pb"]
    R = b.R

    off = 0
    woutn = man(st, "p7_wout", [128, 16, 1024], BF16, off); off += 32 * 1024
    q_st = man(st, "p7_qst", [128, 16, 512], BF16, off)
    y_tile = man(st, "p7_yt", [128, 1024], F32, off)
    s1_tile = man(st, "p7_s1t", [128, 1024], F32, off + 4096)
    off += 16 * 1024
    k_st = [man(st, "p7_kst0", [128, 16, 512], BF16, off),
            man(st, "p7_kst1", [128, 16, 512], BF16, off + 16384)]
    off += 32 * 1024
    v_st = [man(st, "p7_vst0", [128, 4, 2048], BF16, off),
            man(st, "p7_vst1", [128, 4, 2048], BF16, off + 16384)]
    off += 32 * 1024
    p_st = man(st, "p7_pst", [128, 16, 512], BF16, off)
    o2_sb = man(st, "p7_o2", [128, 16, 512], BF16, off)
    off += 16 * 1024
    p_t = [man(st, f"p7_pt{i}", [128, 512], BF16, off + 1024 * i) for i in range(3)]
    den_sb = man(st, "p7_den", [16, 512], F32, off)
    o_t = man(st, "p7_ot", [128, 512], F32, off + 2048)
    off += 3 * 1024
    p_t2 = [man(st, f"p7_pt2{i}", [128, 512], BF16, off + 1024 * i) for i in range(3)]
    r_bc = man(st, "p7_rbc", [128, 512], F32, off + 1024)
    off += 3 * 1024
    e_sb = [man(st, f"p7_esb{i}", [128, 512], BF16, off + 1024 * i) for i in range(2)]
    off += 2 * 1024
    dext = man(st, "p7_dext", [128, 4480], BF16, off); off += 8960
    o_acc = man(st, "p7_oacc", [128, 16, 512], F32, off); off += 32 * 1024
    eye16 = man(st, "p7_eye", [128, 16, 16], BF16, off); off += 512
    gob = man(st, "p7_gob", [128, 1024], F32, off); off += 4 * 1024
    bob = man(st, "p7_bob", [128, 1024], F32, off); off += 4 * 1024
    stt7 = man(st, "p7_stt", [128, 24], F32, off); off += 96

    denh = st["hbm"]["denh"]

    m_eye0 = R("vector", lambda e: e.memset(eye16[:], 0.0))
    m_eye = m_eye0
    for h in range(16):
        m_eye = R("vector", lambda e, h=h: e.memset(eye16[:, h, h:h + 1], 1.0), waits=[m_eye])
    oscr = o_acc[:].rearrange("p h t -> p (h t)")[:, 0:4480]
    m_dxa = R("gpsimd", lambda e: e.iota(oscr, [[1, 4480]], base=-384, channel_multiplier=-1,
                                         allow_small_or_imprecise_dtypes=True))
    m_dxb = R("vector", lambda e: e.tensor_scalar_max(oscr[:, 0:896], oscr[:, 0:896], 0.0),
              waits=[m_dxa])
    m_dxc = R("vector", lambda e: e.tensor_copy(dext[:], oscr), waits=[m_dxb])
    m_wo = R("gpsimd", lambda e: e.dma_start(
        woutn[:], I["L1_attn_W_out"].rearrange("(c p) d -> p c d", p=128)), dma=True)
    m_g1 = R("sync", lambda e: e.dma_start(st["rowB"][0:1, :],
                                           I["L1_attn_out_ln_w"].rearrange("(one d) -> one d", one=1)),
             dma=True)
    prev = None
    for half in range(2):
        mm = R("tensor", lambda e, half=half: e.matmul(
            pb[half][:], st["ones_fr"][:], st["rowB"][0:1, 512 * half:512 * (half + 1)],
            start=True, stop=True), waits=[m_g1])
        prev = R("scalar", lambda e, half=half: e.copy(gob[:, 512 * half:512 * (half + 1)],
                                                       pb[half][:]), waits=[mm])
    m_g2 = R("sync", lambda e: e.dma_start(st["rowB"][0:1, :],
                                           I["L1_attn_out_ln_b"].rearrange("(one d) -> one d", one=1)),
             waits=[prev], dma=True)
    for half in range(2):
        mm = R("tensor", lambda e, half=half: e.matmul(
            pb[half][:], st["ones_fr"][:], st["rowB"][0:1, 512 * half:512 * (half + 1)],
            start=True, stop=True), waits=[m_g2, prev])
        prev = R("scalar", lambda e, half=half: e.copy(bob[:, 512 * half:512 * (half + 1)],
                                                       pb[half][:]), waits=[mm])
    m_bc_done = prev

    def skip(h, offij):
        return h < 8 and SLOPES[h] * (offij - 127) > 104.0

    rel_q = []
    rel_p = []
    rel_k = [[], []]
    rel_v = [[], []]
    rel_pt = []
    rel_stb = []
    rel_esb = []
    rel_den = []
    rel_s1 = []
    m_rbe = None
    add_by_bank = {}
    for g in range(NG):
        gsl = slice(512 * g, 512 * (g + 1))
        m_q = R("gpsimd", lambda e, gsl=gsl: e.dma_start(
            q_st[:], hbm["q1h"][:, :, gsl].rearrange("h p t -> p h t")),
            waits=rel_q + ([m_bc_done] if g == 0 else []), dma=True)
        m_p = R("gpsimd", lambda e, gsl=gsl: e.dma_start(
            p_st[:], hbm["p1h"][:, :, gsl].rearrange("h p t -> p h t")),
            waits=rel_p, dma=True)

        groups = []
        ntiles = 0
        for w in range(g + 1):
            for h in range(16):
                kcs = [kc for kc in range(4) if not skip(h, 512 * g - 128 * (4 * w + kc))]
                if kcs:
                    groups.append((w, h, kcs))
                    ntiles += len(kcs)
        pp = 0
        pvcnt = 0
        pv_marks = {}
        pmk = {}
        st_marks = {}
        cur_w = -1
        m_k2 = m_v2 = None
        lkb = {0: None, 1: None}
        lvb = {0: None, 1: None}
        seen_h = set()
        m_add = {}
        m_dn_last = None
        pend = None

        def rec_pv(t):
            nonlocal m_dn_last
            wl_dn = list(t["wl_pv"])
            if g == 0 and t["pp"] == 0:
                wl_dn.append(m_eye)
            if t["pp"] == 0:
                wl_dn += rel_den
            m_dn_last = R("tensor", lambda e, h=t["h"], fst_d=t["fst_d"], fsp_d=t["fsp_d"],
                          pvt=t["pvt"]: e.matmul(pb[7][0:16, :], eye16[:, h, :], pvt[:],
                                                 start=fst_d, stop=fsp_d), waits=wl_dn)
            wl2 = list(t["wl_pv"])
            if t["fst"] and t["pvb"] in add_by_bank:
                wl2.append(add_by_bank.pop(t["pvb"]))
            mpv = R("tensor", lambda e, h=t["h"], kc=t["kc"], pvb=t["pvb"], buf=t["buf"],
                    fst=t["fst"], fsp=t["fsp"], pvt=t["pvt"]: e.matmul(
                        pb[pvb][:], v_st[buf][:, kc, 128 * h:128 * (h + 1)], pvt[:],
                        start=fst, stop=fsp), waits=wl2)
            pv_marks[t["pp"]] = mpv
            lvb[t["buf"]] = mpv
            if t["fsp"]:
                h = t["h"]
                pvb = t["pvb"]
                if h not in seen_h:
                    seen_h.add(h)
                    m_add[h] = R("vector", lambda e, h=h, pvb=pvb: e.tensor_copy(
                        o_acc[:, h, :], pb[pvb][:]), waits=[mpv])
                else:
                    m_add[h] = R("vector", lambda e, h=h, pvb=pvb: e.tensor_add(
                        o_acc[:, h, :], o_acc[:, h, :], pb[pvb][:]), waits=[mpv, m_add[h]])
                add_by_bank[pvb] = m_add[h]

        for (w, h, kcs) in groups:
            if w != cur_w:
                cur_w = w
                buf = w % 2
                wk = [lkb[buf]] if lkb[buf] else rel_k[buf]
                wv = [lvb[buf]] if lvb[buf] else rel_v[buf]
                m_k2 = R("sync", lambda e, w=w, buf=buf: e.dma_start(
                    k_st[buf][:], hbm["k1h"][:, :, 512 * w:512 * (w + 1)].rearrange(
                        "h p t -> p h t")), waits=wk, dma=True)
                m_v2 = R("sync", lambda e, w=w, buf=buf: e.dma_start(
                    v_st[buf][:], hbm["v1h"][512 * w:512 * (w + 1), :].rearrange(
                        "(a p) x -> p a x", p=128)), waits=wv, dma=True)
            pvb = pvcnt % 4
            pvcnt += 1
            for ki, kc in enumerate(kcs):
                offij = 512 * g - 128 * (4 * w + kc)
                stb = 5 + pp % 2
                rot = pp % 3
                eb = pp % 2
                diag = (w == g)
                wl = [m_k2, m_q]
                if pp >= 2:
                    wl.append(pmk[pp - 2])
                else:
                    wl += rel_stb
                mm = R("tensor", lambda e, h=h, kc=kc, stb=stb, buf=buf: e.matmul(
                    pb[stb][:], k_st[buf][:, h, 128 * kc:128 * (kc + 1)], q_st[:, h, :],
                    start=True, stop=True), waits=wl)
                st_marks[pp] = mm
                lkb[buf] = mm
                wl_p = [mm]
                if pp >= 3:
                    wl_p.append(pv_marks[pp - 3])
                elif rel_pt:
                    wl_p.append(rel_pt[min(pp, len(rel_pt) - 1)])
                if h < 8:
                    wl_e = []
                    if pp >= 2:
                        wl_e.append(pmk[pp - 2])
                    elif rel_esb:
                        wl_e.append(rel_esb[min(pp, len(rel_esb) - 1)])
                    if g == 0 and pp < 2:
                        wl_e.append(m_dxc)
                    m_e = R("scalar", lambda e, eb=eb, offij=offij, h=h: e.activation(
                        e_sb[eb][:], dext[:, 384 + offij:384 + offij + 512], AF.Exp,
                        scale=-SLOPES[h]), waits=wl_e)
                    m_pt = R("vector", lambda e, rot=rot, eb=eb, stb=stb: e.tensor_mul(
                        p_t[rot][:], pb[stb][:], e_sb[eb][:]), waits=wl_p + [m_e])
                    pmk[pp] = m_pt
                else:
                    m_pt = R("scalar", lambda e, rot=rot, stb=stb: e.copy(
                        p_t[rot][:], pb[stb][:]), waits=wl_p)
                    pmk[pp] = m_pt
                wl_pv = [m_pt, m_v2]
                pvt = p_t[rot]
                if diag:
                    m_af = R("gpsimd", lambda e, rot=rot, kc=kc: e.affine_select(
                        p_t2[rot][:], p_t[rot][:], [[1, 512]], ALU.is_ge, 0.0,
                        base=-128 * kc, channel_multiplier=-1), waits=[m_pt])
                    wl_pv = [m_af, m_v2]
                    pmk[pp] = m_af
                    pvt = p_t2[rot]
                t = dict(pp=pp, h=h, kc=kc, pvb=pvb, buf=buf, pvt=pvt, wl_pv=wl_pv,
                         fst=(ki == 0), fsp=(ki == len(kcs) - 1),
                         fst_d=(pp == 0), fsp_d=(pp == ntiles - 1))
                if pend is not None:
                    rec_pv(pend)
                pend = t
                pp += 1
        rec_pv(pend)
        pend = None
        for bb in (0, 1):
            if lkb[bb]:
                rel_k[bb] = [lkb[bb]]
            if lvb[bb]:
                rel_v[bb] = [lvb[bb]]
        rel_stb = [pmk[pp - 1], pmk[pp - 2]]
        rel_esb = [pmk[k] for k in sorted(pmk)[-2:]]
        last3pv = [pv_marks[k] for k in sorted(pv_marks)[-3:]]

        m_de = R("scalar", lambda e: e.copy(den_sb[:], pb[7][0:16, :]),
                 waits=[m_dn_last] + last3pv)
        rel_den = [m_de]
        m_dh = R("sync", lambda e: e.dma_start(denh[:], den_sb[:]), waits=[m_de], dma=True)
        m_on_prev = None
        m_o2 = {}
        m_dr = None
        for h in range(16):
            m_dr = R("sync", lambda e, h=h: e.dma_start(st["rowA"][0:1, 0:512],
                                                        denh[h:h + 1, :]),
                     waits=[m_dh] + ([m_rbe] if m_rbe else []), dma=True)
            m_ep = R("vector", lambda e: e.tensor_scalar(st["rowA"][0:1, 512:1024],
                                                         st["rowA"][0:1, 0:512],
                                                         EPS, None, ALU.add),
                     waits=[m_dr] + ([m_on_prev] if m_on_prev else []))
            m_rc = R("vector", lambda e: e.reciprocal(st["rowB"][0:1, 0:512],
                                                      st["rowA"][0:1, 512:1024]), waits=[m_ep])
            m_rb = R("tensor", lambda e: e.matmul(pb[4][:], st["ones_fr"][:],
                                                  st["rowB"][0:1, 0:512], start=True, stop=True),
                     waits=[m_rc] + ([m_rbe] if m_rbe else []))
            m_rbe = R("scalar", lambda e: e.copy(r_bc[:], pb[4][:]),
                      waits=[m_rb] + (last3pv if h == 0 else []))
            m_sp = R("scalar", lambda e, h=h: e.activation(o2_sb[:, h, :], p_st[:, h, :],
                                                           AF.Silu), waits=[m_rbe, m_p])
            m_on = R("vector", lambda e, h=h: e.tensor_mul(
                o_t[:], o_acc[:, h, :], r_bc[:]),
                waits=[m_sp, m_add[h]] + (last3pv if h == 0 else []))
            m_o2[h] = R("vector", lambda e, h=h: e.tensor_mul(
                o2_sb[:, h, :], o2_sb[:, h, :], o_t[:]), waits=[m_on])
            m_on_prev = m_o2[h]

        ev_prev = None
        mlast = None
        for tt in range(4):
            ev_y = {}
            for ds in range(2):
                bank = ds % 2
                wl = []
                if tt == 0 and ds == 0:
                    wl = [m_o2[h] for h in range(16)] + [m_wo] + [st_marks[pp - 1]]
                if ev_prev is not None and ds == 0:
                    wl.append(ev_prev)
                for h in range(16):
                    mlast = R("tensor", lambda e, tt=tt, ds=ds, h=h, bank=bank: e.matmul(
                        pb[bank][:], o2_sb[:, h, 128 * tt:128 * (tt + 1)],
                        woutn[:, h, 512 * ds:512 * (ds + 1)],
                        start=(h == 0), stop=(h == 15)), waits=(wl if h == 0 else []))
                ev_y[ds] = R("scalar", lambda e, ds=ds, bank=bank: e.copy(
                    y_tile[:, 512 * ds:512 * (ds + 1)], pb[bank][:]),
                    waits=[mlast] + ([ev_prev] if (ev_prev and ds == 0) else []))
            m1 = R("vector", lambda e: e.bn_stats(stt7[:, 0:6], y_tile[:, 0:512]),
                   waits=[ev_y[0], ev_y[1]])
            m2 = R("vector", lambda e: e.bn_stats(stt7[:, 6:12], y_tile[:, 512:1024]), waits=[m1])
            m3 = R("vector", lambda e: e.bn_aggr(stt7[:, 12:14], stt7[:, 0:12]), waits=[m2])
            mr1 = R("scalar", lambda e: e.activation(stt7[:, 14:15], stt7[:, 13:14], AF.Ln,
                                                     bias=st["epsb"][:]), waits=[m3])
            mr2 = R("scalar", lambda e: e.activation(stt7[:, 15:16], stt7[:, 14:15], AF.Exp,
                                                     scale=-0.5), waits=[mr1])
            ma = R("vector", lambda e: e.tensor_scalar(s1_tile[:], y_tile[:], stt7[:, 12:13],
                                                       stt7[:, 15:16], ALU.subtract, ALU.mult),
                   waits=[mr2] + rel_s1)
            mb = R("vector", lambda e: e.tensor_mul(y_tile[:], s1_tile[:], gob[:]), waits=[ma])
            mc = R("vector", lambda e: e.tensor_add(s1_tile[:], y_tile[:], bob[:]), waits=[mb])
            m_s1 = R("sync", lambda e, g=g, tt=tt: e.dma_start(
                hbm["s1h"][512 * g + 128 * tt:512 * g + 128 * (tt + 1), :], s1_tile[:]),
                waits=[mc], dma=True)
            rel_s1 = [m_s1]
            ev_prev = mc
        rel_q = [st_marks[pp - 1], ev_prev] + rel_s1
        rel_p = [mlast, ev_prev] + rel_s1
        rel_pt = [m_dh, m_dr, m_o2[15]]


def phase_l1_ffn(b, st):
    I, hbm, pb = st["I"], st["hbm"], st["pb"]
    R = b.R

    off = 0
    wg = [man(st, "p8_wg0", [128, 8, 1024], BF16, off),
          man(st, "p8_wg1", [128, 8, 1024], BF16, off + 16384)]
    yg = man(st, "p8_yg", [128, 4, 1024], F32, off)
    off += 32 * 1024
    wu = [man(st, "p8_wu0", [128, 8, 1024], BF16, off),
          man(st, "p8_wu1", [128, 8, 1024], BF16, off + 16384)]
    sg = man(st, "p8_sg", [128, 4, 1024], F32, off)
    off += 32 * 1024
    wd = [man(st, "p8_wd0", [128, 8, 1024], BF16, off),
          man(st, "p8_wd1", [128, 8, 1024], BF16, off + 16384)]
    x2b = man(st, "p8_x2b", [128, 4, 1024], F32, off)
    off += 32 * 1024
    xg = [man(st, "p8_xg0", [128, 8, 512], BF16, off),
          man(st, "p8_xg1", [128, 8, 512], BF16, off + 8192)]
    xb = man(st, "p8_xb", [128, 4, 1024], F32, off)
    off += 16 * 1024
    h_fm = man(st, "p8_hfm", [128, 8, 512], BF16, off); off += 8 * 1024
    ysb = man(st, "p8_ysb", [128, 4, 1024], F32, off); off += 16 * 1024
    w_bc = man(st, "p8_wbc", [128, 512], F32, off); off += 2 * 1024
    tmp = man(st, "p8_tmp", [128, 512], F32, off); off += 2 * 1024
    rwrow = man(st, "p8_rw", [1, 512], BF16, off); off += 1024

    rel_wg = [[], []]
    rel_wu = [[], []]
    rel_wd = [[], []]
    rel_xg = [[], []]
    rel_ysb = []
    rel_yacc = {}
    rel_wbc = []
    rel_tmp = []
    rel_hfm = []
    bank0r = []
    m_ld = {}
    for e_i in range(E1):
        wbuf = e_i % 2
        if e_i == 0:
            m_ld[0] = [
                R("gpsimd", lambda e: e.dma_start(
                    wg[0][:], I["L1_ffn_Wg"][0].rearrange("(c p) i -> p c i", p=128)), dma=True),
                R("gpsimd", lambda e: e.dma_start(
                    wu[0][:], I["L1_ffn_Wu"][0].rearrange("(c p) i -> p c i", p=128)), dma=True),
                R("gpsimd", lambda e: e.dma_start(
                    wd[0][:], I["L1_ffn_Wd"][0].rearrange("(c p) d -> p c d", p=128)), dma=True)]
        if e_i + 1 < E1:
            nbuf = (e_i + 1) % 2
            m_ld[e_i + 1] = [
                R("gpsimd", lambda e, e_i=e_i: e.dma_start(
                    wg[(e_i + 1) % 2][:],
                    I["L1_ffn_Wg"][e_i + 1].rearrange("(c p) i -> p c i", p=128)),
                    waits=rel_wg[nbuf], dma=True),
                R("gpsimd", lambda e, e_i=e_i: e.dma_start(
                    wu[(e_i + 1) % 2][:],
                    I["L1_ffn_Wu"][e_i + 1].rearrange("(c p) i -> p c i", p=128)),
                    waits=rel_wu[nbuf], dma=True),
                R("gpsimd", lambda e, e_i=e_i: e.dma_start(
                    wd[(e_i + 1) % 2][:],
                    I["L1_ffn_Wd"][e_i + 1].rearrange("(c p) d -> p c d", p=128)),
                    waits=rel_wd[nbuf], dma=True)]
        m_w = m_ld[e_i]
        last_g = last_u = last_d = None
        for g in range(NG):
            gsl = slice(512 * g, 512 * (g + 1))
            xbuf = g % 2
            m_x = R("sync", lambda e, gsl=gsl, xbuf=xbuf: e.dma_start(
                xg[xbuf][:], hbm["xn1"][:, :, gsl].rearrange("c p t -> p c t")),
                waits=rel_xg[xbuf], dma=True)
            m_rw = R("sync", lambda e, e_i=e_i, gsl=gsl: e.dma_start(
                rwrow[:], hbm["rw1"][e_i:e_i + 1, gsl]), waits=rel_wbc, dma=True)
            m_wm = R("tensor", lambda e: e.matmul(pb[0][:], st["ones_bf"][:], rwrow[0:1, :],
                                                  start=True, stop=True),
                     waits=[m_rw] + bank0r)
            m_wbc = R("scalar", lambda e: e.copy(w_bc[:], pb[0][:]),
                      waits=[m_wm] + rel_wbc)
            rel_wbc = [m_wbc]

            m_si = {}
            m_m1 = {}
            m_m2 = {}
            for ic in range(8):
                ba = ic % 2
                bu = 2 + ic % 2
                wl_g = []
                if ic == 0:
                    wl_g = [m_wbc, m_x] + m_w + rel_hfm
                if ic >= 2:
                    wl_g.append(m_si[ic - 2])
                for dc in range(8):
                    last_g = R("tensor", lambda e, ic=ic, dc=dc, ba=ba, wbuf=wbuf, xbuf=xbuf:
                               e.matmul(pb[ba][:], wg[wbuf][:, dc, 128 * ic:128 * (ic + 1)],
                                        xg[xbuf][:, dc, :], start=(dc == 0), stop=(dc == 7)),
                               waits=(wl_g if dc == 0 else []))
                wl_u = [m_m1[ic - 2]] if ic >= 2 else []
                for dc in range(8):
                    last_u = R("tensor", lambda e, ic=ic, dc=dc, bu=bu, wbuf=wbuf, xbuf=xbuf:
                               e.matmul(pb[bu][:], wu[wbuf][:, dc, 128 * ic:128 * (ic + 1)],
                                        xg[xbuf][:, dc, :], start=(dc == 0), stop=(dc == 7)),
                               waits=(wl_u if dc == 0 else []))
                m_si[ic] = R("scalar", lambda e, ba=ba: e.activation(tmp[:], pb[ba][:], AF.Silu),
                             waits=[last_g] + ([m_m2[ic - 1]] if ic >= 1 else rel_tmp))
                m_m1[ic] = R("vector", lambda e, bu=bu: e.tensor_mul(tmp[:], tmp[:], pb[bu][:]),
                             waits=[m_si[ic], last_u])
                m_m2[ic] = R("vector", lambda e, ic=ic: e.tensor_mul(h_fm[:, ic, :], tmp[:],
                                                                     w_bc[:]), waits=[m_m1[ic]])
            rel_tmp = [m_m2[7]]
            bank0r = [m_si[6]]
            rel_wbc = [m_m2[7]]
            rel_xg[xbuf] = [last_u]

            ev_y = {}
            for tt in range(4):
                for ds in range(2):
                    ti = 2 * tt + ds
                    bank = 4 + ti % 4
                    wl = []
                    if ti == 0:
                        wl = [m_m2[7], m_m2[6]] + rel_ysb
                    if ti >= 4:
                        wl.append(ev_y[ti - 4])
                    for ic in range(8):
                        last_d = R("tensor", lambda e, tt=tt, ds=ds, ic=ic, bank=bank, wbuf=wbuf:
                                   e.matmul(pb[bank][:], h_fm[:, ic, 128 * tt:128 * (tt + 1)],
                                            wd[wbuf][:, ic, 512 * ds:512 * (ds + 1)],
                                            start=(ic == 0), stop=(ic == 7)),
                                   waits=(wl if ic == 0 else []))
                    ev_y[ti] = R("scalar", lambda e, tt=tt, ds=ds, bank=bank: e.copy(
                        ysb[:, tt, 512 * ds:512 * (ds + 1)], pb[bank][:]), waits=[last_d])
            wl_acc = [ev_y[6], ev_y[7]] + ([rel_yacc[g]] if g in rel_yacc else [])
            if e_i == 0:
                m_acc = R("gpsimd", lambda e, gsl=gsl: e.dma_start(
                    hbm["y1h"][gsl, :].rearrange("(a p) d -> p a d", p=128), ysb[:]),
                    waits=wl_acc, dma=True)
            else:
                m_acc = R("gpsimd", lambda e, gsl=gsl: e.dma_start(
                    hbm["y1h"][gsl, :].rearrange("(a p) d -> p a d", p=128), ysb[:],
                    accum_op=ALU.add), waits=wl_acc, dma=True)
            rel_yacc[g] = m_acc
            rel_ysb = [m_acc]
            rel_hfm = [last_d]
        rel_wg[wbuf] = [last_g]
        rel_wu[wbuf] = [last_u]
        rel_wd[wbuf] = [last_d]

    done_all = rel_wg[0] + rel_wu[0] + rel_wd[0] + rel_xg[0] + rel_xg[1] + rel_ysb + rel_tmp
    rel_bufs = []
    for g in range(NG):
        gsl = slice(512 * g, 512 * (g + 1))
        m_y = R("sync", lambda e, gsl=gsl: e.dma_start(
            yg[:], hbm["y1h"][gsl, :].rearrange("(a p) d -> p a d", p=128)),
            waits=[rel_yacc[g]] + rel_bufs + (done_all if g == 0 else []), dma=True)
        m_s = R("sync", lambda e, gsl=gsl: e.dma_start(
            sg[:], hbm["s1h"][gsl, :].rearrange("(a p) d -> p a d", p=128)), dma=True)
        m_x = R("sync", lambda e, gsl=gsl: e.dma_start(
            xb[:], hbm["x1h"][gsl, :].rearrange("(a p) d -> p a d", p=128)), dma=True)
        m_si = R("scalar", lambda e: e.activation(x2b[:], yg[:], AF.Silu),
                 waits=[m_y] + rel_bufs)
        m_t1 = R("vector", lambda e: e.tensor_mul(x2b[:], sg[:], x2b[:]), waits=[m_si, m_s])
        m_t2 = R("vector", lambda e: e.tensor_add(x2b[:], xb[:], x2b[:]), waits=[m_t1, m_x])
        m_st = R("sync", lambda e, gsl=gsl: e.dma_start(
            st["out"][gsl, :].rearrange("(a p) d -> p a d", p=128), x2b[:]),
            waits=[m_t2], dma=True)
        rel_bufs = [m_st, m_t2]



def make_inmaps(inputs):
    x = np.ascontiguousarray(inputs["L0_attn_x"])
    shared = {}
    for k, v in inputs.items():
        if k != "L0_attn_x":
            shared[k] = np.ascontiguousarray(v)
    return [dict(shared, x0=np.ascontiguousarray(x[c])) for c in range(8)]


def kernel(**inputs):
    nc = build_program()
    maps = make_inmaps(inputs)
    res = run_bass_kernel_spmd(nc, maps, core_ids=list(range(8)))
    return np.stack([r["out"] for r in res.results], axis=0)


# version 06 (final)
# speedup vs baseline: 1.1876x; 1.1876x over previous
import numpy as np
import ml_dtypes
import concourse.bass as bass
import concourse.mybir as mybir
from concourse.bass_utils import run_bass_kernel_spmd

F32 = mybir.dt.float32
BF16 = mybir.dt.bfloat16
AF = mybir.ActivationFunctionType
ALU = mybir.AluOpType
AX = mybir.AxisListType.X

T = 4096
D = 1024
NG = 8
GS = 512
NT = 32
HID0 = 4096
E1 = 8
I1 = 1024
DEXP = 2048
EPS = 1e-5

ENGINES = ("sync", "gpsimd", "scalar", "vector", "tensor")
import os
DEV_SKIP = set(os.environ.get("DEV_SKIP", "").split(","))
DEV_NG = int(os.environ.get("DEV_NG", NG if 'NG' in dir() else 8))
SLOPES = [2.0 ** x for x in np.linspace(0.0, -8.0, 8)]
SBUF_LIMIT = 229344


class Sem:
    def __init__(self, h):
        self.h = h
        self.n = 0

    def inc(self, k=1):
        self.n += k
        return (self, k)

    @property
    def now(self):
        return (self, self.n)


class B:

    def __init__(self, nc):
        self.nc = nc
        self.streams = {e: [] for e in ENGINES}
        self.bar = Sem(nc.alloc_semaphore("bar"))
        self.P = {e: Sem(nc.alloc_semaphore(f"P_{e}")) for e in ENGINES}
        self.DW = {e: Sem(nc.alloc_semaphore(f"D_{e}")) for e in ("sync", "gpsimd", "scalar")}

    def R(self, eng, build, waits=(), dma=None):
        waits = tuple((s, v) for (s, v) in waits if v > 0)
        if build is None:
            assert not dma

            def op(e, waits=waits):
                for (s, v) in waits:
                    e.wait_ge(s.h, v)

            self.streams[eng].append(op)
            return None
        if dma:
            sem = self.DW[eng]
            sem.n += 16
            amt = 16
        else:
            sem = self.P[eng]
            sem.n += 1
            amt = 1
        mark = (sem, sem.n)

        def op(e, build=build, waits=waits, sem=sem, amt=amt):
            for (s, v) in waits:
                e.wait_ge(s.h, v)
            build(e).then_inc(sem.h, amt)

        self.streams[eng].append(op)
        return mark

    def barrier(self):
        self.bar.n += 5
        tgt = self.bar.n
        for eng in ENGINES:
            w = [(self.P[eng], self.P[eng].n)]
            if eng in self.DW:
                w.append((self.DW[eng], self.DW[eng].n))
            self.R(eng, None, waits=w)
            self.streams[eng].append(lambda e, h=self.bar.h: e.nop().then_inc(h, 1))
            self.R(eng, None, waits=[(self.bar, tgt)])

    def replay(self):
        nc = self.nc
        with nc.Block() as block:
            for eng in ENGINES:
                ops = self.streams[eng]
                getattr(block, eng)(lambda e, ops=ops: [op(e) for op in ops])



def build_program(dump=(), upto=None):
    nc = bass.Bass("TRN2", target_bir_lowering=False)

    def dram(name, shape, dt):
        kind = "ExternalOutput" if name in dump else "Internal"
        return nc.dram_tensor(name, shape, dt, kind=kind).ap()

    I = {}

    def inp(name, shape, dt=F32):
        I[name] = nc.dram_tensor(name, shape, dt, kind="ExternalInput").ap()

    inp("x0", [T, D])
    inp("L0_attn_ln_gamma", [D]); inp("L0_attn_ln_beta", [D])
    inp("L0_attn_w_qkv", [3 * D, D]); inp("L0_attn_w_out", [D, D]); inp("L0_attn_b_out", [D])
    inp("L0_ffn_gate_w", [2, D])
    inp("L0_ffn_w1", [2, D, HID0]); inp("L0_ffn_b1", [2, HID0])
    inp("L0_ffn_w2", [2, HID0, D]); inp("L0_ffn_b2", [2, D])
    inp("L0_ffn_sw1", [D, HID0]); inp("L0_ffn_sb1", [HID0])
    inp("L0_ffn_sw2", [HID0, D]); inp("L0_ffn_sb2", [D])
    for nm in ("in_ln", "q_ln", "k_ln", "out_ln"):
        inp(f"L1_attn_{nm}_w", [D]); inp(f"L1_attn_{nm}_b", [D])
    inp("L1_attn_W_in", [D, 2 * DEXP]); inp("L1_attn_W_out", [DEXP, D])
    inp("L1_attn_Wq", [D, DEXP]); inp("L1_attn_bq", [DEXP])
    inp("L1_attn_Wk", [D, DEXP]); inp("L1_attn_bk", [DEXP])
    inp("L1_attn_smear_factor", [16]); inp("L1_attn_log_scale", [16])
    inp("L1_ffn_Wg", [E1, D, I1]); inp("L1_ffn_Wu", [E1, D, I1]); inp("L1_ffn_Wd", [E1, I1, D])
    inp("L1_ffn_Wr", [D, E1]); inp("L1_ffn_temperature", [1])

    out = nc.dram_tensor("out", [T, D], F32, kind="ExternalOutput").ap()

    hbm = dict(
        dbg0=dram("dbg0", [128, 4, 1024], BF16),
        dbg1=dram("dbg1", [128, 4, 1024], F32),
        dbg2=dram("dbg2", [128, 1024], F32),
        dbg3=dram("dbg3", [128, 24], F32),
        dbg4=dram("dbg4", [128, 1024], F32),
        dbg5=dram("dbg5", [128, 32], F32),
        dbg6=dram("dbg6", [1, 2048], F32),
        dbg7=dram("dbg7", [1, 128], F32),
        dbg8=dram("dbg8", [128, 1024], F32),
        dbg9=dram("dbg9", [128, 1024], F32),
        xn0=dram("xn0", [8, 128, T], BF16),
        q0h=dram("q0h", [8, 128, T], BF16),
        k0h=dram("k0h", [8, 128, T], BF16),
        v0h=dram("v0h", [T, 16 * 66], BF16),
        mask01=dram("mask01", [2, T], BF16),
        s0h=dram("s0h", [T, D], F32),
        y0h=dram("y0h", [T, D], F32),
        x1h=dram("x1h", [T, D], F32),
        xn1=dram("xn1", [8, 128, T], BF16),
        xin1=dram("xin1", [8, 128, T], BF16),
        xq1=dram("xq1", [8, 128, T], BF16),
        xk1=dram("xk1", [8, 128, T], BF16),
        wrr=dram("wrr", [8, 1024], F32),
        q1h=dram("q1h", [16, 128, T], BF16),
        k1h=dram("k1h", [16, 128, T], BF16),
        v1h=dram("v1h", [T, DEXP], BF16),
        p1h=dram("p1h", [16, 128, T], BF16),
        rw1=dram("rw1", [E1, T], BF16),
        denh=dram("denh", [16, 512], F32),
        dbgA=dram("dbgA", [128, 16, 512], BF16),
        s1h=dram("s1h", [T, D], F32),
        y1h=dram("y1h", [T, D], F32),
    )

    b = B(nc)
    st = {"nc": nc, "I": I, "out": out, "hbm": hbm}

    A = nc.alloc_sbuf_tensor
    st["ident"] = A("ident", [128, 128], BF16)
    st["ident_f"] = A("ident_f", [128, 128], F32)
    st["ones_bf"] = A("ones_bf", [1, 128], BF16)
    st["ones_fr"] = A("ones_fr", [1, 128], F32)
    st["iotaD"] = A("iotaD", [128, 512], F32)
    st["b1e"] = A("b1e", [128, 3 * 32], F32)
    st["g0a"] = A("g0a", [128, 8], F32)
    st["g0b"] = A("g0b", [128, 8], F32)
    st["l1w"] = A("l1w", [128, 4 * 8], F32)
    st["l1b"] = A("l1b", [128, 4 * 8], F32)
    st["l1b_bf"] = A("l1b_bf", [128, 4 * 8], BF16)
    st["bq1"] = A("bq1", [128, 16], F32)
    st["bk1"] = A("bk1", [128, 16], F32)
    st["smr"] = A("smr", [128, 2 * 16], F32)
    st["srecip_bc"] = A("srecip_bc", [128, 16], F32)
    st["rtemp_bc"] = A("rtemp_bc", [128, 1], F32)
    st["qkvbias"] = A("qkvbias", [128, 24], F32)
    st["scr2"] = A("scr2", [128, 64], F32)
    st["rowA"] = A("rowA", [1, 1024], F32)
    st["rowB"] = A("rowB", [1, 1024], F32)
    st["rowC"] = A("rowC", [1, 256], F32)
    st["bout_row"] = A("bout_row", [1, 1024], BF16)
    st["b2s_row"] = A("b2s_row", [1, 1024], BF16)
    st["epsb"] = A("epsb", [128, 1], F32)

    st["pb"] = [nc.alloc_psum_tensor(f"pb{i}", [128, 512], F32) for i in range(8)]
    st["manual_base"] = (nc.sbuf_base + 31) // 32 * 32

    phases = [
        phase_setup,
        phase_l0_lnqkv,
        phase_l0_attn,
        phase_l0_ffn,
        phase_l1_ln,
        phase_l1_winp,
        phase_l1_qk,
        phase_l1_attn,
        phase_l1_ffn,
    ]
    if upto is not None:
        phases = phases[:upto]
    for i, ph in enumerate(phases):
        if i:
            b.barrier()
        ph(b, st)
    b.barrier()
    b.replay()
    return nc


def man(st, name, shape, dt, off):
    assert off % 32 == 0
    h = st["nc"].alloc_sbuf_tensor_at(name, shape, dt, offset=st["manual_base"] + off)
    nb = int(np.prod(shape[1:])) * (2 if dt == BF16 else 4)
    end = st["manual_base"] + off + nb
    assert end <= SBUF_LIMIT, f"sbuf overflow: {name} ends at {end}"
    return h



def phase_setup(b, st):
    I = st["I"]
    R = b.R

    m_iota = R("gpsimd", lambda e: e.iota(st["iotaD"][:], [[1, 512]], channel_multiplier=-1,
                                          allow_small_or_imprecise_dtypes=True))
    m_iotai = R("gpsimd", lambda e: e.iota(st["ident"][:], [[1, 128]], channel_multiplier=-1,
                                           allow_small_or_imprecise_dtypes=True))
    m_id = R("vector", lambda e: e.tensor_single_scalar(st["ident"][:], st["ident"][:], 0.0,
                                                        ALU.is_equal), waits=[m_iotai])
    m_idf = R("vector", lambda e: e.tensor_copy(st["ident_f"][:], st["ident"][:]), waits=[m_id])
    R("vector", lambda e: e.memset(st["ones_bf"][:], 1.0))
    R("vector", lambda e: e.memset(st["ones_fr"][:], 1.0))
    R("vector", lambda e: e.memset(st["epsb"][:], EPS))

    pm = lambda ap: ap.rearrange("(c p) -> p c", p=128)
    loads = []

    def gld(dst, src, **kw):
        loads.append(R("gpsimd", lambda e, d=dst, s=src, kw=kw: e.dma_start(d, s, **kw), dma=True))

    gld(st["b1e"][:, 0:32], pm(I["L0_ffn_b1"][0]), allow_slow_non_contiguous=True)
    gld(st["b1e"][:, 32:64], pm(I["L0_ffn_b1"][1]), allow_slow_non_contiguous=True)
    gld(st["b1e"][:, 64:96], pm(I["L0_ffn_sb1"]), allow_slow_non_contiguous=True)
    gld(st["g0a"][:], pm(I["L0_attn_ln_gamma"]), allow_slow_non_contiguous=True)
    gld(st["g0b"][:], pm(I["L0_attn_ln_beta"]), allow_slow_non_contiguous=True)
    for i, nm in enumerate(("in_ln", "q_ln", "k_ln", "out_ln")):
        gld(st["l1w"][:, 8 * i:8 * i + 8], pm(I[f"L1_attn_{nm}_w"]), allow_slow_non_contiguous=True)
        gld(st["l1b"][:, 8 * i:8 * i + 8], pm(I[f"L1_attn_{nm}_b"]), allow_slow_non_contiguous=True)
    gld(st["bq1"][:], pm(I["L1_attn_bq"]), allow_slow_non_contiguous=True)
    gld(st["bk1"][:], pm(I["L1_attn_bk"]), allow_slow_non_contiguous=True)
    gld(st["rowC"][0:1, 0:16], I["L1_attn_smear_factor"].rearrange("(one h) -> one h", one=1))
    gld(st["rowC"][0:1, 16:32], I["L1_attn_log_scale"].rearrange("(one h) -> one h", one=1))
    gld(st["rowC"][0:1, 32:33], I["L1_ffn_temperature"].rearrange("(one h) -> one h", one=1))
    gld(st["bout_row"][:], I["L0_attn_b_out"].rearrange("(one d) -> one d", one=1))
    gld(st["rowA"][0:1, :], I["L0_ffn_b2"][0].rearrange("(one d) -> one d", one=1))
    gld(st["rowB"][0:1, :], I["L0_ffn_b2"][1].rearrange("(one d) -> one d", one=1))
    mld = loads[-1]
    m_b2a = R("vector", lambda e: e.tensor_add(st["rowA"][0:1, :], st["rowA"][0:1, :],
                                               st["rowB"][0:1, :]), waits=[mld])
    m_sb2 = R("gpsimd", lambda e: e.dma_start(
        st["rowB"][0:1, :], I["L0_ffn_sb2"].rearrange("(one d) -> one d", one=1)),
        waits=[m_b2a], dma=True)
    m_b2b = R("vector", lambda e: e.tensor_add(st["rowA"][0:1, :], st["rowA"][0:1, :],
                                               st["rowB"][0:1, :]), waits=[m_sb2])
    R("vector", lambda e: e.tensor_copy(st["b2s_row"][:], st["rowA"][0:1, :]), waits=[m_b2b])

    R("vector", lambda e: e.tensor_copy(st["l1b_bf"][:], st["l1b"][:]), waits=[mld])
    m_sg = R("scalar", lambda e: e.activation(st["rowC"][0:1, 64:80], st["rowC"][0:1, 0:16],
                                              AF.Sigmoid), waits=[mld])
    m_sr = R("scalar", lambda e: e.activation(st["rowC"][0:1, 80:96], st["rowC"][0:1, 16:32],
                                              AF.Exp, scale=-1.0), waits=[m_sg])
    m_a0 = R("vector", lambda e: e.tensor_scalar(st["rowC"][0:1, 48:64], st["rowC"][0:1, 64:80],
                                                 -1.0, 1.0, ALU.mult, ALU.add), waits=[m_sr])
    m_a = R("vector", lambda e: e.tensor_mul(st["rowC"][0:1, 96:112], st["rowC"][0:1, 48:64],
                                             st["rowC"][0:1, 80:96]), waits=[m_a0])
    m_b = R("vector", lambda e: e.tensor_mul(st["rowC"][0:1, 112:128], st["rowC"][0:1, 64:80],
                                             st["rowC"][0:1, 80:96]), waits=[m_a])
    m_rt = R("vector", lambda e: e.reciprocal(st["rowC"][0:1, 33:34], st["rowC"][0:1, 32:33]),
             waits=[m_b])

    pb = st["pb"]
    prev_copy = None
    for dst, off_, n, bank in ((st["smr"][:, 0:16], 96, 16, 2), (st["smr"][:, 16:32], 112, 16, 3),
                               (st["srecip_bc"][:], 80, 16, 4), (st["rtemp_bc"][:], 33, 1, 5)):
        mm = R("tensor", lambda e, d=dst, o=off_, n=n, bank=bank:
               e.matmul(pb[bank][:, 0:n], st["ones_fr"][:], st["rowC"][0:1, o:o + n],
                        start=True, stop=True), waits=[m_rt])
        prev_copy = R("scalar", lambda e, d=dst, n=n, bank=bank: e.copy(d, pb[bank][:, 0:n]),
                      waits=[mm])



def phase_l0_lnqkv(b, st):
    I, hbm, pb = st["I"], st["hbm"], st["pb"]
    R = b.R

    off = 0
    wnat = man(st, "p2_wnat", [128, 24, 1024], BF16, off)
    xnfm = man(st, "p2_xnfm", [128, 8, 512], BF16, off)
    xn2fm = man(st, "p2_xn2fm", [128, 8, 512], BF16, off + 8192)
    qsb = man(st, "p2_qsb", [128, 8, 512], BF16, off + 16384)
    ksb = man(st, "p2_ksb", [128, 8, 512], BF16, off + 24576)
    vsb = man(st, "p2_vsb", [128, 4, 1056], BF16, off + 32768)
    off += 48 * 1024
    wqkvT = man(st, "p2_wqkvT", [128, 8, 3072], BF16, off); off += 48 * 1024
    xg = [man(st, "p2_xg0", [128, 4, 1024], F32, off),
          man(st, "p2_xg1", [128, 4, 1024], F32, off + 16 * 1024)]
    off += 32 * 1024
    sq = man(st, "p2_sq", [128, 1024], F32, off); off += 4 * 1024
    xn_tm = man(st, "p2_xntm", [128, 4, 1024], BF16, off); off += 8 * 1024
    xn2_tm = man(st, "p2_xn2tm", [128, 4, 1024], BF16, off); off += 8 * 1024
    gdiff_bc = man(st, "p2_gdbc", [128, 1024], F32, off); off += 4 * 1024
    stt = man(st, "p2_stt", [128, 24], F32, off); off += 96
    mcols = man(st, "p2_mcols", [128, 32], F32, off); off += 160
    grow = man(st, "p2_grow", [1, 2 * 1024], F32, off); off += 8 * 1024
    p2scr = man(st, "p2_scr", [128, 256], F32, off); off += 1024
    bcbeta = man(st, "p2_bcbeta", [128, 1024], BF16, off); off += 2 * 1024

    pbh = [pb[6].bitcast(BF16), pb[7].bitcast(BF16)]
    pbh4 = [pb[4].bitcast(BF16), pb[5].bitcast(BF16), pb[6].bitcast(BF16), pb[7].bitcast(BF16)]

    mg0 = R("gpsimd", lambda e: e.dma_start(
        grow[0:1, 0:1024], I["L0_ffn_gate_w"][0].rearrange("(one d) -> one d", one=1)), dma=True)
    mg1 = R("gpsimd", lambda e: e.dma_start(
        grow[0:1, 1024:2048], I["L0_ffn_gate_w"][1].rearrange("(one d) -> one d", one=1)), dma=True)
    m_sub = R("vector", lambda e: e.tensor_sub(grow[0:1, 0:1024], grow[0:1, 1024:2048],
                                               grow[0:1, 0:1024]), waits=[mg0, mg1])
    for half in range(2):
        mm = R("tensor", lambda e, half=half: e.matmul(
            pb[half][:], st["ones_fr"][:], grow[0:1, 512 * half:512 * (half + 1)],
            start=True, stop=True), waits=[m_sub])
        m_gd = R("scalar", lambda e, half=half: e.copy(gdiff_bc[:, 512 * half:512 * (half + 1)],
                                                       pb[half][:]), waits=[mm])

    m_wnat = R("gpsimd", lambda e: e.dma_start(
        wnat[:], I["L0_attn_w_qkv"].rearrange("(c p) d -> p c d", p=128)), dma=True)
    m_brow = R("sync", lambda e: e.dma_start(
        st["rowA"][0:1, :], I["L0_attn_ln_beta"].rearrange("(one d) -> one d", one=1)), dma=True)
    m_bb = []
    for half in range(2):
        mm = R("tensor", lambda e, half=half: e.matmul(
            pb[2 + half][:], st["ones_fr"][:], st["rowA"][0:1, 512 * half:512 * (half + 1)],
            start=True, stop=True), waits=[m_brow])
        m_bb.append(R("scalar", lambda e, half=half: e.copy(
            bcbeta[:, 512 * half:512 * (half + 1)], pb[2 + half][:]), waits=[mm]))

    m_gq = R("vector", lambda e: e.tensor_scalar(st["scr2"][:, 16:24], st["g0a"][:], 0.125, None,
                                                 ALU.mult))
    m_bias = None
    for ec in range(24):
        m_t = R("vector", lambda e, ec=ec: e.tensor_mul(sq[:], wnat[:, ec, :], bcbeta[:]),
                waits=([m_wnat] + m_bb) if ec == 0 else [m_bias])
        m_bias = R("vector", lambda e, ec=ec: e.reduce_sum(st["qkvbias"][:, ec:ec + 1], sq[:],
                                                           axis=AX), waits=[m_t])
    m_bias = R("vector", lambda e: e.tensor_scalar(st["qkvbias"][:, 0:8], st["qkvbias"][:, 0:8],
                                                   0.125, None, ALU.mult), waits=[m_bias])

    tcnt = 0
    evac = {}
    for ec in range(24):
        for dc in range(8):
            bank = tcnt % 4
            wl = [m_wnat, m_gq] if tcnt == 0 else []
            if tcnt >= 4:
                wl.append(evac[tcnt - 4])
            m_tr = R("tensor", lambda e, ec=ec, dc=dc, bank=bank:
                     e.transpose(pbh4[bank].ap()[:, 0:128], wnat[:, ec, 128 * dc:128 * (dc + 1)],
                                 st["ident"][:]), waits=wl)
            scl = st["scr2"][:, 16 + dc:17 + dc] if ec < 8 else st["g0a"][:, dc:dc + 1]
            evac[tcnt] = R("scalar", lambda e, ec=ec, dc=dc, bank=bank, scl=scl:
                           e.activation(wqkvT[:, dc, 128 * ec:128 * (ec + 1)],
                                        pbh4[bank].ap()[:, 0:128], AF.Identity, scale=scl),
                           waits=[m_tr])
            tcnt += 1
    m_wq = [evac[tcnt - 1], evac[tcnt - 2], evac[tcnt - 3], evac[tcnt - 4]]

    m_vones = None
    for tt in range(4):
        m_vones = R("vector", lambda e, tt=tt: e.memset(
            vsb[:, tt, :].rearrange("p (h x) -> p h x", h=16)[:, :, 64:66], 1.0),
            waits=(m_wq if tt == 0 else []))
    for tt in range(4):
        m_vones = R("vector", lambda e, tt=tt: e.memset(
            vsb[:, tt, :].rearrange("p (h x) -> p h x", h=16)[:, :, 65:66], 0.0),
            waits=[m_vones] if tt == 0 else [])

    rel_x = [[], []]
    rel_q = []
    rel_k = []
    rel_v = []
    rel_xnfm = []
    rel_tm = []
    for g in range(NG):
        buf = g % 2
        xga = xg[buf]
        m_x = R("sync", lambda e, g=g, xga=xga: e.dma_start(
            xga[:], I["x0"][512 * g:512 * (g + 1), :].rearrange("(a p) d -> p a d", p=128)),
            waits=rel_x[buf], dma=True)

        m_tm = []
        m_gate = []
        last_v = None
        for tt in range(4):
            ti = 4 * g + tt
            m1 = R("vector", lambda e, xga=xga, tt=tt: e.bn_stats(stt[:, 0:6], xga[:, tt, 0:512]),
                   waits=[m_x] + ([m_vones] if g == 0 and tt == 0 else [])
                   + (rel_tm if tt == 0 else []) + ([last_v] if last_v else []))
            m2 = R("vector", lambda e, xga=xga, tt=tt: e.bn_stats(stt[:, 6:12], xga[:, tt, 512:1024]),
                   waits=[m1])
            m3 = R("vector", lambda e: e.bn_aggr(stt[:, 12:14], stt[:, 0:12]), waits=[m2])
            mr1 = R("scalar", lambda e: e.activation(stt[:, 14:15], stt[:, 13:14], AF.Ln,
                                                     bias=st["epsb"][:]), waits=[m3])
            mr2 = R("scalar", lambda e: e.activation(stt[:, 15:16], stt[:, 14:15], AF.Exp,
                                                     scale=-0.5), waits=[mr1])
            ma1 = R("vector", lambda e: e.tensor_scalar(stt[:, 16:17], stt[:, 13:14], EPS, None,
                                                        ALU.add), waits=[m3])
            ma2 = R("vector", lambda e: e.reciprocal(stt[:, 17:18], stt[:, 16:17]), waits=[ma1])
            ma3 = R("vector", lambda e: e.tensor_mul(stt[:, 18:19], stt[:, 13:14], stt[:, 17:18]),
                    waits=[ma2])
            ma4 = R("scalar", lambda e: e.activation(stt[:, 19:20], stt[:, 18:19], AF.Ln,
                                                     bias=st["epsb"][:]), waits=[ma3, mr2])
            ma5 = R("scalar", lambda e: e.activation(stt[:, 20:21], stt[:, 19:20], AF.Exp,
                                                     scale=-0.5), waits=[ma4])
            mra = R("vector", lambda e: e.tensor_mul(stt[:, 21:22], stt[:, 15:16], stt[:, 20:21]),
                    waits=[ma5])
            mx1 = R("vector", lambda e, xga=xga, tt=tt: e.tensor_scalar(
                xn_tm[:, tt, :], xga[:, tt, :], stt[:, 12:13], stt[:, 15:16],
                ALU.subtract, ALU.mult), waits=[mra])
            mx2 = R("vector", lambda e, xga=xga, tt=tt: e.tensor_scalar(
                xn2_tm[:, tt, :], xga[:, tt, :], stt[:, 12:13], stt[:, 21:22],
                ALU.subtract, ALU.mult), waits=[mx1])
            last_v = R("vector", lambda e, xga=xga, tt=tt, ti=ti: e.scalar_tensor_tensor(
                sq[:], xga[:, tt, :], stt[:, 12:13], gdiff_bc[:],
                ALU.subtract, ALU.mult, accum_out=mcols[:, ti:ti + 1]), waits=[mx2])
            m_tm.append(last_v)
        rel_x[buf] = [last_v]

        tpos = 0
        evf = {}
        for tt in range(4):
            for c in range(8):
                for (src, dst) in ((xn_tm, xnfm), (xn2_tm, xn2fm)):
                    bank = tpos % 4
                    wl = [m_tm[tt]]
                    if tpos == 0:
                        wl += rel_xnfm
                    if tpos >= 4:
                        wl.append(evf[tpos - 4])
                    m_tr = R("tensor", lambda e, tt=tt, c=c, bank=bank, src=src: e.transpose(
                        pbh4[bank].ap()[:, 0:128], src[:, tt, 128 * c:128 * (c + 1)],
                        st["ident"][:]), waits=wl)
                    evf[tpos] = R("scalar", lambda e, tt=tt, c=c, bank=bank, dst=dst: e.copy(
                        dst[:, c, 128 * tt:128 * (tt + 1)], pbh4[bank].ap()[:, 0:128]),
                        waits=[m_tr])
                    tpos += 1
        m_fm = [evf[tpos - 1], evf[tpos - 2], evf[tpos - 3], evf[tpos - 4]]

        m_st_xn = R("sync", lambda e, g=g: e.dma_start(
            hbm["xn0"][:, :, 512 * g:512 * (g + 1)].rearrange("c p t -> p c t"), xnfm[:]),
            waits=m_fm, dma=True)
        rel_xnfm = [m_st_xn]

        m_ev = {}
        for part, dstsb, rel in ((0, qsb, rel_q), (1, ksb, rel_k)):
            for ec8 in range(8):
                ec = part * 8 + ec8
                bank = ec8 % 2
                key = (part, ec8)
                wl0 = list(m_fm)
                if g == 0 and ec == 0:
                    wl0.append(m_bias)
                if ec8 == 0:
                    wl0 += rel
                prevk = (part, ec8 - 2) if ec8 >= 2 else ((1 - part, 6 + ec8) if (part == 1) else None)
                if prevk in m_ev:
                    wl0.append(m_ev[prevk])
                mlast = None
                for dc in range(8):
                    mlast = R("tensor", lambda e, ec=ec, dc=dc, bank=bank: e.matmul(
                        pb[bank][:], wqkvT[:, dc, 128 * ec:128 * (ec + 1)], xn2fm[:, dc, :],
                        start=(dc == 0), stop=(dc == 7)), waits=(wl0 if dc == 0 else []))
                m_ev[key] = R("scalar", lambda e, ec=ec, ec8=ec8, bank=bank, dstsb=dstsb:
                              e.activation(dstsb[:, ec8, :], pb[bank][:], AF.Identity,
                                           bias=st["qkvbias"][:, ec:ec + 1]), waits=[mlast])
        m_q = [m_ev[(0, 7)], m_ev[(0, 6)]]
        m_k = [m_ev[(1, 7)], m_ev[(1, 6)]]
        rel_q = [R("sync", lambda e, g=g: e.dma_start(
            hbm["q0h"][:, :, 512 * g:512 * (g + 1)].rearrange("c p t -> p c t"), qsb[:]),
            waits=m_q, dma=True)]
        rel_k = [R("sync", lambda e, g=g: e.dma_start(
            hbm["k0h"][:, :, 512 * g:512 * (g + 1)].rearrange("c p t -> p c t"), ksb[:]),
            waits=m_k, dma=True)]

        m_vev = {}
        last_vm = None
        for tt in range(4):
            for vc in range(2):
                vi = tt * 2 + vc
                bank = 2 + vi % 2
                wl0 = [m_ev[(1, 7)]]
                if vi == 0:
                    wl0 += rel_v
                if vi >= 2:
                    wl0.append(m_vev[vi - 2])
                for dc in range(8):
                    last_vm = R("tensor", lambda e, tt=tt, vc=vc, dc=dc, bank=bank: e.matmul(
                        pb[bank][:], xn2fm[:, dc, 128 * tt:128 * (tt + 1)],
                        wqkvT[:, dc, 2048 + 512 * vc:2048 + 512 * (vc + 1)],
                        start=(dc == 0), stop=(dc == 7)), waits=(wl0 if dc == 0 else []))
                m_vev[vi] = R("scalar", lambda e, tt=tt, vc=vc, bank=bank: e.activation(
                    vsb[:, tt, 528 * vc:528 * (vc + 1)].rearrange("p (h x) -> p h x", h=8)[:, :, 0:64],
                    pb[bank][:].rearrange("p (h x) -> p h x", h=8), AF.Identity), waits=[last_vm])
        rel_tm = [last_vm]
        rel_v = [R("sync", lambda e, g=g: e.dma_start(
            hbm["v0h"][512 * g:512 * (g + 1), :].rearrange("(a p) x -> p a x", p=128), vsb[:]),
            waits=[m_vev[6], m_vev[7]], dma=True)]

    m_mt = R("tensor", lambda e: e.transpose(pb[4][0:32, 0:128], mcols[:, 0:32], st["ident_f"][:]),
             waits=[last_v] + m_fm)
    m_m1 = R("vector", lambda e: e.tensor_single_scalar(p2scr[0:32, 0:128], pb[4][0:32, 0:128],
                                                        0.0, ALU.is_gt), waits=[m_mt])
    m_m0 = R("vector", lambda e: e.tensor_scalar(p2scr[0:32, 128:256], p2scr[0:32, 0:128],
                                                 -1.0, 1.0, ALU.mult, ALU.add), waits=[m_m1])
    R("gpsimd", lambda e: e.dma_start(hbm["mask01"][1:2, :], p2scr[0:32, 0:128]),
      waits=[m_m0], dma=True)
    R("gpsimd", lambda e: e.dma_start(hbm["mask01"][0:1, :], p2scr[0:32, 128:256]),
      waits=[m_m0], dma=True)


def phase_l0_attn(b, st):
    I, hbm, pb = st["I"], st["hbm"], st["pb"]
    R = b.R
    pbh = [pb[6].bitcast(BF16), pb[7].bitcast(BF16)]

    off = 0
    wonat = man(st, "p3_wonat", [128, 8, 1024], BF16, off)
    o_sb = man(st, "p3_osb", [128, 8, 512], BF16, off)
    s_sb = man(st, "p3_ssb", [128, 4, 1024], F32, off + 8192)
    off += 24 * 1024
    woutT = man(st, "p3_woutT", [128, 8, 1024], BF16, off); off += 16 * 1024
    q_aug = man(st, "p3_qaug", [65, 16, 512], BF16, off); off += 16 * 1024
    k_aug = [man(st, "p3_kaug0", [65, 16, 512], BF16, off),
             man(st, "p3_kaug1", [65, 16, 512], BF16, off + 16 * 1024)]
    off += 32 * 1024
    v_st = [man(st, "p3_vst0", [128, 4, 1056], BF16, off),
            man(st, "p3_vst1", [128, 4, 1056], BF16, off + 8448)]
    off += 16896
    p_t = [man(st, f"p3_pt{i}", [128, 512], BF16, off + 1024 * i) for i in range(3)]
    off += 3 * 1024
    p_t2 = [man(st, f"p3_pt2{i}", [128, 512], BF16, off + 1024 * i) for i in range(3)]
    off += 3 * 1024
    cT = man(st, "p3_cT", [16, 512], BF16, off); off += 1024
    cg = man(st, "p3_cg", [128, 16], F32, off); off += 64
    mwin = man(st, "p3_mwin", [128, 16, 4, 2], F32, off); off += 512
    mtri = man(st, "p3_mtri", [128, 128], F32, off); off += 512
    s1m = man(st, "p3_s1m", [128, 128], F32, off); off += 512
    rrow = man(st, "p3_rrow", [1, 512], F32, off); off += 2048
    r_bc = man(st, "p3_rbc", [128, 512], F32, off); off += 2048
    o_t = man(st, "p3_ot", [64, 512], F32, off); off += 2048
    o_acc = man(st, "p3_oacc", [66, 16, 512], F32, off); off += 32 * 1024

    m1 = R("vector", lambda e: e.tensor_single_scalar(mtri[:], st["iotaD"][:, 0:128], 0.0, ALU.is_gt))
    m_mtri = R("vector", lambda e: e.tensor_scalar(mtri[:], mtri[:], -1e30, None, ALU.mult),
               waits=[m1])

    m_wo = R("gpsimd", lambda e: e.dma_start(
        wonat[:], I["L0_attn_w_out"].rearrange("(c p) e -> p c e", p=128)), dma=True)
    evw = {}
    for dc in range(8):
        for ec in range(8):
            t = dc * 8 + ec
            bank = t % 2
            wl = [m_wo] if t == 0 else []
            if t >= 2:
                wl.append(evw[t - 2])
            m_tr = R("tensor", lambda e, dc=dc, ec=ec, bank=bank: e.transpose(
                pbh[bank].ap()[:, 0:128], wonat[:, dc, 128 * ec:128 * (ec + 1)], st["ident"][:]),
                waits=wl)
            evw[t] = R("scalar", lambda e, dc=dc, ec=ec, bank=bank: e.copy(
                woutT[:, ec, 128 * dc:128 * (dc + 1)], pbh[bank].ap()[:, 0:128]), waits=[m_tr])
    m_woutT = [evw[62], evw[63]]

    m_krow = []
    for buf in range(2):
        m_krow.append(R("vector", lambda e, buf=buf: e.memset(k_aug[buf][64:65, :, :], -1.0)))

    qsrc = hbm["q0h"].rearrange("c (p2 d) t -> d (c p2) t", p2=2)
    ksrc = hbm["k0h"].rearrange("c (p2 d) t -> d (c p2) t", p2=2)

    rel_q = []
    rel_k = [[], []]
    rel_v = [[], []]
    rel_osb = []
    rel_ssb = []
    rel_pt = []
    rel_stb = []
    prev_evc = []
    add_by_bank = {}
    m_rbe = None
    for g in range(NG):
        gsl = slice(512 * g, 512 * (g + 1))
        m_q = R("gpsimd", lambda e, gsl=gsl: e.dma_start(q_aug[0:64, :, :], qsrc[:, :, gsl]),
                waits=rel_q + ([m_krow[0], m_krow[1], m_mtri] if g == 0 else []), dma=True)

        dbuf = g % 2
        m_k = R("sync", lambda e, g=g, dbuf=dbuf: e.dma_start(
            k_aug[dbuf][0:64, :, :], ksrc[:, :, 512 * g:512 * (g + 1)]),
            waits=rel_k[dbuf] + ([m_woutT[0]] if g == 0 else []), dma=True)
        red_marks = {}
        pp = 0
        last_p1 = None
        for h in range(16):
            for qt in range(4):
                bank = pp % 2
                N = 128 * qt + 128
                wl = [m_q, m_k]
                if pp >= 2:
                    wl += red_marks[pp - 2]
                elif rel_osb:
                    wl += rel_osb
                mm = R("tensor", lambda e, h=h, qt=qt, bank=bank, N=N, dbuf=dbuf: e.matmul(
                    pb[bank][:, 0:N], q_aug[0:64, h, 128 * qt:128 * (qt + 1)],
                    k_aug[dbuf][0:64, h, 0:N], start=True, stop=True), waits=wl)
                last_p1 = mm
                rm = []
                if qt > 0:
                    rm.append(R("vector", lambda e, h=h, qt=qt, bank=bank: e.reduce_max(
                        mwin[:, h, qt, 0:1], pb[bank][:, 0:128 * qt], axis=AX),
                        waits=[mm] + (prev_evc if pp == 0 else [])))
                mt = R("vector", lambda e, h=h, qt=qt, bank=bank: e.tensor_add(
                    s1m[:], pb[bank][:, 128 * qt:128 * qt + 128], mtri[:]),
                    waits=[mm] + (rm if rm else []) + (prev_evc if (pp == 0 and not rm) else []))
                rm.append(R("vector", lambda e, h=h, qt=qt: e.reduce_max(
                    mwin[:, h, qt, 1:2], s1m[:], axis=AX), waits=[mt]))
                red_marks[pp] = rm
                pp += 1
        rel_k[dbuf] = [last_p1]
        m_p1 = red_marks[pp - 1] + red_marks[pp - 2]

        ev_c = []
        for qt in range(4):
            mred = []
            for h in range(16):
                lo = 1 if qt == 0 else 0
                mred.append(R("vector", lambda e, h=h, qt=qt, lo=lo: e.reduce_max(
                    cg[:, h:h + 1], mwin[:, h, qt, lo:2], axis=AX),
                    waits=(m_p1 if (qt == 0 and h == 0) else [])
                    + ([ev_c[-1]] if (h == 0 and qt > 0) else [])))
            m_ct = R("tensor", lambda e, qt=qt: e.transpose(pb[4][0:16, 0:128], cg[:, 0:16],
                                                            st["ident_f"][:]),
                     waits=mred + ([m_rbe] if (qt == 0 and m_rbe) else []))
            ev_c.append(R("scalar", lambda e, qt=qt: e.copy(cT[:, 128 * qt:128 * (qt + 1)],
                                                            pb[4][0:16, 0:128]), waits=[m_ct]))
        m_cdma = R("gpsimd", lambda e: e.dma_start(q_aug[64:65, :, :], cT[:]),
                   waits=ev_c, dma=True)
        prev_evc = [ev_c[-1]]

        pp2 = 0
        pvcnt = 0
        pv_marks = {}
        exp_marks = {}
        st_marks = {}
        cur_w = -1
        m_k2 = m_v2 = None
        lkb = {0: None, 1: None}
        lkb[dbuf] = last_p1
        lvb = {0: None, 1: None}
        m_add = {}
        pend = None

        def rec_pv(t):
            wl2 = list(t["wl_pv"])
            if t["fst"] and t["pvb"] in add_by_bank:
                wl2.append(add_by_bank.pop(t["pvb"]))
            mpv = R("tensor", lambda e, h=t["h"], kc=t["kc"], pvb=t["pvb"], buf=t["buf"],
                    fst=t["fst"], fsp=t["fsp"], pvt=t["pvt"]: e.matmul(
                        pb[pvb][0:66, :], v_st[buf][:, kc, 66 * h:66 * h + 66],
                        pvt[:], start=fst, stop=fsp), waits=wl2)
            pv_marks[t["pp"]] = mpv
            lvb[t["buf"]] = mpv
            if t["fsp"]:
                h = t["h"]
                pvb = t["pvb"]
                if t["w"] == 0:
                    m_add[h] = R("vector", lambda e, h=h, pvb=pvb: e.tensor_copy(
                        o_acc[:, h, :], pb[pvb][0:66, :]), waits=[mpv])
                else:
                    m_add[h] = R("vector", lambda e, h=h, pvb=pvb: e.tensor_add(
                        o_acc[:, h, :], o_acc[:, h, :], pb[pvb][0:66, :]),
                        waits=[mpv, m_add[h]])
                add_by_bank[pvb] = m_add[h]

        for w in range(g + 1):
            buf = w % 2
            wk = [lkb[buf]] if lkb[buf] else rel_k[buf]
            wv = [lvb[buf]] if lvb[buf] else rel_v[buf]
            m_k2 = R("sync", lambda e, w=w, buf=buf: e.dma_start(
                k_aug[buf][0:64, :, :], ksrc[:, :, 512 * w:512 * (w + 1)]),
                waits=wk, dma=True)
            m_v2 = R("sync", lambda e, w=w, buf=buf: e.dma_start(
                v_st[buf][:], hbm["v0h"][512 * w:512 * (w + 1), :].rearrange(
                    "(a p) x -> p a x", p=128)),
                waits=wv, dma=True)
            for h in range(16):
                pvb = pvcnt % 4
                pvcnt += 1
                for kc in range(4):
                    stbank = 6 + pp2 % 2
                    rot = pp2 % 3
                    diag = (w == g)
                    wl = [m_k2, m_cdma]
                    if pp2 >= 2:
                        wl.append(exp_marks[pp2 - 2])
                    else:
                        wl += rel_stb
                    mm = R("tensor", lambda e, h=h, kc=kc, stbank=stbank, buf=buf: e.matmul(
                        pb[stbank][:], k_aug[buf][0:65, h, 128 * kc:128 * (kc + 1)],
                        q_aug[0:65, h, :], start=True, stop=True), waits=wl)
                    lkb[buf] = mm
                    st_marks[pp2] = mm
                    wl_e = [mm]
                    if pp2 >= 3:
                        wl_e.append(pv_marks[pp2 - 3])
                    elif rel_pt:
                        wl_e.append(rel_pt[min(pp2, len(rel_pt) - 1)])
                    m_exp = R("scalar", lambda e, rot=rot, stbank=stbank: e.activation(
                        p_t[rot][:], pb[stbank][:], AF.Exp), waits=wl_e)
                    exp_marks[pp2] = m_exp
                    wl_pv = [m_exp, m_v2]
                    pvt = p_t[rot]
                    if diag:
                        m_af = R("gpsimd", lambda e, rot=rot, kc=kc: e.affine_select(
                            p_t2[rot][:], p_t[rot][:], [[1, 512]], ALU.is_ge, 0.0,
                            base=-128 * kc, channel_multiplier=-1), waits=[m_exp])
                        wl_pv = [m_af, m_v2]
                        pvt = p_t2[rot]
                    t = dict(pp=pp2, h=h, kc=kc, pvb=pvb, buf=buf, pvt=pvt, wl_pv=wl_pv,
                             fst=(kc == 0), fsp=(kc == 3), w=w)
                    if pend is not None:
                        rec_pv(pend)
                    pend = t
                    pp2 += 1
        rec_pv(pend)
        pend = None
        for bb in (0, 1):
            if lkb[bb]:
                rel_k[bb] = [lkb[bb]]
            if lvb[bb]:
                rel_v[bb] = [lvb[bb]]
        rel_stb = [exp_marks[pp2 - 1], exp_marks[pp2 - 2]]
        rel_pt = [pv_marks[k] for k in sorted(pv_marks)[-3:]]
        rel_q = [st_marks[pp2 - 1]]

        ev_o = []
        m_on_prev = None
        for h in range(16):
            m_r = R("vector", lambda e, h=h: e.reciprocal(rrow[:], o_acc[64:65, h, :]),
                    waits=[m_add[h]] + ([m_rbe] if m_rbe else []))
            m_rb = R("tensor", lambda e: e.matmul(pb[4][:], st["ones_fr"][:], rrow[:],
                                                  start=True, stop=True),
                     waits=[m_r] + ([m_rbe] if m_rbe else []))
            m_rbe = R("scalar", lambda e: e.copy(r_bc[:], pb[4][:]), waits=[m_rb])
            m_on = R("vector", lambda e, h=h: e.tensor_mul(
                o_t[:], o_acc[0:64, h, :], r_bc[0:64, :]),
                waits=[m_rbe] + ([m_on_prev] if m_on_prev else []))
            m_on_prev = m_on
            vb = st["qkvbias"][(h % 2) * 64:(h % 2) * 64 + 64, 16 + h // 2:17 + h // 2]
            ev_o.append(R("scalar", lambda e, h=h, vb=vb: e.activation(
                o_sb[(h % 2) * 64:(h % 2) * 64 + 64, h // 2, :], o_t[:], AF.Identity,
                bias=vb), waits=[m_on] + (rel_osb if (g > 0 and h == 0) else []
                                          + ([m_woutT[1]] if g == 0 and h == 0 else []))))

        ev_s = {}
        for tt in range(4):
            for ds in range(2):
                ti2 = tt * 2 + ds
                bank = ti2 % 2
                wl = []
                if ti2 == 0:
                    wl = list(ev_o) + rel_ssb
                    if 0 in add_by_bank:
                        wl.append(add_by_bank.pop(0))
                if ti2 == 1 and 1 in add_by_bank:
                    wl.append(add_by_bank.pop(1))
                if ti2 >= 2:
                    wl.append(ev_s[ti2 - 2])
                mlast = None
                for ec in range(8):
                    mlast = R("tensor", lambda e, tt=tt, ds=ds, ec=ec, bank=bank: e.matmul(
                        pb[bank][:], o_sb[:, ec, 128 * tt:128 * (tt + 1)],
                        woutT[:, ec, 512 * ds:512 * (ds + 1)],
                        start=(ec == 0), stop=False), waits=(wl if ec == 0 else []))
                mlast = R("tensor", lambda e, ds=ds, bank=bank: e.matmul(
                    pb[bank][:], st["ones_bf"][:], st["bout_row"][0:1, 512 * ds:512 * (ds + 1)],
                    start=False, stop=True), waits=[])
                ev_s[ti2] = R("scalar", lambda e, tt=tt, ds=ds, bank=bank: e.copy(
                    s_sb[:, tt, 512 * ds:512 * (ds + 1)], pb[bank][:]), waits=[mlast])
        m_sst = R("sync", lambda e, g=g, gsl=gsl: e.dma_start(
            hbm["s0h"][gsl, :].rearrange("(a p) d -> p a d", p=128), s_sb[:]),
            waits=[ev_s[6], ev_s[7]], dma=True)
        rel_ssb = [m_sst]
        rel_osb = [ev_s[6], ev_s[7]]
        add_by_bank[0] = ev_s[6]
        add_by_bank[1] = ev_s[7]


def phase_l0_ffn(b, st):
    I, hbm, pb = st["I"], st["hbm"], st["pb"]
    R = b.R

    off = 0
    xnr = man(st, "p4_xnr", [128, 8, T], BF16, off); off += 64 * 1024
    w1h = man(st, "p4_w1h", [128, 8, 2048], BF16, off)
    yg = man(st, "p4_yg", [128, 4, 1024], F32, off)
    sg = man(st, "p4_sg", [128, 4, 1024], F32, off + 16384)
    off += 32 * 1024
    w2h = man(st, "p4_w2h", [128, 16, 1024], BF16, off)
    xg2 = man(st, "p4_xg2", [128, 4, 1024], F32, off)
    x1s = man(st, "p4_x1s", [128, 4, 1024], F32, off + 16384)
    off += 32 * 1024
    xm = man(st, "p4_xm", [128, 8, 512], BF16, off); off += 8 * 1024
    h_sb = man(st, "p4_hsb", [128, 16, 512], BF16, off); off += 16 * 1024
    ysb = man(st, "p4_ysb", [128, 4, 1024], F32, off); off += 16 * 1024
    mrow = man(st, "p4_mrow", [1, T], BF16, off); off += 8 * 1024
    mbc = man(st, "p4_mbc", [128, 512], BF16, off); off += 1024

    m_xn = R("gpsimd", lambda e: e.dma_start(xnr[:], hbm["xn0"].rearrange("c p t -> p c t")),
             dma=True)


    WSRC = [(I["L0_ffn_w1"][0], I["L0_ffn_w2"][0]), (I["L0_ffn_w1"][1], I["L0_ffn_w2"][1]),
            (I["L0_ffn_sw1"], I["L0_ffn_sw2"])]
    rel_w = []
    rel_xm = []
    rel_hsb = []
    rel_ysb = []
    rel_yacc = {}
    rel_mbc = []
    first = True
    for e_i in range(3):
        w1s, w2s = WSRC[e_i]
        if e_i < 2:
            m_mr = R("sync", lambda e, e_i=e_i: e.dma_start(
                mrow[:], hbm["mask01"][e_i:e_i + 1, :]), waits=rel_mbc, dma=True)
        for hh in range(2):
            m_w1 = R("gpsimd", lambda e, w1s=w1s, hh=hh: e.dma_start(
                w1h[:], w1s[:, 2048 * hh:2048 * (hh + 1)].rearrange("(c p) h -> p c h", p=128)),
                waits=rel_w, dma=True)
            m_w2 = R("gpsimd", lambda e, w2s=w2s, hh=hh: e.dma_start(
                w2h[:], w2s[2048 * hh:2048 * (hh + 1), :].rearrange("(c p) d -> p c d", p=128)),
                waits=rel_w, dma=True)
            for g in range(NG):
                gsl = slice(512 * g, 512 * (g + 1))
                if e_i < 2:
                    m_mm = R("tensor", lambda e, e_i=e_i, g=g: e.matmul(
                        pb[6][:], st["ones_bf"][:], mrow[0:1, 512 * g:512 * (g + 1)],
                        start=True, stop=True), waits=[m_mr] + rel_mbc)
                    m_mb = R("scalar", lambda e: e.copy(mbc[:], pb[6][:]), waits=[m_mm])
                    rel_mbc = [m_mb]
                    m_xm = None
                    for c in range(8):
                        m_xm = R("vector", lambda e, c=c, g=g: e.tensor_mul(
                            xm[:, c, :], xnr[:, c, 512 * g:512 * (g + 1)], mbc[:]),
                            waits=([m_mb, m_xn] + rel_xm) if c == 0 else [])
                    rhs_x = xm
                    xoff = 0
                    m_rx = m_xm
                else:
                    rhs_x = xnr
                    xoff = 512 * g
                    m_rx = m_xn if first else None

                ev_h = {}
                for hc in range(16):
                    bank = 4 + hc % 2
                    wl = []
                    if hc == 0:
                        wl = [m_w1] + ([m_rx] if m_rx else []) + rel_hsb
                    if hc >= 2:
                        wl.append(ev_h[hc - 2])
                    mlast = None
                    for dc in range(8):
                        mlast = R("tensor", lambda e, hc=hc, dc=dc, bank=bank, rhs_x=rhs_x,
                                  xoff=xoff: e.matmul(
                                      pb[bank][:], w1h[:, dc, 128 * hc:128 * (hc + 1)],
                                      rhs_x[:, dc, xoff:xoff + 512],
                                      start=(dc == 0), stop=(dc == 7)),
                                  waits=(wl if dc == 0 else []))
                    bcol = 32 * e_i + 16 * hh + hc
                    ev_h[hc] = R("scalar", lambda e, hc=hc, bank=bank, bcol=bcol: e.activation(
                        h_sb[:, hc, :], pb[bank][:], AF.Relu, bias=st["b1e"][:, bcol:bcol + 1]),
                        waits=[mlast])
                rel_hsb = []

                ev_y = {}
                for tt in range(4):
                    for ds in range(2):
                        ti2 = tt * 2 + ds
                        bank = ti2 % 4
                        wl = [m_w2]
                        if ti2 == 0:
                            wl += rel_ysb
                        if ti2 < 2:
                            wl += [ev_h[14], ev_h[15]]
                        if ti2 >= 4:
                            wl.append(ev_y[ti2 - 4])
                        mlast = None
                        for hc in range(16):
                            mlast = R("tensor", lambda e, tt=tt, ds=ds, hc=hc, bank=bank: e.matmul(
                                pb[bank][:], h_sb[:, hc, 128 * tt:128 * (tt + 1)],
                                w2h[:, hc, 512 * ds:512 * (ds + 1)],
                                start=(hc == 0), stop=(hc == 15 and not first)),
                                waits=(wl if hc == 0 else []))
                        if first:
                            mlast = R("tensor", lambda e, ds=ds, bank=bank: e.matmul(
                                pb[bank][:], st["ones_bf"][:],
                                st["b2s_row"][0:1, 512 * ds:512 * (ds + 1)],
                                start=False, stop=True), waits=[])
                        ev_y[ti2] = R("scalar", lambda e, tt=tt, ds=ds, bank=bank: e.copy(
                            ysb[:, tt, 512 * ds:512 * (ds + 1)], pb[bank][:]), waits=[mlast])
                rel_hsb = [mlast]
                wl = [ev_y[k] for k in (6, 7)] + ([rel_yacc[g]] if g in rel_yacc else [])
                if first:
                    m_acc = R("gpsimd", lambda e, g=g, gsl=gsl: e.dma_start(
                        hbm["y0h"][gsl, :].rearrange("(a p) d -> p a d", p=128), ysb[:]),
                        waits=wl, dma=True)
                else:
                    m_acc = R("gpsimd", lambda e, g=g, gsl=gsl: e.dma_start(
                        hbm["y0h"][gsl, :].rearrange("(a p) d -> p a d", p=128), ysb[:],
                        accum_op=ALU.add), waits=wl, dma=True)
                rel_yacc[g] = m_acc
                rel_ysb = [m_acc]
                rel_xm = [mlast]
            rel_w = [mlast, ev_h[15]]
            first = False

    rel_bufs = []
    for g in range(NG):
        gsl = slice(512 * g, 512 * (g + 1))
        m_y = R("sync", lambda e, g=g, gsl=gsl: e.dma_start(
            yg[:], hbm["y0h"][gsl, :].rearrange("(a p) d -> p a d", p=128)),
            waits=[rel_yacc[g]] + rel_bufs + (rel_w if g == 0 else []), dma=True)
        m_s = R("sync", lambda e, g=g, gsl=gsl: e.dma_start(
            sg[:], hbm["s0h"][gsl, :].rearrange("(a p) d -> p a d", p=128)), dma=True)
        m_x = R("sync", lambda e, g=g, gsl=gsl: e.dma_start(
            xg2[:], I["x0"][gsl, :].rearrange("(a p) d -> p a d", p=128)), dma=True)
        m_si = R("scalar", lambda e: e.activation(x1s[:], yg[:], AF.Silu), waits=[m_y] + rel_bufs)
        m_t1 = R("vector", lambda e: e.tensor_mul(yg[:], sg[:], x1s[:]), waits=[m_si, m_s])
        m_t2 = R("vector", lambda e: e.tensor_add(x1s[:], xg2[:], yg[:]), waits=[m_t1, m_x])
        m_st = R("sync", lambda e, g=g, gsl=gsl: e.dma_start(
            hbm["x1h"][gsl, :].rearrange("(a p) d -> p a d", p=128), x1s[:]),
            waits=[m_t2], dma=True)
        rel_bufs = [m_st, m_t2]


def phase_l1_ln(b, st):
    I, hbm, pb = st["I"], st["hbm"], st["pb"]
    R = b.R
    pbh = [pb[6].bitcast(BF16), pb[7].bitcast(BF16)]

    off = 0
    xg = [man(st, "p5_xg0", [128, 4, 1024], F32, off),
          man(st, "p5_xg1", [128, 4, 1024], F32, off + 16 * 1024)]
    off += 32 * 1024
    xn_tm = man(st, "p5_xntm", [128, 4, 1024], BF16, off); off += 8 * 1024
    xnn_tm = man(st, "p5_xnntm", [128, 4, 1024], BF16, off); off += 8 * 1024
    xnfm = man(st, "p5_xnfm", [128, 8, 512], BF16, off); off += 8 * 1024
    xnnfm = man(st, "p5_xnnfm", [128, 8, 512], BF16, off); off += 8 * 1024
    xifm = man(st, "p5_xifm", [128, 8, 512], BF16, off); off += 8 * 1024
    xqfm = man(st, "p5_xqfm", [128, 8, 512], BF16, off); off += 8 * 1024
    xkfm = man(st, "p5_xkfm", [128, 8, 512], BF16, off); off += 8 * 1024
    wrbc = man(st, "p5_wrbc", [128, 8, 1024], F32, off); off += 32 * 1024
    wrfm = man(st, "p5_wrfm", [128, 8, 8], F32, off); off += 256
    wrows = man(st, "p5_wrows", [8, 8, 128], F32, off); off += 4 * 1024
    stt = man(st, "p5_stt", [128, 24], F32, off); off += 96
    zc = man(st, "p5_zc", [128, 32, 8], F32, off); off += 1024
    RT = {}
    for i, nm in enumerate(("inh", "dist", "ez", "fb", "t1", "t2", "w", "fbw", "wf", "tmp")):
        RT[nm] = man(st, f"p5_rt_{nm}", [128, 32, 8], F32, off); off += 1024
    RC = {}
    for i, nm in enumerate(("ws", "ne", "m1", "m2", "fbs", "rfb", "om", "ws2", "rw2")):
        RC[nm] = man(st, f"p5_rc_{nm}", [128, 32], F32, off); off += 128
    off = (off + 31) // 32 * 32
    rw_sb = man(st, "p5_rwsb", [8, T], BF16, off); off += 8 * 1024
    cgoldb = man(st, "p5_cgold", [128, 1], F32, off); off += 32
    sq5 = man(st, "p5_sq", [128, 1024], F32, off); off += 4 * 1024

    m_cg = R("vector", lambda e: e.memset(cgoldb[:], -0.36787944117144233))

    m_wr = R("gpsimd", lambda e: e.dma_start(
        wrfm[:], I["L1_ffn_Wr"].rearrange("(c p) e -> p c e", p=128)), dma=True)
    ev_wr = {}
    for c in range(8):
        m_t = R("tensor", lambda e, c=c: e.transpose(pb[4][0:8, 0:128], wrfm[:, c, :],
                                                     st["ident_f"][:]),
                waits=[m_wr] + ([ev_wr[c - 1]] if c else []))
        ev_wr[c] = R("scalar", lambda e, c=c: e.copy(wrows[:, c, :], pb[4][0:8, 0:128]),
                     waits=[m_t])
    m_wrst = R("sync", lambda e: e.dma_start(hbm["wrr"].rearrange("e (c d) -> e c d", c=8),
                                             wrows[:]), waits=[ev_wr[7]], dma=True)
    prev_ev = None
    for e_i in range(8):
        m_ld = R("sync", lambda e, e_i=e_i: e.dma_start(
            st["rowB"][0:1, :], hbm["wrr"][e_i:e_i + 1, :]),
            waits=[m_wrst] + ([prev_ev] if prev_ev else []), dma=True)
        for half in range(2):
            m_mm = R("tensor", lambda e, half=half: e.matmul(
                pb[4][:], st["ones_fr"][:], st["rowB"][0:1, 512 * half:512 * (half + 1)],
                start=True, stop=True), waits=[m_ld] + ([prev_ev] if prev_ev else []))
            prev_ev = R("scalar", lambda e, e_i=e_i, half=half: e.copy(
                wrbc[:, e_i, 512 * half:512 * (half + 1)], pb[4][:]), waits=[m_mm])

    rel_x = [[], []]
    rel_tm = []
    rel_fm = []
    last_v = None
    for g in range(NG):
        buf = g % 2
        xga = xg[buf]
        gsl = slice(512 * g, 512 * (g + 1))
        m_x = R("sync", lambda e, gsl=gsl, xga=xga: e.dma_start(
            xga[:], hbm["x1h"][gsl, :].rearrange("(a p) d -> p a d", p=128)),
            waits=rel_x[buf], dma=True)
        m_tm = []
        for tt in range(4):
            ti = 4 * g + tt
            m1 = R("vector", lambda e, xga=xga, tt=tt: e.bn_stats(stt[:, 0:6], xga[:, tt, 0:512]),
                   waits=[m_x] + ([m_cg, prev_ev] if g == 0 and tt == 0 else [])
                   + (rel_tm if tt == 0 else []) + ([last_v] if last_v else []))
            m2 = R("vector", lambda e, xga=xga, tt=tt: e.bn_stats(stt[:, 6:12], xga[:, tt, 512:1024]),
                   waits=[m1])
            m3 = R("vector", lambda e: e.bn_aggr(stt[:, 12:14], stt[:, 0:12]), waits=[m2])
            mr1 = R("scalar", lambda e: e.activation(stt[:, 14:15], stt[:, 13:14], AF.Ln,
                                                     bias=st["epsb"][:]), waits=[m3])
            mr2 = R("scalar", lambda e: e.activation(stt[:, 15:16], stt[:, 14:15], AF.Exp,
                                                     scale=-0.5), waits=[mr1])
            ma1 = R("vector", lambda e: e.tensor_scalar(stt[:, 16:17], stt[:, 13:14], EPS, None,
                                                        ALU.add), waits=[m3])
            ma2 = R("vector", lambda e: e.reciprocal(stt[:, 17:18], stt[:, 16:17]), waits=[ma1])
            ma3 = R("vector", lambda e: e.tensor_mul(stt[:, 18:19], stt[:, 13:14], stt[:, 17:18]),
                    waits=[ma2])
            ma4 = R("scalar", lambda e: e.activation(stt[:, 19:20], stt[:, 18:19], AF.Ln,
                                                     bias=st["epsb"][:]), waits=[ma3, mr2])
            ma5 = R("scalar", lambda e: e.activation(stt[:, 20:21], stt[:, 19:20], AF.Exp,
                                                     scale=-0.5), waits=[ma4])
            mra = R("vector", lambda e: e.tensor_mul(stt[:, 21:22], stt[:, 15:16], stt[:, 20:21]),
                    waits=[ma5])
            mx1 = R("vector", lambda e, xga=xga, tt=tt: e.tensor_scalar(
                xn_tm[:, tt, :], xga[:, tt, :], stt[:, 12:13], stt[:, 15:16],
                ALU.subtract, ALU.mult), waits=[mra])
            mx2 = R("vector", lambda e, xga=xga, tt=tt: e.tensor_scalar(
                xnn_tm[:, tt, :], xga[:, tt, :], stt[:, 12:13], stt[:, 21:22],
                ALU.subtract, ALU.mult), waits=[mx1])
            mz = mx2
            for e_i in range(8):
                mz = R("vector", lambda e, xga=xga, tt=tt, ti=ti, e_i=e_i: e.scalar_tensor_tensor(
                    sq5[:], xga[:, tt, :], stt[:, 12:13], wrbc[:, e_i, :],
                    ALU.subtract, ALU.mult, accum_out=zc[:, ti, e_i:e_i + 1]), waits=[mz])
            last_v = R("vector", lambda e, ti=ti: e.tensor_scalar(
                zc[:, ti, :], zc[:, ti, :], stt[:, 15:16], None, ALU.mult), waits=[mz])
            m_tm.append(last_v)
        rel_x[buf] = [last_v]

        tpos = 0
        evf = {}
        for tt in range(4):
            for c in range(8):
                for (src, dst) in ((xn_tm, xnfm), (xnn_tm, xnnfm)):
                    bank = tpos % 2
                    wl = [m_tm[tt]]
                    if tpos == 0:
                        wl += rel_fm
                    if tpos >= 2:
                        wl.append(evf[tpos - 2])
                    m_tr = R("tensor", lambda e, tt=tt, c=c, bank=bank, src=src: e.transpose(
                        pbh[bank].ap()[:, 0:128], src[:, tt, 128 * c:128 * (c + 1)],
                        st["ident"][:]), waits=wl)
                    evf[tpos] = R("scalar", lambda e, tt=tt, c=c, bank=bank, dst=dst: e.copy(
                        dst[:, c, 128 * tt:128 * (tt + 1)], pbh[bank].ap()[:, 0:128]),
                        waits=[m_tr])
                    tpos += 1
        m_fm = [evf[tpos - 1], evf[tpos - 2]]
        ev_ap = []
        for idx, dstt in ((0, xifm), (1, xqfm), (2, xkfm)):
            for c in range(8):
                ev_ap.append(R("vector", lambda e, idx=idx, dstt=dstt, c=c: e.tensor_scalar(
                    dstt[:, c, :], xnnfm[:, c, :], st["l1w"][:, 8 * idx + c:8 * idx + c + 1],
                    st["l1b"][:, 8 * idx + c:8 * idx + c + 1], ALU.mult, ALU.add),
                    waits=(m_fm if (idx == 0 and c == 0) else []) + (rel_fm if (idx == 0 and c == 0) else [])))
        m_st = [
            R("sync", lambda e, gsl=gsl: e.dma_start(
                hbm["xn1"][:, :, gsl].rearrange("c p t -> p c t"), xnfm[:]),
                waits=m_fm, dma=True),
            R("sync", lambda e, gsl=gsl: e.dma_start(
                hbm["xin1"][:, :, gsl].rearrange("c p t -> p c t"), xifm[:]),
                waits=ev_ap[0:8], dma=True),
            R("sync", lambda e, gsl=gsl: e.dma_start(
                hbm["xq1"][:, :, gsl].rearrange("c p t -> p c t"), xqfm[:]),
                waits=ev_ap[8:16], dma=True),
            R("sync", lambda e, gsl=gsl: e.dma_start(
                hbm["xk1"][:, :, gsl].rearrange("c p t -> p c t"), xkfm[:]),
                waits=ev_ap[16:24], dma=True),
        ]
        rel_fm = list(m_st) + [ev_ap[-1]]
        rel_tm = [evf[tpos - 1]]

    zall = zc[:].rearrange("p a b -> p (a b)")
    m_inh = R("scalar", lambda e: e.activation(RT["inh"][:].rearrange("p a b -> p (a b)"), zall,
                                               AF.Sigmoid, scale=st["rtemp_bc"][:]),
              waits=[last_v])
    m_d = R("scalar", lambda e: e.activation(RT["dist"][:].rearrange("p a b -> p (a b)"),
                                             RT["inh"][:].rearrange("p a b -> p (a b)"),
                                             AF.Abs, bias=cgoldb[:]), waits=[m_inh])
    m_ez = R("scalar", lambda e: e.activation(RT["ez"][:].rearrange("p a b -> p (a b)"),
                                              RT["dist"][:].rearrange("p a b -> p (a b)"),
                                              AF.Exp, scale=-10.0), waits=[m_d])
    m_fb = R("scalar", lambda e: e.activation(RT["fb"][:].rearrange("p a b -> p (a b)"),
                                              RT["dist"][:].rearrange("p a b -> p (a b)"),
                                              AF.Exp, scale=-1.0 / 0.3), waits=[m_ez])
    GL = 0.5 - float(np.log(4.0 / 3.0))
    m_t1 = R("vector", lambda e: e.tensor_single_scalar(RT["t1"][:], RT["inh"][:], GL, ALU.is_ge),
             waits=[m_inh])
    m_t2 = R("vector", lambda e: e.tensor_single_scalar(RT["t2"][:], RT["inh"][:], 0.5, ALU.is_le),
             waits=[m_t1])
    m_zn = R("vector", lambda e: e.tensor_mul(RT["t1"][:], RT["t1"][:], RT["t2"][:]), waits=[m_t2])
    m_w = R("vector", lambda e: e.tensor_mul(RT["w"][:], RT["ez"][:], RT["t1"][:]),
            waits=[m_zn, m_ez])
    m_ws = R("vector", lambda e: e.reduce_sum(RC["ws"][:], RT["w"][:], axis=AX), waits=[m_w])
    m_ne = R("vector", lambda e: e.tensor_single_scalar(RC["ne"][:], RC["ws"][:], 1e-8, ALU.is_lt),
             waits=[m_ws])
    m_m1 = R("vector", lambda e: e.reduce_max(RC["m1"][:], RT["fb"][:], axis=AX), waits=[m_fb])
    m_eq = R("vector", lambda e: e.tensor_tensor(RT["t2"][:], RT["fb"][:],
                                                 RC["m1"][:].unsqueeze(2).broadcast_to((128, 32, 8)),
                                                 ALU.is_ge), waits=[m_m1])
    m_fbm = R("vector", lambda e: e.scalar_tensor_tensor(RT["tmp"][:], RT["t2"][:], -1e30,
                                                         RT["fb"][:], ALU.mult, ALU.add),
              waits=[m_eq])
    m_m2 = R("vector", lambda e: e.reduce_max(RC["m2"][:], RT["tmp"][:], axis=AX), waits=[m_fbm])
    m_k2 = R("vector", lambda e: e.tensor_tensor(RT["t2"][:], RT["fb"][:],
                                                 RC["m2"][:].unsqueeze(2).broadcast_to((128, 32, 8)),
                                                 ALU.is_ge), waits=[m_m2])
    m_fbw = R("vector", lambda e: e.tensor_mul(RT["fbw"][:], RT["fb"][:], RT["t2"][:]),
              waits=[m_k2])
    m_fbs = R("vector", lambda e: e.reduce_sum(RC["fbs"][:], RT["fbw"][:], axis=AX), waits=[m_fbw])
    m_fbc = R("vector", lambda e: e.tensor_scalar_max(RC["fbs"][:], RC["fbs"][:], 1e-8),
              waits=[m_fbs])
    m_rfb = R("vector", lambda e: e.reciprocal(RC["rfb"][:], RC["fbs"][:]), waits=[m_fbc])
    m_fbn = R("vector", lambda e: e.tensor_tensor(RT["fbw"][:], RT["fbw"][:],
                                                  RC["rfb"][:].unsqueeze(2).broadcast_to((128, 32, 8)),
                                                  ALU.mult), waits=[m_rfb])
    m_om = R("vector", lambda e: e.tensor_scalar(RC["om"][:], RC["ne"][:], -1.0, 1.0,
                                                 ALU.mult, ALU.add), waits=[m_ne])
    m_wa = R("vector", lambda e: e.tensor_tensor(RT["w"][:], RT["w"][:],
                                                 RC["om"][:].unsqueeze(2).broadcast_to((128, 32, 8)),
                                                 ALU.mult), waits=[m_om, m_ws])
    m_wb = R("vector", lambda e: e.tensor_tensor(RT["fbw"][:], RT["fbw"][:],
                                                 RC["ne"][:].unsqueeze(2).broadcast_to((128, 32, 8)),
                                                 ALU.mult), waits=[m_fbn, m_wa])
    m_wf = R("vector", lambda e: e.tensor_add(RT["wf"][:], RT["w"][:], RT["fbw"][:]),
             waits=[m_wb])
    m_ws2 = R("vector", lambda e: e.reduce_sum(RC["ws2"][:], RT["wf"][:], axis=AX), waits=[m_wf])
    m_wsc = R("vector", lambda e: e.tensor_scalar_max(RC["ws2"][:], RC["ws2"][:], 1e-8),
              waits=[m_ws2])
    m_rw2 = R("vector", lambda e: e.reciprocal(RC["rw2"][:], RC["ws2"][:]), waits=[m_wsc])
    m_wn = R("vector", lambda e: e.tensor_tensor(RT["wf"][:], RT["wf"][:],
                                                 RC["rw2"][:].unsqueeze(2).broadcast_to((128, 32, 8)),
                                                 ALU.mult), waits=[m_rw2])
    prev = None
    for ti in range(32):
        m_t = R("tensor", lambda e, ti=ti: e.transpose(pb[4][0:8, 0:128], RT["wf"][:, ti, :],
                                                       st["ident_f"][:]),
                waits=[m_wn] + ([prev] if prev else []))
        prev = R("scalar", lambda e, ti=ti: e.copy(rw_sb[:, 128 * ti:128 * (ti + 1)],
                                                   pb[4][0:8, 0:128]), waits=[m_t])
    R("sync", lambda e: e.dma_start(hbm["rw1"], rw_sb[:]), waits=[prev], dma=True)


def phase_l1_winp(b, st):
    I, hbm, pb = st["I"], st["hbm"], st["pb"]
    R = b.R

    off = 0
    winv = man(st, "p6_winv", [128, 8, 2048], BF16, off); off += 32 * 1024
    winp = man(st, "p6_winp", [128, 8, 2048], BF16, off); off += 32 * 1024
    xi = [man(st, "p6_xi0", [128, 8, 512], BF16, off),
          man(st, "p6_xi1", [128, 8, 512], BF16, off + 8192)]
    off += 16 * 1024
    vsb = man(st, "p6_vsb", [128, 4, 2048], BF16, off); off += 16 * 1024
    psb = man(st, "p6_psb", [128, 16, 512], BF16, off); off += 16 * 1024

    m_wv = R("gpsimd", lambda e: e.dma_start(
        winv[:], I["L1_attn_W_in"][:, 0:2048].rearrange("(c p) o -> p c o", p=128)), dma=True)
    m_wp = R("gpsimd", lambda e: e.dma_start(
        winp[:], I["L1_attn_W_in"][:, 2048:4096].rearrange("(c p) o -> p c o", p=128)), dma=True)

    rel_xi = [[], []]
    rel_v = []
    rel_p = []
    for g in range(NG):
        buf = g % 2
        gsl = slice(512 * g, 512 * (g + 1))
        m_x = R("gpsimd", lambda e, gsl=gsl, buf=buf: e.dma_start(
            xi[buf][:], hbm["xin1"][:, :, gsl].rearrange("c p t -> p c t")),
            waits=rel_xi[buf], dma=True)
        ev_v = {}
        for tt in range(4):
            for vs in range(4):
                ti2 = tt * 4 + vs
                bank = ti2 % 2
                wl = [m_x, m_wv] if ti2 == 0 else []
                if ti2 == 0:
                    wl += rel_v
                if ti2 >= 2:
                    wl.append(ev_v[ti2 - 2])
                mlast = None
                for dc in range(8):
                    mlast = R("tensor", lambda e, tt=tt, vs=vs, dc=dc, bank=bank, buf=buf: e.matmul(
                        pb[bank][:], xi[buf][:, dc, 128 * tt:128 * (tt + 1)],
                        winv[:, dc, 512 * vs:512 * (vs + 1)],
                        start=(dc == 0), stop=(dc == 7)), waits=(wl if dc == 0 else []))
                ev_v[ti2] = R("scalar", lambda e, tt=tt, vs=vs, bank=bank: e.copy(
                    vsb[:, tt, 512 * vs:512 * (vs + 1)], pb[bank][:]), waits=[mlast])
        rel_v = [R("sync", lambda e, gsl=gsl: e.dma_start(
            hbm["v1h"][gsl, :].rearrange("(a p) x -> p a x", p=128), vsb[:]),
            waits=[ev_v[14], ev_v[15]], dma=True)]
        ev_p = {}
        for pc in range(16):
            bank = 2 + pc % 2
            wl = [m_x, m_wp] if pc == 0 else []
            if pc == 0:
                wl += rel_p
            if pc >= 2:
                wl.append(ev_p[pc - 2])
            mlast = None
            for dc in range(8):
                mlast = R("tensor", lambda e, pc=pc, dc=dc, bank=bank, buf=buf: e.matmul(
                    pb[bank][:], winp[:, dc, 128 * pc:128 * (pc + 1)], xi[buf][:, dc, :],
                    start=(dc == 0), stop=(dc == 7)), waits=(wl if dc == 0 else []))
            ev_p[pc] = R("scalar", lambda e, pc=pc, bank=bank: e.copy(
                psb[:, pc, :], pb[bank][:]), waits=[mlast])
        rel_p = [R("sync", lambda e, gsl=gsl: e.dma_start(
            hbm["p1h"][:, :, gsl].rearrange("c p t -> p c t"), psb[:]),
            waits=[ev_p[14], ev_p[15]], dma=True)]
        rel_xi[buf] = [mlast, ev_v[15]]




def phase_l1_qk(b, st):
    I, hbm, pb = st["I"], st["hbm"], st["pb"]
    R = b.R
    pbh = [pb[6].bitcast(BF16), pb[7].bitcast(BF16)]

    off = 0
    wq = man(st, "p6b_wq", [128, 8, 2048], BF16, off); off += 32 * 1024
    wk = man(st, "p6b_wk", [128, 8, 2048], BF16, off); off += 32 * 1024
    xq = [man(st, "p6b_xq0", [128, 8, 512], BF16, off),
          man(st, "p6b_xq1", [128, 8, 512], BF16, off + 8192)]
    off += 16 * 1024
    xk = [man(st, "p6b_xk0", [128, 8, 512], BF16, off),
          man(st, "p6b_xk1", [128, 8, 512], BF16, off + 8192)]
    off += 16 * 1024
    tm = man(st, "p6b_tm", [128, 4, 2048], BF16, off); off += 16 * 1024
    tb = man(st, "p6b_tb", [128, 4, 2048], BF16, off); off += 16 * 1024
    qfm = man(st, "p6b_qfm", [128, 16, 512], BF16, off); off += 16 * 1024
    kfm = man(st, "p6b_kfm", [128, 16, 512], BF16, off); off += 16 * 1024
    ksm = man(st, "p6b_ksm", [128, 16, 512], BF16, off); off += 16 * 1024
    kprev = man(st, "p6b_kprev", [128, 16], BF16, off); off += 32
    sfx = man(st, "p6b_sfx", [128, 64], F32, off); off += 256
    bqrow = st["rowA"].bitcast(BF16)
    bkrow = st["rowB"].bitcast(BF16)

    m_wq = R("gpsimd", lambda e: e.dma_start(
        wq[:], I["L1_attn_Wq"].rearrange("(c p) o -> p c o", p=128)), dma=True)
    m_wk = R("gpsimd", lambda e: e.dma_start(
        wk[:], I["L1_attn_Wk"].rearrange("(c p) o -> p c o", p=128)), dma=True)
    m_bq = R("gpsimd", lambda e: e.dma_start(
        bqrow[:], I["L1_attn_bq"].rearrange("(one o) -> one o", one=1)), dma=True)
    m_bk = R("gpsimd", lambda e: e.dma_start(
        bkrow[:], I["L1_attn_bk"].rearrange("(one o) -> one o", one=1)), dma=True)
    m_kp = R("vector", lambda e: e.memset(kprev[:], 0.0))

    rel_x = [[], []]
    rel_tm = []
    rel_tb = []
    rel_qfm = []
    rel_kfm = []
    rel_ksm = []
    for g in range(NG):
        buf = g % 2
        gsl = slice(512 * g, 512 * (g + 1))
        m_xq = R("gpsimd", lambda e, gsl=gsl, buf=buf: e.dma_start(
            xq[buf][:], hbm["xq1"][:, :, gsl].rearrange("c p t -> p c t")),
            waits=rel_x[buf], dma=True)
        m_xk = R("gpsimd", lambda e, gsl=gsl, buf=buf: e.dma_start(
            xk[buf][:], hbm["xk1"][:, :, gsl].rearrange("c p t -> p c t")), dma=True)

        for which in range(2):
            xsrc = xq if which == 0 else xk
            wsrc = wq if which == 0 else wk
            brow = bqrow if which == 0 else bkrow
            m_ld = m_xq if which == 0 else m_xk
            dstf = qfm if which == 0 else kfm
            relf = rel_qfm if which == 0 else rel_kfm
            ev = {}
            for tt in range(4):
                for qs in range(4):
                    ti2 = tt * 4 + qs
                    bank = ti2 % 2
                    wl = []
                    if ti2 == 0:
                        wl = [m_ld, m_wq, m_wk, m_bq, m_bk] + rel_tm
                    if ti2 >= 2:
                        wl.append(ev[ti2 - 2])
                    mlast = None
                    for dc in range(8):
                        mlast = R("tensor", lambda e, tt=tt, qs=qs, dc=dc, bank=bank,
                                  xsrc=xsrc, wsrc=wsrc, buf=buf: e.matmul(
                                      pb[bank][:], xsrc[buf][:, dc, 128 * tt:128 * (tt + 1)],
                                      wsrc[:, dc, 512 * qs:512 * (qs + 1)],
                                      start=(dc == 0), stop=False), waits=(wl if dc == 0 else []))
                    mlast = R("tensor", lambda e, qs=qs, bank=bank, brow=brow: e.matmul(
                        pb[bank][:], st["ones_bf"][:], brow[0:1, 512 * qs:512 * (qs + 1)],
                        start=False, stop=True), waits=[])
                    ev[ti2] = R("scalar", lambda e, tt=tt, qs=qs, bank=bank: e.copy(
                        tm[:, tt, 512 * qs:512 * (qs + 1)], pb[bank][:]), waits=[mlast])
            ev_sm = []
            for tt in range(4):
                base = [ev[tt * 4 + qs] for qs in range(4)]
                m1 = R("vector", lambda e, tt=tt: e.reduce_max(
                    sfx[:, 0:16], tm[:, tt, :].rearrange("p (h d) -> p h d", h=16), axis=AX),
                    waits=base + (ev_sm[-1:] if ev_sm else []) + ([m_kp] if g == 0 else []))
                m2 = R("vector", lambda e, tt=tt: e.tensor_tensor(
                    tm[:, tt, :].rearrange("p (h d) -> p h d", h=16),
                    tm[:, tt, :].rearrange("p (h d) -> p h d", h=16),
                    sfx[:, 0:16].unsqueeze(2).broadcast_to((128, 16, 128)), ALU.subtract),
                    waits=[m1])
                m3 = R("scalar", lambda e, tt=tt: e.activation(
                    tm[:, tt, :], tm[:, tt, :], AF.Exp), waits=[m2])
                m4 = R("vector", lambda e, tt=tt: e.reduce_sum(
                    sfx[:, 16:32], tm[:, tt, :].rearrange("p (h d) -> p h d", h=16), axis=AX),
                    waits=[m3])
                m5 = R("vector", lambda e: e.reciprocal(sfx[:, 32:48], sfx[:, 16:32]), waits=[m4])
                if which == 0:
                    m5 = R("vector", lambda e: e.tensor_mul(sfx[:, 32:48], sfx[:, 32:48],
                                                            st["srecip_bc"][:]), waits=[m5])
                m7 = R("vector", lambda e, tt=tt: e.tensor_tensor(
                    tb[:, tt, :].rearrange("p (h d) -> p h d", h=16),
                    tm[:, tt, :].rearrange("p (h d) -> p h d", h=16),
                    sfx[:, 32:48].unsqueeze(2).broadcast_to((128, 16, 128)), ALU.mult),
                    waits=[m5] + (rel_tb if tt == 0 else []))
                ev_sm.append(m7)
            rel_tm = [ev_sm[-1]]
            evt = {}
            tpos = 0
            for tt in range(4):
                for c in range(16):
                    bank = tpos % 2
                    wl = [ev_sm[tt]]
                    if tpos == 0:
                        wl += relf
                    if tpos >= 2:
                        wl.append(evt[tpos - 2])
                    m_tr = R("tensor", lambda e, tt=tt, c=c, bank=bank: e.transpose(
                        pbh[bank].ap()[:, 0:128], tb[:, tt, 128 * c:128 * (c + 1)],
                        st["ident"][:]), waits=wl)
                    evt[tpos] = R("scalar", lambda e, tt=tt, c=c, bank=bank, dstf=dstf: e.copy(
                        dstf[:, c, 128 * tt:128 * (tt + 1)], pbh[bank].ap()[:, 0:128]),
                        waits=[m_tr])
                    tpos += 1
            rel_tb = [evt[tpos - 1]]
            if which == 0:
                m_stq = R("sync", lambda e, gsl=gsl: e.dma_start(
                    hbm["q1h"][:, :, gsl].rearrange("h p t -> p h t"), qfm[:]),
                    waits=[evt[62], evt[63]], dma=True)
                rel_qfm = [m_stq]
            else:
                m_fmk = [evt[62], evt[63]]

        ev_s = []
        for h in range(16):
            wl = m_fmk + (rel_ksm if h == 0 else []) + (ev_s[-1:] if ev_s else [])
            m_a = R("vector", lambda e, h=h: e.tensor_scalar(
                ksm[:, h, :], kfm[:, h, :], st["smr"][:, h:h + 1], None, ALU.mult), waits=wl)
            m_b = R("vector", lambda e, h=h: e.scalar_tensor_tensor(
                ksm[:, h, 1:512], kfm[:, h, 0:511], st["smr"][:, 16 + h:16 + h + 1],
                ksm[:, h, 1:512], ALU.mult, ALU.add), waits=[m_a])
            m_c = R("vector", lambda e, h=h: e.scalar_tensor_tensor(
                ksm[:, h, 0:1], kprev[:, h:h + 1], st["smr"][:, 16 + h:16 + h + 1],
                ksm[:, h, 0:1], ALU.mult, ALU.add), waits=[m_b])
            ev_s.append(m_c)
        m_kpv = R("vector", lambda e: e.tensor_copy(kprev[:], kfm[:, :, 511]), waits=[ev_s[-1]])
        m_stk = R("sync", lambda e, gsl=gsl: e.dma_start(
            hbm["k1h"][:, :, gsl].rearrange("h p t -> p h t"), ksm[:]),
            waits=[m_kpv], dma=True)
        rel_kfm = [m_kpv]
        rel_ksm = [m_stk]
        rel_x[buf] = [rel_tm[0]]


def phase_l1_attn(b, st):
    I, hbm, pb = st["I"], st["hbm"], st["pb"]
    R = b.R

    off = 0
    woutn = man(st, "p7_wout", [128, 16, 1024], BF16, off); off += 32 * 1024
    q_st = man(st, "p7_qst", [128, 16, 512], BF16, off)
    y_tile = man(st, "p7_yt", [128, 1024], F32, off)
    s1_tile = man(st, "p7_s1t", [128, 1024], F32, off + 4096)
    off += 16 * 1024
    k_st = [man(st, "p7_kst0", [128, 16, 512], BF16, off),
            man(st, "p7_kst1", [128, 16, 512], BF16, off + 16384)]
    off += 32 * 1024
    v_st = [man(st, "p7_vst0", [128, 4, 2048], BF16, off),
            man(st, "p7_vst1", [128, 4, 2048], BF16, off + 16384)]
    off += 32 * 1024
    p_st = man(st, "p7_pst", [128, 16, 512], BF16, off)
    o2_sb = man(st, "p7_o2", [128, 16, 512], BF16, off)
    off += 16 * 1024
    p_t = [man(st, f"p7_pt{i}", [128, 512], BF16, off + 1024 * i) for i in range(3)]
    den_sb = man(st, "p7_den", [16, 512], F32, off)
    o_t = man(st, "p7_ot", [128, 512], F32, off + 2048)
    off += 3 * 1024
    p_t2 = [man(st, f"p7_pt2{i}", [128, 512], BF16, off + 1024 * i) for i in range(3)]
    r_bc = man(st, "p7_rbc", [128, 512], F32, off + 1024)
    off += 3 * 1024
    e_sb = [man(st, f"p7_esb{i}", [128, 512], BF16, off + 1024 * i) for i in range(2)]
    off += 2 * 1024
    dext = man(st, "p7_dext", [128, 4480], BF16, off); off += 8960
    o_acc = man(st, "p7_oacc", [128, 16, 512], F32, off); off += 32 * 1024
    eye16 = man(st, "p7_eye", [128, 16, 16], BF16, off); off += 512
    kq16 = man(st, "p7_kq", [128, 16], F32, off); off += 64
    kqsep = man(st, "p7_kqs", [128, 8, 16], BF16, off); off += 256
    gob = man(st, "p7_gob", [128, 1024], F32, off); off += 4 * 1024
    bob = man(st, "p7_bob", [128, 1024], F32, off); off += 4 * 1024
    stt7 = man(st, "p7_stt", [128, 24], F32, off); off += 96

    denh = st["hbm"]["denh"]

    m_eye0 = R("vector", lambda e: e.memset(eye16[:], 0.0))
    m_kq0 = R("vector", lambda e: e.memset(kq16[:], 0.0))
    m_kqs0 = R("vector", lambda e: e.memset(kqsep[:], 0.0))
    m_eye = m_eye0
    for h in range(16):
        m_eye = R("vector", lambda e, h=h: e.memset(eye16[:, h, h:h + 1], 1.0), waits=[m_eye])
    oscr = o_acc[:].rearrange("p h t -> p (h t)")[:, 0:4480]
    m_dxa = R("gpsimd", lambda e: e.iota(oscr, [[1, 4480]], base=-384, channel_multiplier=-1,
                                         allow_small_or_imprecise_dtypes=True))
    m_dxb = R("vector", lambda e: e.tensor_scalar_max(oscr[:, 0:896], oscr[:, 0:896], 0.0),
              waits=[m_dxa])
    m_dxc = R("vector", lambda e: e.tensor_copy(dext[:], oscr), waits=[m_dxb])
    m_wo = R("gpsimd", lambda e: e.dma_start(
        woutn[:], I["L1_attn_W_out"].rearrange("(c p) d -> p c d", p=128)), dma=True)
    m_g1 = R("sync", lambda e: e.dma_start(st["rowB"][0:1, :],
                                           I["L1_attn_out_ln_w"].rearrange("(one d) -> one d", one=1)),
             dma=True)
    prev = None
    for half in range(2):
        mm = R("tensor", lambda e, half=half: e.matmul(
            pb[half][:], st["ones_fr"][:], st["rowB"][0:1, 512 * half:512 * (half + 1)],
            start=True, stop=True), waits=[m_g1])
        prev = R("scalar", lambda e, half=half: e.copy(gob[:, 512 * half:512 * (half + 1)],
                                                       pb[half][:]), waits=[mm])
    m_g2 = R("sync", lambda e: e.dma_start(st["rowB"][0:1, :],
                                           I["L1_attn_out_ln_b"].rearrange("(one d) -> one d", one=1)),
             waits=[prev], dma=True)
    for half in range(2):
        mm = R("tensor", lambda e, half=half: e.matmul(
            pb[half][:], st["ones_fr"][:], st["rowB"][0:1, 512 * half:512 * (half + 1)],
            start=True, stop=True), waits=[m_g2, prev])
        prev = R("scalar", lambda e, half=half: e.copy(bob[:, 512 * half:512 * (half + 1)],
                                                       pb[half][:]), waits=[mm])
    m_bc_done = prev

    def skip(h, offij):
        return h < 8 and SLOPES[h] * (offij - 127) > 104.0

    rel_q = []
    rel_p = []
    rel_k = [[], []]
    rel_v = [[], []]
    rel_pt = []
    rel_stb = []
    rel_esb = []
    rel_den = []
    rel_s1 = []
    rel_kq = []
    m_rbe = None
    add_by_bank = {}
    for g in range(NG):
        gsl = slice(512 * g, 512 * (g + 1))
        m_q = R("gpsimd", lambda e, gsl=gsl: e.dma_start(
            q_st[:], hbm["q1h"][:, :, gsl].rearrange("h p t -> p h t")),
            waits=rel_q + ([m_bc_done] if g == 0 else []), dma=True)
        m_p = R("gpsimd", lambda e, gsl=gsl: e.dma_start(
            p_st[:], hbm["p1h"][:, :, gsl].rearrange("h p t -> p h t")),
            waits=rel_p, dma=True)

        groups = []
        ntiles = 0
        for w in range(g + 1):
            for h in range(16):
                kcs = [kc for kc in range(4) if not skip(h, 512 * g - 128 * (4 * w + kc))]
                if kcs:
                    groups.append((w, h, kcs))
                    ntiles += len(kcs)
        pp = 0
        pvcnt = 0
        pv_marks = {}
        pmk = {}
        st_marks = {}
        cur_w = -1
        m_k2 = m_v2 = None
        lkb = {0: None, 1: None}
        lvb = {0: None, 1: None}
        seen_h = set()
        m_add = {}
        m_dn_last = None
        pend = None

        def rec_pv(t):
            nonlocal m_dn_last
            if not (t["h"] >= 8 and not t["diag"]):
                wl_dn = list(t["wl_pv"])
                if g == 0 and t["pp"] == 0:
                    wl_dn.append(m_eye)
                if t["pp"] == 0:
                    wl_dn += rel_den
                m_dn_last = R("tensor", lambda e, h=t["h"], fst_d=t["fst_d"], fsp_d=t["fsp_d"],
                              pvt=t["pvt"]: e.matmul(pb[7][0:16, :], eye16[:, h, :], pvt[:],
                                                     start=fst_d, stop=fsp_d), waits=wl_dn)
            wl2 = list(t["wl_pv"])
            if t["fst"] and t["pvb"] in add_by_bank:
                wl2.append(add_by_bank.pop(t["pvb"]))
            mpv = R("tensor", lambda e, h=t["h"], kc=t["kc"], pvb=t["pvb"], buf=t["buf"],
                    fst=t["fst"], fsp=t["fsp"], pvt=t["pvt"]: e.matmul(
                        pb[pvb][:], v_st[buf][:, kc, 128 * h:128 * (h + 1)], pvt[:],
                        start=fst, stop=fsp), waits=wl2)
            pv_marks[t["pp"]] = mpv
            lvb[t["buf"]] = mpv
            if t["fsp"]:
                h = t["h"]
                pvb = t["pvb"]
                if h not in seen_h:
                    seen_h.add(h)
                    m_add[h] = R("vector", lambda e, h=h, pvb=pvb: e.tensor_copy(
                        o_acc[:, h, :], pb[pvb][:]), waits=[mpv])
                else:
                    m_add[h] = R("vector", lambda e, h=h, pvb=pvb: e.tensor_add(
                        o_acc[:, h, :], o_acc[:, h, :], pb[pvb][:]), waits=[mpv, m_add[h]])
                add_by_bank[pvb] = m_add[h]

        for (w, h, kcs) in groups:
            if w != cur_w:
                cur_w = w
                buf = w % 2
                wk = [lkb[buf]] if lkb[buf] else rel_k[buf]
                wv = [lvb[buf]] if lvb[buf] else rel_v[buf]
                m_k2 = R("sync", lambda e, w=w, buf=buf: e.dma_start(
                    k_st[buf][:], hbm["k1h"][:, :, 512 * w:512 * (w + 1)].rearrange(
                        "h p t -> p h t")), waits=wk, dma=True)
                m_v2 = R("sync", lambda e, w=w, buf=buf: e.dma_start(
                    v_st[buf][:], hbm["v1h"][512 * w:512 * (w + 1), :].rearrange(
                        "(a p) x -> p a x", p=128)), waits=wv, dma=True)
            pvb = pvcnt % 4
            pvcnt += 1
            for ki, kc in enumerate(kcs):
                offij = 512 * g - 128 * (4 * w + kc)
                stb = 5 + pp % 2
                rot = pp % 3
                eb = pp % 2
                diag = (w == g)
                wl = [m_k2, m_q]
                if pp >= 2:
                    wl.append(pmk[pp - 2])
                else:
                    wl += rel_stb
                mm = R("tensor", lambda e, h=h, kc=kc, stb=stb, buf=buf: e.matmul(
                    pb[stb][:], k_st[buf][:, h, 128 * kc:128 * (kc + 1)], q_st[:, h, :],
                    start=True, stop=True), waits=wl)
                st_marks[pp] = mm
                lkb[buf] = mm
                wl_p = [mm]
                if pp >= 3:
                    wl_p.append(pv_marks[pp - 3])
                elif rel_pt:
                    wl_p.append(rel_pt[min(pp, len(rel_pt) - 1)])
                if h < 8:
                    wl_e = []
                    if pp >= 2:
                        wl_e.append(pmk[pp - 2])
                    elif rel_esb:
                        wl_e.append(rel_esb[min(pp, len(rel_esb) - 1)])
                    if g == 0 and pp < 2:
                        wl_e.append(m_dxc)
                    m_e = R("scalar", lambda e, eb=eb, offij=offij, h=h: e.activation(
                        e_sb[eb][:], dext[:, 384 + offij:384 + offij + 512], AF.Exp,
                        scale=-SLOPES[h]), waits=wl_e)
                    m_pt = R("vector", lambda e, rot=rot, eb=eb, stb=stb: e.tensor_mul(
                        p_t[rot][:], pb[stb][:], e_sb[eb][:]), waits=wl_p + [m_e])
                    pmk[pp] = m_pt
                else:
                    m_pt = R("scalar", lambda e, rot=rot, stb=stb: e.copy(
                        p_t[rot][:], pb[stb][:]), waits=wl_p)
                    pmk[pp] = m_pt
                wl_pv = [m_pt, m_v2]
                pvt = p_t[rot]
                if diag:
                    m_af = R("gpsimd", lambda e, rot=rot, kc=kc: e.affine_select(
                        p_t2[rot][:], p_t[rot][:], [[1, 512]], ALU.is_ge, 0.0,
                        base=-128 * kc, channel_multiplier=-1), waits=[m_pt])
                    wl_pv = [m_af, m_v2]
                    pmk[pp] = m_af
                    pvt = p_t2[rot]
                t = dict(pp=pp, h=h, kc=kc, pvb=pvb, buf=buf, pvt=pvt, wl_pv=wl_pv,
                         fst=(ki == 0), fsp=(ki == len(kcs) - 1), diag=diag,
                         fst_d=(pp == 0), fsp_d=(pp == ntiles - 1))
                if pend is not None:
                    rec_pv(pend)
                pend = t
                pp += 1
        m_d1 = {}
        if g > 0:
            for h in range(8, 16):
                m_d1[h] = R("tensor", lambda e, h=h: e.matmul(
                    pb[7][0:16, :], kqsep[:, h - 8, :], q_st[:, h, :],
                    start=False, stop=False),
                    waits=([m_q] + rel_kq if h == 8 else []))
        rec_pv(pend)
        pend = None
        dbuf7 = g % 2
        m_kqc = None
        if g < NG - 1:
            prevadd = None
            for h in range(8, 16):
                m_ks = R("vector", lambda e, h=h, dbuf7=dbuf7: e.reduce_sum(
                    stt7[:, 20:21], k_st[dbuf7][:, h, :], axis=AX),
                    waits=[lkb[dbuf7]] + ([m_kq0] if g == 0 else [])
                    + ([m_d1[h]] if h in m_d1 else []) + ([prevadd] if prevadd else []))
                pa = R("vector", lambda e, h=h: e.tensor_add(
                    kq16[:, h:h + 1], kq16[:, h:h + 1], stt7[:, 20:21]), waits=[m_ks])
                prevadd = R("vector", lambda e, h=h: e.tensor_copy(
                    kqsep[:, h - 8, h:h + 1], kq16[:, h:h + 1]),
                    waits=[pa] + ([m_kqs0] if g == 0 else []))
            m_kqc = prevadd
            rel_kq = [m_kqc]
        for bb in (0, 1):
            if lkb[bb]:
                rel_k[bb] = [lkb[bb]]
            if lvb[bb]:
                rel_v[bb] = [lvb[bb]]
        if m_kqc is not None:
            rel_k[dbuf7] = rel_k[dbuf7] + [m_kqc]
        rel_stb = [pmk[pp - 1], pmk[pp - 2]]
        rel_esb = [pmk[k] for k in sorted(pmk)[-2:]]
        last3pv = [pv_marks[k] for k in sorted(pv_marks)[-3:]]

        m_de = R("scalar", lambda e: e.copy(den_sb[:], pb[7][0:16, :]),
                 waits=[m_dn_last] + last3pv)
        rel_den = [m_de]
        m_dh = R("sync", lambda e: e.dma_start(denh[:], den_sb[:]), waits=[m_de], dma=True)
        m_on_prev = None
        m_o2 = {}
        m_dr = None
        for h in range(16):
            m_dr = R("sync", lambda e, h=h: e.dma_start(st["rowA"][0:1, 0:512],
                                                        denh[h:h + 1, :]),
                     waits=[m_dh] + ([m_rbe] if m_rbe else []), dma=True)
            m_ep = R("vector", lambda e: e.tensor_scalar(st["rowA"][0:1, 512:1024],
                                                         st["rowA"][0:1, 0:512],
                                                         EPS, None, ALU.add),
                     waits=[m_dr] + ([m_on_prev] if m_on_prev else []))
            m_rc = R("vector", lambda e: e.reciprocal(st["rowB"][0:1, 0:512],
                                                      st["rowA"][0:1, 512:1024]), waits=[m_ep])
            m_rb = R("tensor", lambda e: e.matmul(pb[4][:], st["ones_fr"][:],
                                                  st["rowB"][0:1, 0:512], start=True, stop=True),
                     waits=[m_rc] + ([m_rbe] if m_rbe else []))
            m_rbe = R("scalar", lambda e: e.copy(r_bc[:], pb[4][:]),
                      waits=[m_rb] + (last3pv if h == 0 else []))
            m_sp = R("scalar", lambda e, h=h: e.activation(o2_sb[:, h, :], p_st[:, h, :],
                                                           AF.Silu), waits=[m_rbe, m_p])
            m_on = R("vector", lambda e, h=h: e.tensor_mul(
                o_t[:], o_acc[:, h, :], r_bc[:]),
                waits=[m_sp, m_add[h]] + (last3pv if h == 0 else []))
            m_o2[h] = R("vector", lambda e, h=h: e.tensor_mul(
                o2_sb[:, h, :], o2_sb[:, h, :], o_t[:]), waits=[m_on])
            m_on_prev = m_o2[h]

        ev_prev = None
        mlast = None
        for tt in range(4):
            ev_y = {}
            for ds in range(2):
                bank = ds % 2
                wl = []
                if tt == 0 and ds == 0:
                    wl = [m_o2[h] for h in range(16)] + [m_wo] + [st_marks[pp - 1]]
                if ev_prev is not None and ds == 0:
                    wl.append(ev_prev)
                for h in range(16):
                    mlast = R("tensor", lambda e, tt=tt, ds=ds, h=h, bank=bank: e.matmul(
                        pb[bank][:], o2_sb[:, h, 128 * tt:128 * (tt + 1)],
                        woutn[:, h, 512 * ds:512 * (ds + 1)],
                        start=(h == 0), stop=(h == 15)), waits=(wl if h == 0 else []))
                ev_y[ds] = R("scalar", lambda e, ds=ds, bank=bank: e.copy(
                    y_tile[:, 512 * ds:512 * (ds + 1)], pb[bank][:]),
                    waits=[mlast] + ([ev_prev] if (ev_prev and ds == 0) else []))
            m1 = R("vector", lambda e: e.bn_stats(stt7[:, 0:6], y_tile[:, 0:512]),
                   waits=[ev_y[0], ev_y[1]])
            m2 = R("vector", lambda e: e.bn_stats(stt7[:, 6:12], y_tile[:, 512:1024]), waits=[m1])
            m3 = R("vector", lambda e: e.bn_aggr(stt7[:, 12:14], stt7[:, 0:12]), waits=[m2])
            mr1 = R("scalar", lambda e: e.activation(stt7[:, 14:15], stt7[:, 13:14], AF.Ln,
                                                     bias=st["epsb"][:]), waits=[m3])
            mr2 = R("scalar", lambda e: e.activation(stt7[:, 15:16], stt7[:, 14:15], AF.Exp,
                                                     scale=-0.5), waits=[mr1])
            ma = R("vector", lambda e: e.tensor_scalar(s1_tile[:], y_tile[:], stt7[:, 12:13],
                                                       stt7[:, 15:16], ALU.subtract, ALU.mult),
                   waits=[mr2] + rel_s1)
            mb = R("vector", lambda e: e.tensor_mul(y_tile[:], s1_tile[:], gob[:]), waits=[ma])
            mc = R("vector", lambda e: e.tensor_add(s1_tile[:], y_tile[:], bob[:]), waits=[mb])
            m_s1 = R("sync", lambda e, g=g, tt=tt: e.dma_start(
                hbm["s1h"][512 * g + 128 * tt:512 * g + 128 * (tt + 1), :], s1_tile[:]),
                waits=[mc], dma=True)
            rel_s1 = [m_s1]
            ev_prev = mc
        rel_q = [st_marks[pp - 1], ev_prev] + rel_s1
        rel_p = [mlast, ev_prev] + rel_s1
        rel_pt = [m_dh, m_dr, m_o2[15]]


def phase_l1_ffn(b, st):
    I, hbm, pb = st["I"], st["hbm"], st["pb"]
    R = b.R

    off = 0
    wg = [man(st, "p8_wg0", [128, 8, 1024], BF16, off),
          man(st, "p8_wg1", [128, 8, 1024], BF16, off + 16384)]
    yg = man(st, "p8_yg", [128, 4, 1024], F32, off)
    off += 32 * 1024
    wu = [man(st, "p8_wu0", [128, 8, 1024], BF16, off),
          man(st, "p8_wu1", [128, 8, 1024], BF16, off + 16384)]
    sg = man(st, "p8_sg", [128, 4, 1024], F32, off)
    off += 32 * 1024
    wd = [man(st, "p8_wd0", [128, 8, 1024], BF16, off),
          man(st, "p8_wd1", [128, 8, 1024], BF16, off + 16384)]
    x2b = man(st, "p8_x2b", [128, 4, 1024], F32, off)
    off += 32 * 1024
    xg = [man(st, "p8_xg0", [128, 8, 512], BF16, off),
          man(st, "p8_xg1", [128, 8, 512], BF16, off + 8192)]
    xb = man(st, "p8_xb", [128, 4, 1024], F32, off)
    off += 16 * 1024
    h_fm = man(st, "p8_hfm", [128, 8, 512], BF16, off); off += 8 * 1024
    ysb = man(st, "p8_ysb", [128, 4, 1024], F32, off); off += 16 * 1024
    w_bc = man(st, "p8_wbc", [128, 512], F32, off); off += 2 * 1024
    tmp = man(st, "p8_tmp", [128, 512], F32, off); off += 2 * 1024
    rwrow = man(st, "p8_rw", [1, 512], BF16, off); off += 1024

    rel_wg = [[], []]
    rel_wu = [[], []]
    rel_wd = [[], []]
    rel_xg = [[], []]
    rel_ysb = []
    rel_yacc = {}
    rel_wbc = []
    rel_tmp = []
    rel_hfm = []
    bank0r = []
    m_ld = {}
    for e_i in range(E1):
        wbuf = e_i % 2
        if e_i == 0:
            m_ld[0] = [
                R("gpsimd", lambda e: e.dma_start(
                    wg[0][:], I["L1_ffn_Wg"][0].rearrange("(c p) i -> p c i", p=128)), dma=True),
                R("gpsimd", lambda e: e.dma_start(
                    wu[0][:], I["L1_ffn_Wu"][0].rearrange("(c p) i -> p c i", p=128)), dma=True),
                R("gpsimd", lambda e: e.dma_start(
                    wd[0][:], I["L1_ffn_Wd"][0].rearrange("(c p) d -> p c d", p=128)), dma=True)]
        if e_i + 1 < E1:
            nbuf = (e_i + 1) % 2
            m_ld[e_i + 1] = [
                R("gpsimd", lambda e, e_i=e_i: e.dma_start(
                    wg[(e_i + 1) % 2][:],
                    I["L1_ffn_Wg"][e_i + 1].rearrange("(c p) i -> p c i", p=128)),
                    waits=rel_wg[nbuf], dma=True),
                R("gpsimd", lambda e, e_i=e_i: e.dma_start(
                    wu[(e_i + 1) % 2][:],
                    I["L1_ffn_Wu"][e_i + 1].rearrange("(c p) i -> p c i", p=128)),
                    waits=rel_wu[nbuf], dma=True),
                R("gpsimd", lambda e, e_i=e_i: e.dma_start(
                    wd[(e_i + 1) % 2][:],
                    I["L1_ffn_Wd"][e_i + 1].rearrange("(c p) d -> p c d", p=128)),
                    waits=rel_wd[nbuf], dma=True)]
        m_w = m_ld[e_i]
        last_g = last_u = last_d = None
        for g in range(NG):
            gsl = slice(512 * g, 512 * (g + 1))
            xbuf = g % 2
            m_x = R("sync", lambda e, gsl=gsl, xbuf=xbuf: e.dma_start(
                xg[xbuf][:], hbm["xn1"][:, :, gsl].rearrange("c p t -> p c t")),
                waits=rel_xg[xbuf], dma=True)
            m_rw = R("sync", lambda e, e_i=e_i, gsl=gsl: e.dma_start(
                rwrow[:], hbm["rw1"][e_i:e_i + 1, gsl]), waits=rel_wbc, dma=True)
            m_wm = R("tensor", lambda e: e.matmul(pb[0][:], st["ones_bf"][:], rwrow[0:1, :],
                                                  start=True, stop=True),
                     waits=[m_rw] + bank0r)
            m_wbc = R("scalar", lambda e: e.copy(w_bc[:], pb[0][:]),
                      waits=[m_wm] + rel_wbc)
            rel_wbc = [m_wbc]

            m_si = {}
            m_m1 = {}
            m_m2 = {}
            for ic in range(8):
                ba = ic % 2
                bu = 2 + ic % 2
                wl_g = []
                if ic == 0:
                    wl_g = [m_wbc, m_x] + m_w + rel_hfm
                if ic >= 2:
                    wl_g.append(m_si[ic - 2])
                for dc in range(8):
                    last_g = R("tensor", lambda e, ic=ic, dc=dc, ba=ba, wbuf=wbuf, xbuf=xbuf:
                               e.matmul(pb[ba][:], wg[wbuf][:, dc, 128 * ic:128 * (ic + 1)],
                                        xg[xbuf][:, dc, :], start=(dc == 0), stop=(dc == 7)),
                               waits=(wl_g if dc == 0 else []))
                wl_u = [m_m1[ic - 2]] if ic >= 2 else []
                for dc in range(8):
                    last_u = R("tensor", lambda e, ic=ic, dc=dc, bu=bu, wbuf=wbuf, xbuf=xbuf:
                               e.matmul(pb[bu][:], wu[wbuf][:, dc, 128 * ic:128 * (ic + 1)],
                                        xg[xbuf][:, dc, :], start=(dc == 0), stop=(dc == 7)),
                               waits=(wl_u if dc == 0 else []))
                m_si[ic] = R("scalar", lambda e, ba=ba: e.activation(tmp[:], pb[ba][:], AF.Silu),
                             waits=[last_g] + ([m_m2[ic - 1]] if ic >= 1 else rel_tmp))
                m_m1[ic] = R("vector", lambda e, bu=bu: e.tensor_mul(tmp[:], tmp[:], pb[bu][:]),
                             waits=[m_si[ic], last_u])
                m_m2[ic] = R("vector", lambda e, ic=ic: e.tensor_mul(h_fm[:, ic, :], tmp[:],
                                                                     w_bc[:]), waits=[m_m1[ic]])
            rel_tmp = [m_m2[7]]
            bank0r = [m_si[6]]
            rel_wbc = [m_m2[7]]
            rel_xg[xbuf] = [last_u]

            ev_y = {}
            for tt in range(4):
                for ds in range(2):
                    ti = 2 * tt + ds
                    bank = 4 + ti % 4
                    wl = []
                    if ti == 0:
                        wl = [m_m2[7], m_m2[6]] + rel_ysb
                    if ti >= 4:
                        wl.append(ev_y[ti - 4])
                    for ic in range(8):
                        last_d = R("tensor", lambda e, tt=tt, ds=ds, ic=ic, bank=bank, wbuf=wbuf:
                                   e.matmul(pb[bank][:], h_fm[:, ic, 128 * tt:128 * (tt + 1)],
                                            wd[wbuf][:, ic, 512 * ds:512 * (ds + 1)],
                                            start=(ic == 0), stop=(ic == 7)),
                                   waits=(wl if ic == 0 else []))
                    ev_y[ti] = R("scalar", lambda e, tt=tt, ds=ds, bank=bank: e.copy(
                        ysb[:, tt, 512 * ds:512 * (ds + 1)], pb[bank][:]), waits=[last_d])
            wl_acc = [ev_y[6], ev_y[7]] + ([rel_yacc[g]] if g in rel_yacc else [])
            if e_i == 0:
                m_acc = R("gpsimd", lambda e, gsl=gsl: e.dma_start(
                    hbm["y1h"][gsl, :].rearrange("(a p) d -> p a d", p=128), ysb[:]),
                    waits=wl_acc, dma=True)
            else:
                m_acc = R("gpsimd", lambda e, gsl=gsl: e.dma_start(
                    hbm["y1h"][gsl, :].rearrange("(a p) d -> p a d", p=128), ysb[:],
                    accum_op=ALU.add), waits=wl_acc, dma=True)
            rel_yacc[g] = m_acc
            rel_ysb = [m_acc]
            rel_hfm = [last_d]
        rel_wg[wbuf] = [last_g]
        rel_wu[wbuf] = [last_u]
        rel_wd[wbuf] = [last_d]

    done_all = rel_wg[0] + rel_wu[0] + rel_wd[0] + rel_xg[0] + rel_xg[1] + rel_ysb + rel_tmp
    rel_bufs = []
    for g in range(NG):
        gsl = slice(512 * g, 512 * (g + 1))
        m_y = R("sync", lambda e, gsl=gsl: e.dma_start(
            yg[:], hbm["y1h"][gsl, :].rearrange("(a p) d -> p a d", p=128)),
            waits=[rel_yacc[g]] + rel_bufs + (done_all if g == 0 else []), dma=True)
        m_s = R("sync", lambda e, gsl=gsl: e.dma_start(
            sg[:], hbm["s1h"][gsl, :].rearrange("(a p) d -> p a d", p=128)), dma=True)
        m_x = R("sync", lambda e, gsl=gsl: e.dma_start(
            xb[:], hbm["x1h"][gsl, :].rearrange("(a p) d -> p a d", p=128)), dma=True)
        m_si = R("scalar", lambda e: e.activation(x2b[:], yg[:], AF.Silu),
                 waits=[m_y] + rel_bufs)
        m_t1 = R("vector", lambda e: e.tensor_mul(x2b[:], sg[:], x2b[:]), waits=[m_si, m_s])
        m_t2 = R("vector", lambda e: e.tensor_add(x2b[:], xb[:], x2b[:]), waits=[m_t1, m_x])
        m_st = R("sync", lambda e, gsl=gsl: e.dma_start(
            st["out"][gsl, :].rearrange("(a p) d -> p a d", p=128), x2b[:]),
            waits=[m_t2], dma=True)
        rel_bufs = [m_st, m_t2]



def make_inmaps(inputs):
    x = np.ascontiguousarray(inputs["L0_attn_x"])
    shared = {}
    for k, v in inputs.items():
        if k != "L0_attn_x":
            shared[k] = np.ascontiguousarray(v)
    return [dict(shared, x0=np.ascontiguousarray(x[c])) for c in range(8)]


def kernel(**inputs):
    nc = build_program()
    maps = make_inmaps(inputs)
    res = run_bass_kernel_spmd(nc, maps, core_ids=list(range(8)))
    return np.stack([r["out"] for r in res.results], axis=0)
